# version 19
# baseline (speedup 1.0000x reference)
import numpy as np
import ml_dtypes
import concourse.bass as bass
import concourse.mybir as mybir
from concourse.bass_utils import run_bass_kernel_spmd

F32 = mybir.dt.float32
BF16 = mybir.dt.bfloat16
U8 = mybir.dt.uint8
I8 = mybir.dt.int8
AF = mybir.ActivationFunctionType
ALU = mybir.AluOpType
AX = mybir.AxisListType

BF = ml_dtypes.bfloat16
NEG_BIG = -1.0e30
LN_EPS = 1e-5
PAIRS = [[0, 1], [2, 3], [4, 5], [6, 7]]
ENGS = ("sync", "tensor", "vector", "scalar", "gpsimd")


class Cfg:
    def __init__(self, n=4096, D=2048, hc=8, inter=8192):
        self.n, self.D, self.hc, self.inter = n, D, hc, inter
        self.dh = 128
        self.H = 2 * hc
        self.F = self.H * self.dh
        self.NT = n // 128
        self.NCC = D // 128
        self.qw = hc * self.dh
        self.fw = 3 * self.qw
        self.half = n // 2
        self.HT = self.half // 128
        self.TB = min(512, self.half)
        self.NB = self.half // self.TB
        self.NTS = self.TB // 128
        self.NOC = inter // 128
        self.NOB = inter // 512
        self.NFC = self.F // 128
        sec_blocks = []
        w = self.qw
        nblk = (w + 511) // 512
        for sec in range(3):
            for b in range(nblk):
                o = b * 512
                sec_blocks.append((sec, sec * w + o, min(512, w - o)))
        self.fblocks = sec_blocks
        self.nkc = [(128 * (j + 1) + 511) // 512 for j in range(self.NT)]
        assert self.qw <= 1024, "k/q transpose staging assumes qw <= 1024"



def np_ln(v):
    m = v.mean(-1, keepdims=True)
    var = ((v - m) ** 2).mean(-1, keepdims=True)
    return (v - m) / np.sqrt(var + LN_EPS)


def np_ref_batch(cfg, x, w_qkv, w_out, wg, wu, wd, sg, su, sd):
    n, D = x.shape
    H, dh, ROT = cfg.H, cfg.dh, 32
    Fi = H * dh
    x1 = np_ln(x)
    qkv = x1 @ w_qkv.T
    q, k, v = qkv[:, :Fi], qkv[:, Fi:2 * Fi], qkv[:, 2 * Fi:]

    def heads(t):
        return t.reshape(n, H, dh).transpose(1, 0, 2)
    q, k, v = map(heads, (q, k, v))
    half = ROT // 2
    inv = 10000.0 ** (-(np.arange(0, ROT, 2)) / ROT)
    th = np.arange(n)[:, None] * inv[None, :]
    cos, sin = np.cos(th).astype(np.float32), np.sin(th).astype(np.float32)

    def rot(t):
        r = t.copy()
        a, b = t[..., 0:half], t[..., half:ROT]
        r[..., 0:half] = a * cos - b * sin
        r[..., half:ROT] = b * cos + a * sin
        return r
    q, k = rot(q), rot(k)
    q = q * dh ** -0.5
    mask = np.triu(np.ones((n, n), bool), 1)
    O = np.zeros((H, n, dh), np.float32)
    for h in range(H):
        s = (q[h] @ k[h].T).astype(np.float32)
        s[mask] = -np.finfo(np.float32).max
        s = s - s.max(-1, keepdims=True)
        p = np.exp(s)
        p /= p.sum(-1, keepdims=True)
        O[h] = p @ v[h]
    O = O.transpose(1, 0, 2).reshape(n, Fi)
    y = O @ w_out.T
    x = x + np_ln(y)
    x2 = np_ln(x)
    gate = (x2 @ wg.astype(np.float32)) * sg
    up = (x2 @ wu.astype(np.float32)) * su
    fuse = gate / (1.0 + np.exp(-gate)) * up
    z = (fuse @ wd.astype(np.float32)) * sd
    return x + np_ln(z)



def host_tables(cfg):
    n, NT = cfg.n, cfg.NT
    inv = 10000.0 ** (-(np.arange(0, 32, 2)) / 32)
    pos = np.arange(n)[:, None] * inv[None, :]
    cos = np.cos(pos).astype(np.float32)
    sin = np.sin(pos).astype(np.float32)
    cosx = np.zeros((128, NT * 32), np.float32)
    sinx = np.zeros((128, NT * 32), np.float32)
    for t in range(NT):
        rows = slice(t * 128, (t + 1) * 128)
        cosx[:, t * 32:t * 32 + 16] = cos[rows]
        cosx[:, t * 32 + 16:t * 32 + 32] = cos[rows]
        sinx[:, t * 32:t * 32 + 16] = -sin[rows]
        sinx[:, t * 32 + 16:t * 32 + 32] = sin[rows]
    ii = np.arange(128)[:, None]
    cc = np.arange(896)[None, :]
    maskx = np.where(cc <= ii + 384, np.float32(0.0),
                     np.float32(NEG_BIG)).astype(BF)
    return dict(cosx=cosx, sinx=sinx, maskx=maskx,
                identb=np.eye(128, dtype=BF),
                identf=np.eye(128, dtype=np.float32),
                qscale=np.full((128, 1), cfg.dh ** -0.5, np.float32),
                onesb=np.ones((1, 128), BF))


def prep_in_maps(cfg, xb, w_qkv, w_out, wg, wu, wd, sg, su, sd):
    c = cfg
    tabs = host_tables(c)
    Fi = c.F
    wgt = np.ascontiguousarray(wg.reshape(c.D, c.NOB, 512).transpose(1, 0, 2))
    wut = np.ascontiguousarray(wu.reshape(c.D, c.NOB, 512).transpose(1, 0, 2))
    wdt = np.ascontiguousarray(wd)
    sgt = np.ascontiguousarray(sg.reshape(c.NOC, 128).T.astype(np.float32))
    sut = np.ascontiguousarray(su.reshape(c.NOC, 128).T.astype(np.float32))
    sdt = np.ascontiguousarray(sd.reshape(c.NCC, 128).T.astype(np.float32))
    woutT = np.ascontiguousarray(w_out.T.astype(np.float32))
    in_maps = []
    for core in range(2 * len(xb)):
        b, p = core // 2, core % 2
        hsl = slice(p * c.qw, (p + 1) * c.qw)
        wqkvT = np.ascontiguousarray(np.concatenate(
            [w_qkv[0 * Fi:1 * Fi][hsl].T,
             w_qkv[1 * Fi:2 * Fi][hsl].T,
             w_qkv[2 * Fi:3 * Fi][hsl].T], axis=1).astype(np.float32))
        x = np.ascontiguousarray(xb[b].astype(np.float32))
        m = dict(x=x,
                 xown=np.ascontiguousarray(x[p * c.half:(p + 1) * c.half]),
                 wqkvT=wqkvT, woutT=woutT,
                 wg=wgt, wu=wut, wd=wdt, sg=sgt, su=sut, sd=sdt,
                 selA=np.full((128, 1), 1.0 - p, np.float32),
                 selB=np.full((128, 1), float(p), np.float32))
        m.update(tabs)
        in_maps.append(m)
    return in_maps



class Plan:
    def __init__(self):
        self.ops = {e: [] for e in ENGS}

    def on(self, eng, fn):
        self.ops[eng].append(fn)

    def wait(self, eng, sem, val):
        if val > 0:
            self.on(eng, lambda e, s=sem, v=val: e.wait_ge(s.h, v))

    def replay(self, block):
        deco = dict(sync=block.sync, tensor=block.tensor, vector=block.vector,
                    scalar=block.scalar, gpsimd=block.gpsimd)
        for eng in ENGS:
            ops = self.ops[eng]

            def body(e, ops=ops):
                for fn in ops:
                    fn(e)
            deco[eng](body)


class Sem:
    def __init__(self, nc, ctxs, name):
        cm = nc.semaphore(name)
        self.h = cm.__enter__()
        ctxs.append(cm)
        self.n = 0

    def bump(self, k=1):
        self.n += k
        return self.n


class Arena:
    def __init__(self, ap, nbytes):
        self.ap = ap
        self.nbytes = nbytes
        self.off = 0
        self.marks = []
        self.hi = 0

    def take(self, nelem, dt, name=""):
        sz = mybir.dt.size(dt)
        a = (self.off + 31) // 32 * 32
        b = a + nelem * sz
        assert b <= self.nbytes, f"arena overflow at {name}: {b} > {self.nbytes}"
        self.off = b
        self.hi = max(self.hi, b)
        return self.ap[:, a:b].bitcast(dt)

    def push(self):
        self.marks.append(self.off)

    def pop(self):
        self.off = self.marks.pop()


def barrier(plan, sem):
    tgt = sem.n + len(ENGS)
    for eng in ENGS:
        plan.on(eng, lambda e, s=sem: e.sem_inc(s.h, 1))
    sem.n = tgt
    for eng in ENGS:
        plan.wait(eng, sem, tgt)


def emit_ln(plan, c, lnA, lnD, bufs, src_ap, dst_ap, prod_sem, prod_val,
            dst_free=None):
    D = src_ap.shape[-1]
    sq, st = bufs["sq"], bufs["st"]
    ssum, ssq = st[:, 0:1], st[:, 1:2]
    mean, var = st[:, 2:3], st[:, 3:4]
    rstd, negmb = st[:, 4:5], st[:, 5:6]
    tmp, rvar = st[:, 6:7], st[:, 7:8]
    plan.wait("vector", prod_sem, prod_val)
    plan.on("vector", lambda e, o=ssum, i=src_ap:
            e.reduce_sum(o, i, axis=AX.X))
    plan.wait("scalar", prod_sem, prod_val)
    a1 = lnA.bump()
    plan.on("scalar", lambda e, o=sq[:, 0:D], i=src_ap, a=ssq:
            e.activation(o, i, AF.Square, accum_out=a).then_inc(lnA.h, 1))
    plan.wait("vector", lnA, a1)
    cD = 1.0 / D

    def stats1(e):
        e.drain()
        e.tensor_scalar_mul(mean, ssum, cD)
        e.tensor_scalar_mul(var, ssq, cD)
        e.drain()
        e.tensor_tensor(tmp, mean, mean, ALU.mult)
        e.drain()
        e.tensor_sub(var, var, tmp)
        e.drain()
        e.tensor_scalar_add(var, var, LN_EPS)
        e.drain()
        e.reciprocal(rvar, var).then_inc(lnD.h, 1)
    plan.on("vector", stats1)
    d1 = lnD.bump()
    plan.wait("scalar", lnD, d1)
    a2 = lnA.bump()
    plan.on("scalar", lambda e, o=rstd, i=rvar:
            e.activation(o, i, AF.Sqrt).then_inc(lnA.h, 1))
    plan.wait("vector", lnA, a2)

    def stats2(e):
        e.tensor_tensor(tmp, mean, rstd, ALU.mult)
        e.drain()
        e.tensor_scalar_mul(negmb, tmp, -1.0).then_inc(lnD.h, 1)
    plan.on("vector", stats2)
    d2 = lnD.bump()
    plan.wait("scalar", lnD, d2)
    if dst_free is not None:
        plan.wait("scalar", dst_free[0], dst_free[1])
    a3 = lnA.bump()
    plan.on("scalar", lambda e, o=dst_ap, i=src_ap, sc=rstd, b=negmb:
            e.activation(o, i, AF.Identity, scale=sc, bias=b).then_inc(lnA.h, 1))
    return lnA, a3



def build_nc(cfg, debug_phase=0):
    c = cfg
    nc = bass.Bass("TRN2", target_bir_lowering=False,
                   detect_race_conditions=False)

    def din(name, shape, dt):
        return nc.dram_tensor(name, shape, dt, kind="ExternalInput").ap()

    x_d = din("x", [c.n, c.D], F32)
    xown_d = din("xown", [c.half, c.D], F32)
    wqkvT_d = din("wqkvT", [c.D, c.fw], F32)
    woutT_d = din("woutT", [c.F, c.D], F32)
    wg_d = din("wg", [c.NOB, c.D, 512], I8)
    wu_d = din("wu", [c.NOB, c.D, 512], I8)
    wd_d = din("wd", [c.inter, c.D], I8)
    sg_d = din("sg", [128, c.NOC], F32)
    su_d = din("su", [128, c.NOC], F32)
    sd_d = din("sd", [128, c.NCC], F32)
    cosx_d = din("cosx", [128, c.NT * 32], F32)
    sinx_d = din("sinx", [128, c.NT * 32], F32)
    maskx_d = din("maskx", [128, 896], BF16)
    identb_d = din("identb", [128, 128], BF16)
    identf_d = din("identf", [128, 128], F32)
    qscale_d = din("qscale", [128, 1], F32)
    onesb_d = din("onesb", [1, 128], BF16)
    selA_d = din("selA", [128, 1], F32)
    selB_d = din("selB", [128, 1], F32)

    out_d = nc.dram_tensor("out", [c.half, c.D], F32, kind="ExternalOutput").ap()

    kt_d = nc.dram_tensor("kt_d", [c.hc, 128, c.n], BF16, kind="Internal").ap()
    qt_d = nc.dram_tensor("qt_d", [c.hc, 128, c.n], BF16, kind="Internal").ap()
    v_d = nc.dram_tensor("v_d", [c.NT, 128, c.qw], BF16, kind="Internal").ap()
    hh = c.hc // 2
    ccsrc_t = [nc.dram_tensor(f"ccsrc{i}", [hh * 128, c.n], BF16,
                              kind="Internal") for i in range(2)]
    ccdst_t = [nc.dram_tensor(f"ccdst{i}", [2 * hh * 128, c.n], BF16,
                              kind="Internal") for i in range(2)]
    ccsrc = [t.ap() for t in ccsrc_t]
    ccdst = [t.ap() for t in ccdst_t]
    xatt_d = nc.dram_tensor("xatt_d", [c.half, c.D], F32, kind="Internal").ap()
    x2t_d = nc.dram_tensor("x2t_d", [c.NCC, 128, c.half], BF16, kind="Internal").ap()

    dbg = {}
    if debug_phase == 1:
        dbg["kt"] = nc.dram_tensor("dbg_kt", [c.hc, 128, c.n], BF16,
                                   kind="ExternalOutput").ap()
        dbg["qt"] = nc.dram_tensor("dbg_qt", [c.hc, 128, c.n], BF16,
                                   kind="ExternalOutput").ap()
        dbg["v"] = nc.dram_tensor("dbg_v", [c.NT, 128, c.qw], BF16,
                                  kind="ExternalOutput").ap()
        dbg["nb"] = nc.dram_tensor("dbg_nb", [128, c.NT * c.hc], F32,
                                   kind="ExternalOutput").ap()
    if debug_phase == 2:
        dbg["ot"] = nc.dram_tensor("dbg_ot", [2 * c.hc * 128, c.n], BF16,
                                   kind="ExternalOutput").ap()
    if debug_phase == 3:
        dbg["xatt"] = nc.dram_tensor("dbg_xatt", [c.half, c.D], F32,
                                     kind="ExternalOutput").ap()
        dbg["x2t"] = nc.dram_tensor("dbg_x2t", [c.NCC, 128, c.half], BF16,
                                    kind="ExternalOutput").ap()
    if debug_phase in (4, 10):
        dbg["x2b"] = nc.dram_tensor("dbg_x2b", [c.NB, 128, c.NCC * c.TB], BF16,
                                    kind="ExternalOutput").ap()
        dbg["fuse"] = nc.dram_tensor("dbg_fuse", [c.NB, 128, c.NOC * c.TB], BF16,
                                     kind="ExternalOutput").ap()
        dbg["z"] = nc.dram_tensor("dbg_z", [c.NB, 128, c.NTS * c.D], F32,
                                  kind="ExternalOutput").ap()

    ARENA_BYTES = 200 * 1024
    ctxs = []
    arena_cm = nc.sbuf_tensor("arena", [128, ARENA_BYTES], U8)
    arena_t = arena_cm.__enter__()
    ctxs.append(arena_cm)
    pool_cm = nc.psum_tensor("pool", [128, 4096], F32)
    pool_t = pool_cm.__enter__()
    ctxs.append(pool_cm)
    block_cm = nc.Block()
    block = block_cm.__enter__()

    A = Arena(arena_t.ap(), ARENA_BYTES)
    pool = pool_t.ap()

    def bank(i, dt=F32, width=None):
        w = width if width is not None else (512 if dt == F32 else 1024)
        v = pool[:, i * 512:(i + 1) * 512]
        if dt != F32:
            v = v.bitcast(dt)
        return v[:, 0:w]

    plan = Plan()

    def S(name):
        return Sem(nc, ctxs, name)

    def finish():
        plan.replay(block)
        block_cm.__exit__(None, None, None)
        for cm in reversed(ctxs):
            cm.__exit__(None, None, None)
        return nc

    cosx = A.take(c.NT * 32, F32, "cosx")
    sinx = A.take(c.NT * 32, F32, "sinx")
    maskx = A.take(896, BF16, "maskx")
    identb = A.take(128, BF16, "identb")
    identf = A.take(128, F32, "identf")
    qscale = A.take(1, F32, "qscale")
    selA = A.take(1, F32, "selA")
    selB = A.take(1, F32, "selB")
    onesb = A.take(128, BF16, "onesb")
    sg_sb = A.take(c.NOC, F32, "sg")
    su_sb = A.take(c.NOC, F32, "su")
    sd_sb = A.take(c.NCC, F32, "sd")
    qn2 = A.take(c.NT * c.hc, F32, "qn2")
    kn2t = A.take(c.hc, F32, "kn2t")
    kn2max = A.take(c.hc, F32, "kn2max")
    negb = A.take(c.NT * c.hc, F32, "negb")
    km2rep = A.take(c.hc, F32, "km2rep")
    kn2row = A.take(128, F32, "kn2row")
    km2row_bf = A.take(c.hc, BF16, "km2row")
    lparts = A.take(16, F32, "lparts")
    linv = A.take(1, F32, "linv")
    lsum = A.take(1, F32, "lsum")
    st_sb = A.take(8, F32, "lnstats")
    sq_sb = A.take(c.D, BF16, "sq_ln_discard")

    s_const = S("s_const")

    def pconst(e):
        for dst, src in [(cosx, cosx_d), (sinx, sinx_d), (maskx, maskx_d),
                         (identb, identb_d), (identf, identf_d),
                         (qscale, qscale_d), (selA, selA_d), (selB, selB_d),
                         (onesb[0:1, :], onesb_d),
                         (sg_sb, sg_d), (su_sb, su_d), (sd_sb, sd_d)]:
            e.dma_start(out=dst, in_=src).then_inc(s_const.h, 16)
    plan.on("sync", pconst)
    s_const.n = 12 * 16
    CONST = s_const.n
    for eng in ("tensor", "vector", "scalar"):
        plan.wait(eng, s_const, CONST)

    lnbufs = {"sq": sq_sb, "st": st_sb}

    A.push()
    w_sb = A.take(c.NCC * c.fw, BF16, "w_sb")
    xt = [A.take(c.D, F32, f"xt{i}") for i in range(2)]
    x1 = [A.take(c.D, BF16, f"x1_{i}") for i in range(2)]
    x1T = [A.take(c.D, BF16, f"x1T{i}") for i in range(2)]
    q_sb = [A.take(c.qw, BF16, f"q_sb{i}") for i in range(2)]
    k_sb = [A.take(c.qw, BF16, f"k_sb{i}") for i in range(2)]
    rotA = A.take(c.hc * 32, F32, "rotA")
    rotB = A.take(c.hc * 32, F32, "rotB")
    kqstage = [A.take(2 * c.qw, BF16, f"kqst{i}") for i in range(2)]
    vstage = [A.take(c.qw, BF16, f"vst{i}") for i in range(2)]
    sqn = A.take(c.qw, BF16, "sqn")
    sqk = A.take(c.qw, BF16, "sqk")

    s_xp = [S("s_x0"), S("s_x1")]
    s_w = S("s_w")
    s_lnA = S("s_lnA")
    s_lnD = S("s_lnD")
    s_trx = S("s_trx")
    s_trxe = S("s_trxe")
    s_qkm = S("s_qkm")
    s_qkeA = S("s_qkeA")
    s_qkeD = S("s_qkeD")
    s_nsq = S("s_nsq")
    s_nrd = S("s_nrd")
    s_rot = S("s_rot")
    s_trkq = S("s_trkq")
    s_trkqe = S("s_trkqe")
    s_stkq = [S("s_stkq0"), S("s_stkq1")]
    s_stv = [S("s_stv0"), S("s_stv1")]
    s_fin = S("s_fin")

    def pweights(e):
        for cc in range(c.NCC):
            e.dma_start(out=w_sb[:, cc * c.fw:(cc + 1) * c.fw],
                        in_=wqkvT_d[cc * 128:(cc + 1) * 128, :]
                        ).then_inc(s_w.h, 16)
    plan.on("gpsimd", pweights)
    s_w.n = c.NCC * 16
    W_READY = s_w.n

    plan.on("sync", lambda e: e.dma_start(out=xt[0], in_=x_d[0:128, :])
            .then_inc(s_xp[0].h, 16))
    s_xp[0].bump(16)

    nbat = 2 if c.NCC > 8 else 1
    per = c.NCC // nbat
    prev_ln_done = 0
    NFB = len(c.fblocks)
    qke_evac = {}

    ln_done = {}

    def emit_xtrans(tl):
        tbl = tl % 2
        plan.wait("tensor", s_lnA, ln_done[tl])
        for bti in range(nbat):
            gi = tl * nbat + bti
            if gi >= 2:
                plan.wait("tensor", s_trxe, gi - 1)
            pb = bank(gi % 2, BF16)
            for u in range(per):
                ccg = bti * per + u

                def trx(e, src=x1[tbl][:, ccg * 128:(ccg + 1) * 128],
                        dst=pb[:, u * 128:(u + 1) * 128], fin=(u == per - 1)):
                    ins = e.transpose(dst, src, identb)
                    if fin:
                        ins.then_inc(s_trx.h, 1)
                plan.on("tensor", trx)
            s_trx.bump()
            plan.wait("vector", s_trx, s_trx.n)
            if tl >= 2 and bti == 0:
                plan.wait("vector", s_qkm, (tl - 1) * NFB)
            plan.on("vector", lambda e,
                    o=x1T[tbl][:, bti * per * 128:(bti + 1) * per * 128],
                    i=pb[:, 0:per * 128]:
                    e.tensor_copy(o, i).then_inc(s_trxe.h, 1))
            s_trxe.bump()

    for t in range(c.NT):
        tb = t % 2
        if t == 0:
            sem, val = emit_ln(plan, c, s_lnA, s_lnD, lnbufs, xt[0], x1[0],
                               s_xp[0], 16)
            ln_done[0] = val
            emit_xtrans(0)
        if t + 1 < c.NT:
            if t >= 1:
                plan.wait("sync", s_lnA, ln_done[t - 1])
            plan.on("sync", lambda e, i=(t + 1) % 2, tt=t + 1: e.dma_start(
                out=xt[i], in_=x_d[tt * 128:(tt + 1) * 128, :])
                .then_inc(s_xp[i].h, 16))
            s_xp[(t + 1) % 2].bump(16)
        if t + 1 < c.NT:
            dfree = (s_trx, t * nbat) if t >= 1 else None
            sem2, val2 = emit_ln(plan, c, s_lnA, s_lnD, lnbufs, xt[(t + 1) % 2],
                                 x1[(t + 1) % 2], s_xp[(t + 1) % 2],
                                 16 * ((t + 1) // 2 + 1), dst_free=dfree)
            ln_done[t + 1] = val2
        if t == 0:
            plan.wait("tensor", s_w, W_READY)
        plan.wait("tensor", s_trxe, (t + 1) * nbat)
        QKB = (2, 3, 4, 7)
        for fb, (sec, foff, fwid) in enumerate(c.fblocks):
            gb = t * NFB + fb
            if gb >= 4:
                psem, pval = qke_evac[gb - 4]
                plan.wait("tensor", psem, pval)
            pq = bank(QKB[gb % 4])[:, 0:fwid]
            for ccg in range(c.NCC):
                def qmm(e, o=pq, l=x1T[tb][:, ccg * 128:(ccg + 1) * 128],
                        r=w_sb[:, ccg * c.fw + foff:ccg * c.fw + foff + fwid],
                        st=(ccg == 0), sp=(ccg == c.NCC - 1)):
                    ins = e.matmul(o, l, r, start=st, stop=sp)
                    if sp:
                        ins.then_inc(s_qkm.h, 1)
                plan.on("tensor", qmm)
            s_qkm.bump()
            secoff = foff - sec * c.qw
            if sec == 0:
                plan.wait("scalar", s_qkm, s_qkm.n)
                if t >= 2:
                    plan.wait("scalar", s_trkq, 2 * (t - 1))
                plan.on("scalar", lambda e, o=q_sb[tb][:, secoff:secoff + fwid],
                        i=pq, sc=qscale:
                        e.activation(o, i, AF.Identity, scale=sc
                                     ).then_inc(s_qkeA.h, 1))
                qke_evac[gb] = (s_qkeA, s_qkeA.bump())
            elif sec == 1:
                plan.wait("vector", s_qkm, s_qkm.n)
                if t >= 2:
                    plan.wait("vector", s_trkq, 2 * (t - 1))
                plan.on("vector", lambda e, o=k_sb[tb][:, secoff:secoff + fwid],
                        i=pq: e.tensor_copy(o, i).then_inc(s_qkeD.h, 1))
                qke_evac[gb] = (s_qkeD, s_qkeD.bump())
            else:
                plan.wait("vector", s_qkm, s_qkm.n)
                if t >= 2:
                    plan.wait("vector", s_stv[tb], 16 * (t // 2))
                plan.on("vector", lambda e, o=vstage[tb][:, secoff:secoff + fwid],
                        i=pq: e.tensor_copy(o, i).then_inc(s_qkeD.h, 1))
                qke_evac[gb] = (s_qkeD, s_qkeD.bump())
        if t + 1 < c.NT:
            emit_xtrans(t + 1)
        qkeD_tile = s_qkeD.n
        kevac_done = qke_evac[t * NFB + len(c.fblocks) // 3 * 2 - 1][1]
        plan.wait("scalar", s_qkeD, kevac_done)
        if t >= 1:
            plan.wait("scalar", s_nrd, 2 * t)
        plan.on("scalar", lambda e, src=q_sb[tb], dst=sqn:
                e.activation(dst, src, AF.Square).then_inc(s_nsq.h, 1))
        s_nsq.bump()
        plan.on("scalar", lambda e, src=k_sb[tb], dst=sqk:
                e.activation(dst, src, AF.Square).then_inc(s_nsq.h, 1))
        s_nsq.bump()
        plan.wait("vector", s_nsq, s_nsq.n)
        plan.on("vector", lambda e, o=qn2[:, t * c.hc:(t + 1) * c.hc]
                .rearrange("p (h o) -> p h o", o=1),
                i=sqn.rearrange("p (h d) -> p h d", h=c.hc):
                e.reduce_sum(o, i, axis=AX.X).then_inc(s_nrd.h, 1))
        s_nrd.bump()

        def knred(e, i=sqk.rearrange("p (h d) -> p h d", h=c.hc), t=t):
            e.reduce_sum(kn2t.rearrange("p (h o) -> p h o", o=1), i, axis=AX.X)
            e.drain()
            if t == 0:
                e.tensor_copy(kn2max, kn2t).then_inc(s_nrd.h, 1)
            else:
                e.tensor_max(kn2max, kn2max, kn2t).then_inc(s_nrd.h, 1)
        plan.on("vector", knred)
        s_nrd.bump()
        cosbc = cosx[:, t * 32:(t + 1) * 32].unsqueeze(1) \
            .broadcast_to([128, c.hc, 32])
        sinbc = sinx[:, t * 32:(t + 1) * 32].unsqueeze(1) \
            .broadcast_to([128, c.hc, 32])
        sin4 = sinbc.rearrange("p h (two x) -> p h two x", two=2)
        for buf in (q_sb[tb], k_sb[tb]):
            b3 = buf.rearrange("p (h d) -> p h d", h=c.hc)
            rot_v = b3[:, :, 0:32]
            shuf = b3[:, :, 0:32].rearrange(
                "p h (two x) -> p h two x", two=2)[:, :, ::-1, :]
            rA3 = rotA.rearrange("p (h d) -> p h d", h=c.hc)
            rB4 = rotB.rearrange("p (h two x) -> p h two x", h=c.hc, two=2)
            rB3 = rotB.rearrange("p (h d) -> p h d", h=c.hc)
            plan.wait("vector", s_nsq, s_nsq.n)

            def dorot(e, rv=rot_v, sh=shuf, rA=rA3, rB4=rB4, rB=rB3,
                      cb=cosbc, s4=sin4):
                e.tensor_tensor(rA, rv, cb, ALU.mult)
                e.tensor_tensor(rB4, sh, s4, ALU.mult)
                e.drain()
                e.tensor_add(rv, rA, rB).then_inc(s_rot.h, 1)
            plan.on("vector", dorot)
            s_rot.bump()
        def emit_kq_tail(tl):
            tbl = tl % 2
            for buf, stoff in ((k_sb[tbl], 0), (q_sb[tbl], c.qw)):
                plan.wait("tensor", s_rot, 2 * (tl + 1))
                gi = 2 * tl + (0 if stoff == 0 else 1)
                if gi >= 2:
                    plan.wait("tensor", s_trkqe, gi - 1)
                pb = bank(5 + gi % 2, BF16)[:, 0:c.qw]
                for hl in range(c.hc):
                    def trkq(e, src=buf[:, hl * 128:(hl + 1) * 128],
                             dst=pb[:, hl * 128:(hl + 1) * 128],
                             fin=(hl == c.hc - 1)):
                        ins = e.transpose(dst, src, identb)
                        if fin:
                            ins.then_inc(s_trkq.h, 1)
                    plan.on("tensor", trkq)
                s_trkq.bump()
                plan.wait("vector", s_trkq, s_trkq.n)
                if tl >= 2:
                    plan.wait("vector", s_stkq[tbl], 32 * (tl // 2))
                plan.on("vector", lambda e, o=kqstage[tbl][:, stoff:stoff + c.qw],
                        i=pb: e.tensor_copy(o, i).then_inc(s_trkqe.h, 1))
                s_trkqe.bump()
            plan.wait("gpsimd", s_trkqe, s_trkqe.n)

            def dkq(e, src=kqstage[tbl], t=tl):
                e.dma_start(out=kt_d[:, :, t * 128:(t + 1) * 128]
                            .rearrange("h p d -> p h d"),
                            in_=src[:, 0:c.qw].rearrange("p (h d) -> p h d",
                                                         h=c.hc)
                            ).then_inc(s_stkq[t % 2].h, 16)
                e.dma_start(out=qt_d[:, :, t * 128:(t + 1) * 128]
                            .rearrange("h p d -> p h d"),
                            in_=src[:, c.qw:2 * c.qw].rearrange(
                                "p (h d) -> p h d", h=c.hc)
                            ).then_inc(s_stkq[t % 2].h, 16)
            plan.on("gpsimd", dkq)
            s_stkq[tl % 2].bump(32)

        if t >= 1:
            emit_kq_tail(t - 1)
        if t == c.NT - 1:
            emit_kq_tail(t)
        plan.wait("gpsimd", s_qkeD, qkeD_tile)
        plan.on("gpsimd", lambda e, src=vstage[tb], t=t:
                e.dma_start(out=v_d[t], in_=src).then_inc(s_stv[t % 2].h, 16))
        s_stv[t % 2].bump(16)

    for sx in (s_stkq[0], s_stkq[1], s_stv[0], s_stv[1]):
        plan.wait("gpsimd", sx, sx.n)
    plan.wait("vector", s_nrd, s_nrd.n)

    pbF = bank(4, F32)
    plan.wait("tensor", s_nrd, s_nrd.n)
    plan.wait("tensor", s_qkeA, s_qkeA.n)
    plan.wait("tensor", s_qkeD, s_qkeD.n)
    plan.on("tensor", lambda e: e.transpose(pbF[0:c.hc, 0:128], kn2max, identf)
            .then_inc(s_fin.h, 1))
    s_fin.bump()
    plan.wait("vector", s_fin, s_fin.n)
    plan.on("vector", lambda e: e.tensor_copy(kn2row[0:c.hc, :],
                                              pbF[0:c.hc, 0:128])
            .then_inc(s_fin.h, 1))
    s_fin.bump()
    plan.wait("vector", s_fin, s_fin.n)
    plan.on("vector", lambda e: e.reduce_max(st_sb[0:c.hc, 0:1],
                                             kn2row[0:c.hc, :], axis=AX.X)
            .then_inc(s_fin.h, 1))
    s_fin.bump()
    plan.wait("tensor", s_fin, s_fin.n)
    plan.on("tensor", lambda e: e.transpose(pbF[0:1, 128:128 + c.hc],
                                            st_sb[0:c.hc, 0:1],
                                            identf[0:c.hc, 0:c.hc])
            .then_inc(s_fin.h, 1))
    s_fin.bump()
    plan.wait("vector", s_fin, s_fin.n)
    plan.on("vector", lambda e: e.tensor_copy(km2row_bf[0:1, :],
                                              pbF[0:1, 128:128 + c.hc])
            .then_inc(s_fin.h, 1))
    s_fin.bump()
    plan.wait("tensor", s_fin, s_fin.n)
    plan.on("tensor", lambda e: e.matmul(pbF[:, 256:256 + c.hc],
                                         onesb[0:1, :], km2row_bf[0:1, :],
                                         start=True, stop=True)
            .then_inc(s_fin.h, 1))
    s_fin.bump()
    plan.wait("vector", s_fin, s_fin.n)
    plan.on("vector", lambda e: e.tensor_copy(km2rep, pbF[:, 256:256 + c.hc])
            .then_inc(s_fin.h, 1))
    s_fin.bump()
    plan.wait("vector", s_fin, s_fin.n)
    km2bc = km2rep.unsqueeze(1).broadcast_to([128, c.NT, c.hc])
    def _negbmul(e):
        e.drain()
        e.tensor_tensor(
            negb.rearrange("p (t h) -> p t h", h=c.hc),
            qn2.rearrange("p (t h) -> p t h", h=c.hc), km2bc, ALU.mult
        ).then_inc(s_fin.h, 1)
    plan.on("vector", _negbmul)
    s_fin.bump()
    plan.wait("scalar", s_fin, s_fin.n)
    plan.on("scalar", lambda e: e.activation(negb, negb, AF.Sqrt)
            .then_inc(s_fin.h, 1))
    s_fin.bump()
    plan.wait("vector", s_fin, s_fin.n)
    plan.on("vector", lambda e: e.tensor_scalar_mul(negb, negb, -1.0)
            .then_inc(s_fin.h, 1))
    s_fin.bump()
    NEGB = s_fin.n

    s_bar1 = S("s_bar1")
    barrier(plan, s_bar1)

    if debug_phase == 1:
        s_dbg = S("s_dbg")

        def dump1(e):
            e.dma_start(out=dbg["kt"], in_=kt_d).then_inc(s_dbg.h, 16)
            e.dma_start(out=dbg["qt"], in_=qt_d).then_inc(s_dbg.h, 16)
            e.dma_start(out=dbg["v"], in_=v_d).then_inc(s_dbg.h, 16)
            e.dma_start(out=dbg["nb"], in_=negb).then_inc(s_dbg.h, 16)
            e.wait_ge(s_dbg.h, 64)
        plan.on("sync", dump1)
        return finish()

    A.pop()
    A.push()
    wout_sb = A.take(c.NFC * c.D, BF16, "wout_sb")
    A.push()
    ktb = [A.take(c.n, BF16, f"ktb{i}") for i in range(2)]
    qtb = [A.take(c.n, BF16, f"qtb{i}") for i in range(2)]
    vb = [A.take(c.NT * 128, BF16, f"vb{i}") for i in range(2)]
    P_sb = [A.take(c.n, BF16, f"P{i}") for i in range(2)]
    PT_sb = [A.take(c.n, BF16, f"PT{i}") for i in range(2)]
    ot_sb = [A.take(c.n, BF16, f"ot{i}") for i in range(2)]

    s_kqp = [S("s_kq0"), S("s_kq1")]
    s_simm = S("s_simm")
    s_exp = S("s_exp")
    s_pn = S("s_pn")
    s_ptr = S("s_ptr")
    s_pte = S("s_pte")
    s_av = S("s_av")
    s_ote = S("s_ote")
    s_otdp = [S("s_otd0"), S("s_otd1")]
    s_wo = S("s_wo")
    s_cc = S("s_cc")

    NV = 4 if c.NT % 4 == 0 else 1
    PERHEAD_KQ = 16 * (2 + NV)

    def load_head(e, hl):
        e.dma_start(out=ktb[hl % 2], in_=kt_d[hl]).then_inc(s_kqp[hl % 2].h, 16)
        e.dma_start(out=qtb[hl % 2], in_=qt_d[hl]).then_inc(s_kqp[hl % 2].h, 16)
        vv = v_d[:, :, hl * 128:(hl + 1) * 128]
        step = c.NT // NV
        for part in range(0, c.NT, step):
            sl = vv[part:part + step].rearrange("t p d -> p t d")
            dst = vb[hl % 2][:, part * 128:(part + step) * 128].rearrange(
                "p (t d) -> p t d", t=step)
            e.dma_start(out=dst, in_=sl).then_inc(s_kqp[hl % 2].h, 16)

    plan.on("sync", lambda e: load_head(e, 0))
    s_kqp[0].bump(PERHEAD_KQ)

    def pwout(e):
        for fc in range(c.NFC):
            e.dma_start(out=wout_sb[:, fc * c.D:(fc + 1) * c.D],
                        in_=woutT_d[fc * 128:(fc + 1) * 128, :]
                        ).then_inc(s_wo.h, 16)
    plan.on("gpsimd", pwout)
    s_wo.bump(16 * c.NFC)
    WO_READY = s_wo.n

    sim_cnt = 0
    tr_cnt = 0
    chunk_exp = {}
    nexp_step = {}
    steps = c.hc * c.NT
    exp_done_step = {}
    tr_last = {}
    pn_done = {}
    pte_done = {}
    av_done = {}
    ote_done = {}

    for s in range(steps + 2):
        if s < steps:
            hl, j = s // c.NT, s % c.NT
            nc_j = c.nkc[j]
            plen = 128 * (j + 1)
            if j == 0:
                plan.wait("tensor", s_kqp[hl % 2], PERHEAD_KQ * (hl // 2 + 1))
                if hl == 0:
                    plan.wait("tensor", s_fin, NEGB)
                    plan.wait("scalar", s_fin, NEGB)
            if j == 2 and hl + 1 < c.hc:
                if hl >= 1:
                    plan.wait("sync", s_ote, ote_done[hl * c.NT - 1])
                plan.on("sync", lambda e, h=hl + 1: load_head(e, h))
                s_kqp[(hl + 1) % 2].bump(PERHEAD_KQ)
            if s >= 2:
                plan.wait("scalar", s_ptr, tr_last[s - 2])
            pend = None
            n_expi = 0
            for kc in range(nc_j):
                slot = sim_cnt % 4
                wchunk = min(512, plen - kc * 512)
                if sim_cnt >= 4:
                    plan.wait("tensor", s_exp, chunk_exp[sim_cnt - 4])
                diag = (kc == nc_j - 1)
                pq = bank(slot)[:, 0:wchunk]

                def smm(e, o=pq, l=qtb[hl % 2][:, j * 128:(j + 1) * 128],
                        r=ktb[hl % 2][:, kc * 512:kc * 512 + wchunk], diag=diag):
                    ins = e.matmul(o, l, r, start=True, stop=not diag)
                    if not diag:
                        ins.then_inc(s_simm.h, 1)
                plan.on("tensor", smm)
                if diag:
                    r0 = 128 * (j % 4)

                    def mmm(e, o=pq, msl=maskx[:, 384 - r0:384 - r0 + wchunk]):
                        e.matmul(o, identb, msl, start=False, stop=True
                                 ).then_inc(s_simm.h, 1)
                    plan.on("tensor", mmm)
                sim_cnt += 1
                s_simm.bump()
                if slot % 2 == 0 and not diag:
                    pend = (kc, slot)
                    continue
                plan.wait("scalar", s_simm, s_simm.n)
                bia = negb[:, j * c.hc + hl:j * c.hc + hl + 1]
                acol = lparts[:, (s % 2) * 8 + n_expi:(s % 2) * 8 + n_expi + 1]
                if pend is not None and pend[1] == slot - 1:
                    kc0 = pend[0]
                    pw = 512 + wchunk

                    def dexp2(e, o=P_sb[s % 2][:, kc0 * 512:kc0 * 512 + pw],
                              i=pool[:, (slot - 1) * 512:
                                     (slot - 1) * 512 + pw],
                              bia=bia, a=acol):
                        e.activation(o, i, AF.Exp, bias=bia, accum_out=a
                                     ).then_inc(s_exp.h, 1)
                    plan.on("scalar", dexp2)
                    s_exp.bump()
                    chunk_exp[sim_cnt - 2] = s_exp.n
                    chunk_exp[sim_cnt - 1] = s_exp.n
                else:
                    def dexp(e, o=P_sb[s % 2][:, kc * 512:kc * 512 + wchunk],
                             i=pq, bia=bia, a=acol):
                        e.activation(o, i, AF.Exp, bias=bia, accum_out=a
                                     ).then_inc(s_exp.h, 1)
                    plan.on("scalar", dexp)
                    s_exp.bump()
                    chunk_exp[sim_cnt - 1] = s_exp.n
                n_expi += 1
                pend = None
            exp_done_step[s] = s_exp.n
            nexp_step[s] = n_expi
        sb = s - 1
        if 0 <= sb < steps:
            hl, j = sb // c.NT, sb % c.NT
            plen = 128 * (j + 1)
            plan.wait("vector", s_exp, exp_done_step[sb])

            def dnorm_stats(e,
                            lp=lparts[:, (sb % 2) * 8:
                                      (sb % 2) * 8 + nexp_step[sb]]):
                e.reduce_sum(lsum, lp, axis=AX.X)
                e.drain()
                e.reciprocal(linv, lsum)
                e.drain()
            plan.on("vector", dnorm_stats)
            pn_base = s_pn.n
            for g0 in range(0, plen, 512):
                gw = min(512, plen - g0)
                plan.on("vector", lambda e, pp=P_sb[sb % 2], g0=g0, gw=gw:
                        e.tensor_scalar_mul(pp[:, g0:g0 + gw],
                                            pp[:, g0:g0 + gw], linv
                                            ).then_inc(s_pn.h, 1))
                s_pn.bump()
            pn_done[sb] = s_pn.n
        if 0 <= sb < steps:
            hl, j = sb // c.NT, sb % c.NT
            pn_base_c = pn_done[sb] - (plen + 511) // 512
            ngrp = (j + 4) // 4
            for g in range(ngrp):
                plan.wait("tensor", s_pn, min(pn_done[sb], pn_base_c + g + 1))
                k0 = g * 4
                kcs = list(range(k0, min(k0 + 4, j + 1)))
                if tr_cnt >= 2:
                    plan.wait("tensor", s_pte, tr_cnt - 1)
                pb = bank(4 + tr_cnt % 2, BF16)
                for idx, kc2 in enumerate(kcs):
                    def ptr(e, src=P_sb[sb % 2][:, kc2 * 128:(kc2 + 1) * 128],
                            dst=pb[:, idx * 128:(idx + 1) * 128],
                            fin=(idx == len(kcs) - 1)):
                        ins = e.transpose(dst, src, identb)
                        if fin:
                            ins.then_inc(s_ptr.h, 1)
                    plan.on("tensor", ptr)
                tr_cnt += 1
                s_ptr.bump()
                plan.wait("vector", s_ptr, s_ptr.n)
                plan.on("vector", lambda e,
                        o=PT_sb[sb % 2][:, k0 * 128:k0 * 128 + len(kcs) * 128],
                        i=pb[:, 0:len(kcs) * 128]:
                        e.tensor_copy(o, i).then_inc(s_pte.h, 1))
                s_pte.bump()
                pte_done[(sb, g)] = s_pte.n
            tr_last[sb] = s_ptr.n
            if sb >= 2:
                plan.wait("tensor", s_ote, ote_done[sb - 2])
            for kc2 in range(j + 1):
                if kc2 % 4 == 0:
                    plan.wait("tensor", s_pte, pte_done[(sb, kc2 // 4)])

                def avm(e, o=bank(6 + sb % 2)[:, 0:128],
                        l=vb[hl % 2][:, kc2 * 128:(kc2 + 1) * 128],
                        r=PT_sb[sb % 2][:, kc2 * 128:(kc2 + 1) * 128],
                        st=(kc2 == 0), sp=(kc2 == j)):
                    ins = e.matmul(o, l, r, start=st, stop=sp)
                    if sp:
                        ins.then_inc(s_av.h, 1)
                plan.on("tensor", avm)
            s_av.bump()
            av_done[sb] = s_av.n
        sd_ = s - 2
        if 0 <= sd_ < steps:
            hl, j = sd_ // c.NT, sd_ % c.NT
            plan.wait("vector", s_av, av_done[sd_])
            if hl >= 2 and j == 0:
                plan.wait("vector", s_otdp[hl % 2], 16 * (hl // 2))
            plan.on("vector", lambda e,
                    o=ot_sb[hl % 2][:, j * 128:(j + 1) * 128],
                    i=bank(6 + sd_ % 2)[:, 0:128]:
                    e.tensor_copy(o, i).then_inc(s_ote.h, 1))
            s_ote.bump()
            ote_done[sd_] = s_ote.n
            if j == c.NT - 1:
                plan.wait("sync", s_ote, s_ote.n)
                plan.on("sync", lambda e, h=hl, grp=hl // hh, loc=hl % hh:
                        e.dma_start(
                    out=ccsrc[grp][loc * 128:(loc + 1) * 128, :],
                    in_=ot_sb[h % 2]).then_inc(s_otdp[h % 2].h, 16))
                s_otdp[hl % 2].bump(16)
                if hl == hh - 1 or hl == c.hc - 1:
                    grp = hl // hh
                    plan.wait("gpsimd", s_otdp[0], s_otdp[0].n)
                    plan.wait("gpsimd", s_otdp[1], s_otdp[1].n)
                    plan.on("gpsimd", lambda e, g=grp: e.collective_compute(
                        "AllGather", ALU.bypass, replica_groups=PAIRS,
                        ins=[ccsrc_t[g].ap().opt()],
                        outs=[ccdst_t[g].ap().opt()]).then_inc(s_cc.h, 1))
                    s_cc.bump()

    for eng in ENGS:
        plan.wait(eng, s_cc, s_cc.n)

    s_bar2 = S("s_bar2")
    barrier(plan, s_bar2)

    if debug_phase == 2:
        s_dbg = S("s_dbg")

        def dump2(e):
            d4 = dbg["ot"].rearrange("(r h p) n -> r h p n", r=2, h=c.hc, p=128)
            for i in range(2):
                s4 = ccdst[i].rearrange("(r h p) n -> r h p n", r=2, h=hh, p=128)
                for r in range(2):
                    e.dma_start(out=d4[r, i * hh:(i + 1) * hh], in_=s4[r]
                                ).then_inc(s_dbg.h, 16)
            e.wait_ge(s_dbg.h, 64)
        plan.on("sync", dump2)
        return finish()

    A.pop()
    A.push()
    otA = [A.take(c.F, BF16, f"otA{i}") for i in range(2)]
    otB = [A.take(c.F, BF16, f"otB{i}") for i in range(2)]
    otb = [A.take(c.F, BF16, f"otb{i}") for i in range(2)]
    xo = [A.take(c.D, F32, f"xo{i}") for i in range(2)]
    y_sb = [A.take(c.D, F32, f"y_sb{i}") for i in range(2)]
    yn_sb = A.take(c.D, F32, "yn_sb")
    xatt_sb = A.take(c.D, F32, "xatt_sb")
    x2_sb = [A.take(c.D, BF16, f"x2_sb{i}") for i in range(2)]
    x2st = A.take(c.D, BF16, "x2st")

    s_otl = S("s_otl")
    s_bl = S("s_bl")
    s_ymm = S("s_ymm")
    s_ye = S("s_ye")
    s_xolp = [S("s_xol0"), S("s_xol1")]
    s_ln2A = S("s_ln2A")
    s_ln2D = S("s_ln2D")
    s_tr25 = S("s_tr25")
    s_tr25e = S("s_tr25e")
    s_x2d = S("s_x2d")
    s_xad = S("s_xad")

    cc3 = [ccdst[i].rearrange("(r h p) n -> r h p n", r=2, h=hh, p=128)
           for i in range(2)]
    ncb = (c.D + 511) // 512
    nb2 = 2 if c.NCC > 8 else 1
    per2 = c.NCC // nb2

    plan.on("sync", lambda e: e.dma_start(out=xo[0], in_=xown_d[0:128, :])
            .then_inc(s_xolp[0].h, 16))
    s_xolp[0].bump(16)
    prev_aval = 0

    def lot(e, tt):
        for r in range(2):
            for g in range(2):
                for hs in range(2):
                    col = hs * c.half + tt * 128
                    src = cc3[g][r, :, :, col:col + 128].rearrange(
                        "h p d -> p h d")
                    base = r * c.hc * 128 + g * hh * 128
                    dst = (otA if hs == 0 else otB)[tt % 2][
                        :, base:base + hh * 128].rearrange(
                        "p (h d) -> p h d", h=hh)
                    e.dma_start(out=dst, in_=src).then_inc(s_otl.h, 16)

    plan.on("sync", lambda e: lot(e, 0))
    s_otl.bump(128)
    otl_done = {0: s_otl.n}
    lny_done = {}
    aval_of = {}
    x2_done = {}

    for tt in range(c.HT):
        tb = tt % 2
        if tt + 1 < c.HT:
            if tt >= 1:
                plan.wait("sync", s_bl, tt)
            plan.on("sync", lambda e, t2=tt + 1: lot(e, t2))
            s_otl.bump(128)
            otl_done[tt + 1] = s_otl.n
        def emit_blend(tl):
            plan.wait("vector", s_otl, otl_done[tl])
            if tl >= 2:
                plan.wait("vector", s_ymm, tl - 1)

            def blend(e, tbl=tl % 2):
                e.tensor_scalar_mul(otA[tbl], otA[tbl], selA)
                e.drain()
                e.scalar_tensor_tensor(otb[tbl], otB[tbl], selB, otA[tbl],
                                       ALU.mult, ALU.add).then_inc(s_bl.h, 1)
            plan.on("vector", blend)
            s_bl.bump()

        if tt == 0:
            emit_blend(0)
        plan.wait("tensor", s_bl, tt + 1)
        if tt == 0:
            plan.wait("tensor", s_wo, WO_READY)
        else:
            plan.wait("tensor", s_ye, tt)
        for fc in range(c.NFC):
            for cb in range(ncb):
                cw = min(512, c.D - cb * 512)

                def ymm(e, o=bank(cb)[:, 0:cw],
                        l=otb[tb][:, fc * 128:(fc + 1) * 128],
                        r=wout_sb[:, fc * c.D + cb * 512:fc * c.D + cb * 512 + cw],
                        st=(fc == 0), sp=(fc == c.NFC - 1),
                        fin=(fc == c.NFC - 1 and cb == ncb - 1)):
                    ins = e.matmul(o, l, r, start=st, stop=sp)
                    if fin:
                        ins.then_inc(s_ymm.h, 1)
                plan.on("tensor", ymm)
        s_ymm.bump()
        plan.wait("scalar", s_ymm, s_ymm.n)
        if tt >= 2:
            plan.wait("scalar", s_ln2A, lny_done[tt - 2])
        plan.on("scalar", lambda e, o=y_sb[tb]:
                e.activation(o, pool[:, 0:c.D], AF.Identity).then_inc(s_ye.h, 1))
        s_ye.bump()
        if tt + 1 < c.HT:
            emit_blend(tt + 1)

        def chains_ln(tl):
            tbl = tl % 2
            sem, val = emit_ln(plan, c, s_ln2A, s_ln2D, lnbufs, y_sb[tbl],
                               yn_sb, s_ye, tl + 1,
                               dst_free=(s_ln2D, aval_of[tl - 1])
                               if tl >= 1 else None)
            lny_done[tl] = val
            plan.wait("vector", s_ln2A, val)
            plan.wait("vector", s_xolp[tbl], 16 * (tl // 2 + 1))
            if tl >= 1:
                plan.wait("vector", s_xad, 16 * tl)
            plan.on("vector", lambda e, tbl=tbl:
                    e.tensor_add(xatt_sb, yn_sb, xo[tbl]).then_inc(s_ln2D.h, 1))
            aval_of[tl] = s_ln2D.bump()
            sem2, val2 = emit_ln(plan, c, s_ln2A, s_ln2D, lnbufs, xatt_sb,
                                 x2_sb[tbl], s_ln2D, aval_of[tl],
                                 dst_free=(s_tr25, max(0, (tl - 1) * nb2)))
            x2_done[tl] = val2
            plan.wait("gpsimd", sem2, val2)
            plan.on("gpsimd", lambda e, t2=tl: e.dma_start(
                out=xatt_d[t2 * 128:(t2 + 1) * 128, :], in_=xatt_sb)
                .then_inc(s_xad.h, 16))
            s_xad.bump(16)

        def chains_tr(tl):
            tbl = tl % 2
            plan.wait("tensor", s_ln2A, x2_done[tl])
            for bti in range(nb2):
                gi = tl * nb2 + bti
                if gi >= 2:
                    plan.wait("tensor", s_tr25e, gi - 1)
                pb = bank(4 + gi % 2, BF16)
                for u in range(per2):
                    ccg = bti * per2 + u

                    def tr25(e, src=x2_sb[tbl][:, ccg * 128:(ccg + 1) * 128],
                             dst=pb[:, u * 128:(u + 1) * 128],
                             fin=(u == per2 - 1)):
                        ins = e.transpose(dst, src, identb)
                        if fin:
                            ins.then_inc(s_tr25.h, 1)
                    plan.on("tensor", tr25)
                s_tr25.bump()
                plan.wait("vector", s_tr25, s_tr25.n)
                if tl >= 1:
                    plan.wait("vector", s_x2d, 16 * tl)
                plan.on("vector", lambda e,
                        o=x2st[:, bti * per2 * 128:(bti + 1) * per2 * 128],
                        i=pb[:, 0:per2 * 128]:
                        e.tensor_copy(o, i).then_inc(s_tr25e.h, 1))
                s_tr25e.bump()
            plan.wait("gpsimd", s_tr25e, s_tr25e.n)
            plan.on("gpsimd", lambda e, t2=tl: e.dma_start(
                out=x2t_d[:, :, t2 * 128:(t2 + 1) * 128]
                .rearrange("cc p d -> p cc d"),
                in_=x2st.rearrange("p (cc d) -> p cc d", cc=c.NCC))
                .then_inc(s_x2d.h, 16))
            s_x2d.bump(16)

        if tt >= 2:
            chains_tr(tt - 2)
        if tt >= 1:
            chains_ln(tt - 1)
        if tt + 1 < c.HT:
            if tt >= 1:
                plan.wait("sync", s_ln2D, aval_of[tt - 1])
            plan.on("sync", lambda e, i=(tt + 1) % 2, t2=tt + 1: e.dma_start(
                out=xo[i], in_=xown_d[t2 * 128:(t2 + 1) * 128, :])
                .then_inc(s_xolp[i].h, 16))
            s_xolp[(tt + 1) % 2].bump(16)
        if tt == c.HT - 1:
            chains_ln(tt)
            chains_tr(tt - 1)
            chains_tr(tt)

    plan.wait("gpsimd", s_x2d, s_x2d.n)
    plan.wait("gpsimd", s_xad, s_xad.n)
    s_bar3 = S("s_bar3")
    barrier(plan, s_bar3)

    if debug_phase == 3:
        s_dbg = S("s_dbg")

        def dump3(e):
            e.dma_start(out=dbg["xatt"], in_=xatt_d).then_inc(s_dbg.h, 16)
            e.dma_start(out=dbg["x2t"], in_=x2t_d).then_inc(s_dbg.h, 16)
            e.wait_ge(s_dbg.h, 32)
        plan.on("sync", dump3)
        return finish()

    A.pop()
    A.pop()
    A.push()
    x2tb_sb = A.take(c.NCC * c.TB, BF16, "x2tb")
    fuse_sb = A.take(c.NOC * c.TB, BF16, "fuseT")
    wdslot = [A.take(2 * c.D, BF16, f"wd{i}") for i in range(2)]
    xa3 = A.take(c.D, F32, "xa3")
    wstar = A.take(c.NCC * 512, BF16, "wstar")
    A.push()
    wslot = [A.take(c.NCC * 512, BF16, f"wslot{i}") for i in range(3)]
    gsilu_sb = A.take(4 * c.TB, BF16, "gsilu")
    A.pop()
    gu_mark = A.off
    A.push()
    zT_sb = A.take(8 * c.TB, F32, "zT")
    z_sb = A.take(c.NTS * c.D, F32, "z_sb")
    _pad = A.take((4 * c.TB * 2 + 31) // 32 * 32 // 4, F32, "pad_gsilu")
    zn_sb = A.take(c.D, F32, "zn")
    A.pop()
    xout_sb = xa3

    s_x2b = S("s_x2b")
    s_wgs = [S("s_wg0"), S("s_wg1"), S("s_wg2"), S("s_wg3")]
    s_gmm = S("s_gmm")
    s_umm = S("s_umm")
    s_sil = S("s_sil")
    s_fu = S("s_fu")
    s_wd3p = [S("s_wd30"), S("s_wd31")]
    s_zmm = S("s_zmm")
    s_ze = S("s_ze")
    s_ztr = S("s_ztr")
    s_ztre = S("s_ztre")
    s_ln3A = S("s_ln3A")
    s_ln3D = S("s_ln3D")
    s_xa3 = S("s_xa3")
    s_od = S("s_od")
    s_dbg3 = S("s_dbg3")
    s_blk = S("s_blk")
    prev_ov = 0
    aval3 = {}

    wload_val = {}

    hcc2 = max(1, c.NCC // 2)

    def emit_x2bl(b):
        if b >= 1:
            plan.wait("sync", s_umm, 4 * c.NOB * b)

        def x2bl(e, b=b, hcc=hcc2):
            for part in range(0, c.NCC, hcc):
                e.dma_start(
                    out=x2tb_sb[:, part * c.TB:(part + hcc) * c.TB]
                    .rearrange("p (cc t) -> p cc t", cc=hcc),
                    in_=x2t_d[part:part + hcc, :, b * c.TB:(b + 1) * c.TB]
                    .rearrange("cc p t -> p cc t")).then_inc(s_x2b.h, 16)
        plan.on("sync", x2bl)
        s_x2b.bump(16 * (c.NCC // hcc2))

    emit_x2bl(0)
    for bb in range(c.NB):
        X2B = 16 * (c.NCC // hcc2) * (bb + 1)
        base_fu = s_fu.n

        def emit_wload2(bb2, idx):
            ob, which = idx // 2, ("g" if idx % 2 == 0 else "u")
            bg = 4 * c.NOB * bb2
            if idx == 0:
                slot, buf = 3, wstar
                if bb2 >= 1:
                    plan.wait("gpsimd", s_gmm, bg - 4 * (c.NOB - 1))
            else:
                slot = (idx - 1) % 3
                buf = wslot[slot]
                if bb2 >= 1 and idx <= 3:
                    if idx == 1:
                        ncg = (c.NCC + min(8, c.NCC) - 1) // min(8, c.NCC)
                        plan.wait("gpsimd", s_ztre, bb2 * ncg * c.NTS)
                    elif idx == 2:
                        plan.wait("gpsimd", s_ln3D, aval3[(bb2 - 1, 1)])
                    else:
                        plan.wait("gpsimd", s_ln3D,
                                  aval3[(bb2 - 1, c.NTS - 1)])
                if idx >= 4:
                    pob, pw = (idx - 3) // 2, (idx - 3) % 2
                    semref = s_gmm if pw == 0 else s_umm
                    plan.wait("gpsimd", semref, bg + 4 * (pob + 1))
            src = (wg_d if which == "g" else wu_d)[ob].rearrange(
                "(cc p) o -> p cc o", p=128)
            hcc = max(1, c.NCC // 2)

            def wl(e, src=src, slot=slot, buf=buf, hcc=hcc):
                for part in range(0, c.NCC, hcc):
                    e.dma_start(
                        out=buf[:, part * 512:(part + hcc) * 512]
                        .rearrange("p (cc o) -> p cc o", cc=hcc),
                        in_=src[:, part:part + hcc, :]
                    ).then_inc(s_wgs[slot].h, 16)
            plan.on("gpsimd", wl)
            s_wgs[slot].bump(16 * (c.NCC // hcc))
            wload_val[(bb2, idx)] = (slot, s_wgs[slot].n, buf)

        if bb == 0:
            for _i in range(min(4, 2 * c.NOB)):
                emit_wload2(0, _i)
        nextload = 4
        for ob in range(c.NOB):
            gsl, gv, gbuf = wload_val[(bb, 2 * ob)]
            usl, uv, ubuf = wload_val[(bb, 2 * ob + 1)]
            plan.wait("tensor", s_wgs[gsl], gv)
            plan.wait("tensor", s_x2b, X2B)
            if bb >= 1 and ob == 0:
                plan.wait("tensor", s_ztre, s_ztre.n)
            for oc in range(4):
                if s_gmm.n - 4 * c.NOB * bb >= 3:
                    plan.wait("tensor", s_sil, s_gmm.n - 2)
                pg = bank((0, 1, 4)[(ob * 4 + oc) % 3])[:, 0:c.TB]
                for ci in range(c.NCC):
                    def gmm(e, o=pg,
                            l=gbuf[:, ci * 512 + oc * 128:
                                   ci * 512 + (oc + 1) * 128],
                            r=x2tb_sb[:, ci * c.TB:(ci + 1) * c.TB],
                            st=(ci == 0), sp=(ci == c.NCC - 1)):
                        ins = e.matmul(o, l, r, start=st, stop=sp)
                        if sp:
                            ins.then_inc(s_gmm.h, 1)
                    plan.on("tensor", gmm)
                s_gmm.bump()
                plan.wait("scalar", s_gmm, s_gmm.n)
                ocg = ob * 4 + oc
                if ob >= 1:
                    plan.wait("scalar", s_fu, base_fu + (ob - 1) * 4 + oc + 1)

                def sil(e, o=gsilu_sb[:, oc * c.TB:(oc + 1) * c.TB], i=pg,
                        sc=sg_sb[:, ocg:ocg + 1]):
                    e.activation(o, i, AF.Silu, scale=sc).then_inc(s_sil.h, 1)
                plan.on("scalar", sil)
                s_sil.bump()
            if nextload < 2 * c.NOB:
                emit_wload2(bb, nextload)
                nextload += 1
            elif bb + 1 < c.NB and nextload < 2 * c.NOB + 1:
                emit_wload2(bb + 1, nextload - 2 * c.NOB)
                nextload += 1
            plan.wait("tensor", s_wgs[usl], uv)
            for oc in range(4):
                if s_umm.n - 4 * c.NOB * bb >= 3:
                    plan.wait("tensor", s_fu, s_umm.n - 2)
                pu = bank((2, 3, 5)[(ob * 4 + oc) % 3])[:, 0:c.TB]
                for ci in range(c.NCC):
                    def umm(e, o=pu,
                            l=ubuf[:, ci * 512 + oc * 128:
                                   ci * 512 + (oc + 1) * 128],
                            r=x2tb_sb[:, ci * c.TB:(ci + 1) * c.TB],
                            st=(ci == 0), sp=(ci == c.NCC - 1)):
                        ins = e.matmul(o, l, r, start=st, stop=sp)
                        if sp:
                            ins.then_inc(s_umm.h, 1)
                    plan.on("tensor", umm)
                s_umm.bump()
                ocg = ob * 4 + oc
                plan.wait("vector", s_umm, s_umm.n)
                plan.wait("vector", s_sil, s_sil.n)

                def fu(e, o=fuse_sb[:, ocg * c.TB:(ocg + 1) * c.TB], i=pu,
                       sc=su_sb[:, ocg:ocg + 1],
                       g=gsilu_sb[:, oc * c.TB:(oc + 1) * c.TB]):
                    e.scalar_tensor_tensor(o, i, sc, g, ALU.mult, ALU.mult
                                           ).then_inc(s_fu.h, 1)
                plan.on("vector", fu)
                s_fu.bump()
            if nextload < 2 * c.NOB:
                emit_wload2(bb, nextload)
                nextload += 1
            elif bb + 1 < c.NB and nextload < 2 * c.NOB + 1:
                emit_wload2(bb + 1, nextload - 2 * c.NOB)
                nextload += 1
        while bb + 1 < c.NB and nextload < 2 * c.NOB + 1:
            emit_wload2(bb + 1, nextload - 2 * c.NOB)
            nextload += 1
        FUSE_ALL = s_fu.n
        if debug_phase == 6:
            barrier(plan, s_blk)
        if debug_phase == 4:
            plan.wait("sync", s_x2b, X2B)
            plan.on("sync", lambda e, b=bb: e.dma_start(
                out=dbg["x2b"][b], in_=x2tb_sb).then_inc(s_dbg3.h, 16))
            s_dbg3.bump(16)
            plan.wait("sync", s_fu, FUSE_ALL)
            plan.on("sync", lambda e, b=bb: e.dma_start(
                out=dbg["fuse"][b], in_=fuse_sb).then_inc(s_dbg3.h, 16))
            s_dbg3.bump(16)
            dbg_gate = s_dbg3.n
        gsz = min(8, c.NCC)
        n_ccg = (c.NCC + gsz - 1) // gsz
        for ccg in range(n_ccg):
            plan.wait("tensor", s_fu, FUSE_ALL)
            if bb > 0 or ccg > 0:
                plan.wait("tensor", s_ze, s_ze.n)
                plan.wait("tensor", s_ztre, s_ztre.n)
            npair = (c.NOC + 1) // 2
            for pk in range(npair):
                og0 = 2 * pk
                ng = min(2, c.NOC - og0)
                wq = "gpsimd"
                if s_zmm.n >= 2 * 2:
                    plan.wait(wq, s_zmm, s_zmm.n - 2)
                plan.on(wq, lambda e, o=wdslot[pk % 2], og0=og0, ng=ng,
                        pkb=pk % 2:
                        e.dma_start(
                            out=o[:, 0:ng * c.D].rearrange(
                                "p (g cc) -> p g cc", g=ng),
                            in_=wd_d[og0 * 128:(og0 + ng) * 128, :].rearrange(
                                "(g p) cc -> p g cc", p=128)
                        ).then_inc(s_wd3p[pkb].h, 16))
                s_wd3p[pk % 2].bump(16)
                plan.wait("tensor", s_wd3p[pk % 2], s_wd3p[pk % 2].n)
                for g in range(ng):
                    ocg = og0 + g
                    for ci in range(gsz):
                        cc = ccg * gsz + ci

                        def zmm(e, o=bank(ci)[:, 0:c.TB],
                                l=wdslot[pk % 2][:, g * c.D + cc * 128:
                                                 g * c.D + (cc + 1) * 128],
                                r=fuse_sb[:, ocg * c.TB:(ocg + 1) * c.TB],
                                st=(ocg == 0), sp=(ocg == c.NOC - 1),
                                fin=(ci == gsz - 1)):
                            ins = e.matmul(o, l, r, start=st, stop=sp)
                            if fin:
                                ins.then_inc(s_zmm.h, 1)
                        plan.on("tensor", zmm)
                    s_zmm.bump()
            plan.wait("scalar", s_zmm, s_zmm.n)
            for ci in range(gsz):
                cc = ccg * gsz + ci

                def zev(e, o=zT_sb[:, ci * c.TB:(ci + 1) * c.TB],
                        i=bank(ci)[:, 0:c.TB], sc=sd_sb[:, cc:cc + 1]):
                    e.activation(o, i, AF.Identity, scale=sc).then_inc(s_ze.h, 1)
                plan.on("scalar", zev)
                s_ze.bump()
            ZE = s_ze.n
            plan.wait("tensor", s_ze, ZE)
            for ts in range(c.NTS):
                gi = (bb * n_ccg + ccg) * c.NTS + ts
                if debug_phase == 8:
                    barrier(plan, s_blk)
                if gi >= 2:
                    plan.wait("tensor", s_ztre, gi - 1)
                for ci in range(gsz):
                    def ztr(e, src=zT_sb[:, ci * c.TB + ts * 128:
                                         ci * c.TB + (ts + 1) * 128],
                            ci=ci, gi=gi, fin=(ci == gsz - 1)):
                        b = bank((gi % 2) * 2 + ci // 4)
                        dd = b[:, (ci % 4) * 128:(ci % 4 + 1) * 128]
                        ins = e.transpose(dd, src, identf)
                        if fin:
                            ins.then_inc(s_ztr.h, 1)
                    plan.on("tensor", ztr)
                s_ztr.bump()
                plan.wait("vector", s_ztr, s_ztr.n)

                def zevac(e, ts=ts, ccg=ccg, gi=gi, gsz=gsz):
                    nleft = gsz * 128
                    co = ts * c.D + ccg * gsz * 128
                    w0 = min(512, nleft)
                    b0 = bank((gi % 2) * 2)
                    ins = e.tensor_copy(z_sb[:, co:co + w0], b0[:, 0:w0])
                    if nleft > 512:
                        b1 = bank((gi % 2) * 2 + 1)
                        ins = e.tensor_copy(z_sb[:, co + 512:co + nleft],
                                            b1[:, 0:nleft - 512])
                    ins.then_inc(s_ztre.h, 1)
                plan.on("vector", zevac)
                s_ztre.bump()
        ZTRE_ALL = s_ztre.n
        if bb + 1 < c.NB:
            emit_x2bl(bb + 1)
            emit_wload2(bb + 1, 1)
        if debug_phase == 4:
            plan.wait("sync", s_ztre, ZTRE_ALL)
            plan.on("sync", lambda e, b=bb: e.dma_start(
                out=dbg["z"][b], in_=z_sb).then_inc(s_dbg3.h, 16))
            s_dbg3.bump(16)
            plan.wait("gpsimd", s_dbg3, s_dbg3.n)
            plan.wait("vector", s_dbg3, s_dbg3.n)
        if debug_phase == 5:
            barrier(plan, s_blk)
        if debug_phase == 7:
            barrier(plan, s_blk)
        if debug_phase == 10:
            barrier(plan, s_blk)
            plan.on("sync", lambda e, b=bb: e.dma_start(
                out=dbg["z"][b], in_=z_sb).then_inc(s_dbg3.h, 16))
            s_dbg3.bump(16)
            for eng in ENGS:
                plan.wait(eng, s_dbg3, s_dbg3.n)
        for ts in range(c.NTS):
            row0 = bb * c.TB + ts * 128
            if s_xa3.n >= 16:
                plan.wait("sync", s_od, s_od.n)
            plan.on("sync", lambda e, r=row0: e.dma_start(
                out=xa3, in_=xatt_d[r:r + 128, :]).then_inc(s_xa3.h, 16))
            s_xa3.bump(16)
            zt = z_sb[:, ts * c.D:(ts + 1) * c.D]
            sem, val = emit_ln(plan, c, s_ln3A, s_ln3D, lnbufs, zt, zn_sb,
                               s_ztre, ZTRE_ALL,
                               dst_free=(s_ln3D, prev_ov) if prev_ov else None)
            plan.wait("vector", sem, val)
            plan.wait("vector", s_xa3, s_xa3.n)
            plan.on("vector", lambda e: e.tensor_add(xout_sb, zn_sb, xa3)
                    .then_inc(s_ln3D.h, 1))
            ov = s_ln3D.bump()
            prev_ov = ov
            aval3[(bb, ts)] = ov
            plan.wait("sync", s_ln3D, ov)
            plan.on("sync", lambda e, r=row0: e.dma_start(
                out=out_d[r:r + 128, :], in_=xout_sb).then_inc(s_od.h, 16))
            s_od.bump(16)
        if bb + 1 < c.NB:
            emit_wload2(bb + 1, 2)
            emit_wload2(bb + 1, 3)

    plan.wait("sync", s_od, s_od.n)
    return finish()



def kernel(L0_attn_x, L0_attn_w_qkv, L0_attn_w_out, L0_attn_g,
           L0_ffn_w_gate, L0_ffn_w_up, L0_ffn_w_down,
           L0_ffn_s_gate, L0_ffn_s_up, L0_ffn_s_down):
    cfg = Cfg(n=4096, D=2048, hc=8, inter=8192)
    x = np.asarray(L0_attn_x, np.float32)
    in_maps = prep_in_maps(
        cfg, [x[b] for b in range(x.shape[0])],
        np.asarray(L0_attn_w_qkv, np.float32),
        np.asarray(L0_attn_w_out, np.float32),
        np.asarray(L0_ffn_w_gate),
        np.asarray(L0_ffn_w_up),
        np.asarray(L0_ffn_w_down),
        np.asarray(L0_ffn_s_gate, np.float32),
        np.asarray(L0_ffn_s_up, np.float32),
        np.asarray(L0_ffn_s_down, np.float32))
    nc = build_nc(cfg)
    res = run_bass_kernel_spmd(nc, in_maps, core_ids=list(range(8)))
    out = np.empty((4, 4096, 2048), np.float32)
    for core in range(8):
        b, p = core // 2, core % 2
        out[b, p * cfg.half:(p + 1) * cfg.half] = res.results[core]["out"]
    return out


# version 21
# speedup vs baseline: 1.0059x; 1.0059x over previous
import numpy as np
import ml_dtypes
import concourse.bass as bass
import concourse.mybir as mybir
from concourse.bass_utils import run_bass_kernel_spmd

F32 = mybir.dt.float32
BF16 = mybir.dt.bfloat16
U8 = mybir.dt.uint8
I8 = mybir.dt.int8
AF = mybir.ActivationFunctionType
ALU = mybir.AluOpType
AX = mybir.AxisListType

BF = ml_dtypes.bfloat16
NEG_BIG = -1.0e30
LN_EPS = 1e-5
PAIRS = [[0, 1], [2, 3], [4, 5], [6, 7]]
ENGS = ("sync", "tensor", "vector", "scalar", "gpsimd")


class Cfg:
    def __init__(self, n=4096, D=2048, hc=8, inter=8192):
        self.n, self.D, self.hc, self.inter = n, D, hc, inter
        self.dh = 128
        self.H = 2 * hc
        self.F = self.H * self.dh
        self.NT = n // 128
        self.NCC = D // 128
        self.qw = hc * self.dh
        self.fw = 3 * self.qw
        self.half = n // 2
        self.HT = self.half // 128
        self.TB = min(512, self.half)
        self.NB = self.half // self.TB
        self.NTS = self.TB // 128
        self.NOC = inter // 128
        self.NOB = inter // 512
        self.NFC = self.F // 128
        sec_blocks = []
        w = self.qw
        nblk = (w + 511) // 512
        for sec in range(3):
            for b in range(nblk):
                o = b * 512
                sec_blocks.append((sec, sec * w + o, min(512, w - o)))
        self.fblocks = sec_blocks
        self.nkc = [(128 * (j + 1) + 511) // 512 for j in range(self.NT)]
        assert self.qw <= 1024, "k/q transpose staging assumes qw <= 1024"



def np_ln(v):
    m = v.mean(-1, keepdims=True)
    var = ((v - m) ** 2).mean(-1, keepdims=True)
    return (v - m) / np.sqrt(var + LN_EPS)


def np_ref_batch(cfg, x, w_qkv, w_out, wg, wu, wd, sg, su, sd):
    n, D = x.shape
    H, dh, ROT = cfg.H, cfg.dh, 32
    Fi = H * dh
    x1 = np_ln(x)
    qkv = x1 @ w_qkv.T
    q, k, v = qkv[:, :Fi], qkv[:, Fi:2 * Fi], qkv[:, 2 * Fi:]

    def heads(t):
        return t.reshape(n, H, dh).transpose(1, 0, 2)
    q, k, v = map(heads, (q, k, v))
    half = ROT // 2
    inv = 10000.0 ** (-(np.arange(0, ROT, 2)) / ROT)
    th = np.arange(n)[:, None] * inv[None, :]
    cos, sin = np.cos(th).astype(np.float32), np.sin(th).astype(np.float32)

    def rot(t):
        r = t.copy()
        a, b = t[..., 0:half], t[..., half:ROT]
        r[..., 0:half] = a * cos - b * sin
        r[..., half:ROT] = b * cos + a * sin
        return r
    q, k = rot(q), rot(k)
    q = q * dh ** -0.5
    mask = np.triu(np.ones((n, n), bool), 1)
    O = np.zeros((H, n, dh), np.float32)
    for h in range(H):
        s = (q[h] @ k[h].T).astype(np.float32)
        s[mask] = -np.finfo(np.float32).max
        s = s - s.max(-1, keepdims=True)
        p = np.exp(s)
        p /= p.sum(-1, keepdims=True)
        O[h] = p @ v[h]
    O = O.transpose(1, 0, 2).reshape(n, Fi)
    y = O @ w_out.T
    x = x + np_ln(y)
    x2 = np_ln(x)
    gate = (x2 @ wg.astype(np.float32)) * sg
    up = (x2 @ wu.astype(np.float32)) * su
    fuse = gate / (1.0 + np.exp(-gate)) * up
    z = (fuse @ wd.astype(np.float32)) * sd
    return x + np_ln(z)



def host_tables(cfg):
    n, NT = cfg.n, cfg.NT
    inv = 10000.0 ** (-(np.arange(0, 32, 2)) / 32)
    pos = np.arange(n)[:, None] * inv[None, :]
    cos = np.cos(pos).astype(np.float32)
    sin = np.sin(pos).astype(np.float32)
    cosx = np.zeros((128, NT * 32), np.float32)
    sinx = np.zeros((128, NT * 32), np.float32)
    for t in range(NT):
        rows = slice(t * 128, (t + 1) * 128)
        cosx[:, t * 32:t * 32 + 16] = cos[rows]
        cosx[:, t * 32 + 16:t * 32 + 32] = cos[rows]
        sinx[:, t * 32:t * 32 + 16] = -sin[rows]
        sinx[:, t * 32 + 16:t * 32 + 32] = sin[rows]
    ii = np.arange(128)[:, None]
    cc = np.arange(896)[None, :]
    maskx = np.where(cc <= ii + 384, np.float32(0.0),
                     np.float32(NEG_BIG)).astype(BF)
    return dict(cosx=cosx, sinx=sinx, maskx=maskx,
                identb=np.eye(128, dtype=BF),
                identf=np.eye(128, dtype=np.float32),
                qscale=np.full((128, 1), cfg.dh ** -0.5, np.float32),
                onesb=np.ones((1, 128), BF))


def prep_in_maps(cfg, xb, w_qkv, w_out, wg, wu, wd, sg, su, sd):
    c = cfg
    tabs = host_tables(c)
    Fi = c.F
    wgt = np.ascontiguousarray(wg.reshape(c.D, c.NOB, 512).transpose(1, 0, 2))
    wut = np.ascontiguousarray(wu.reshape(c.D, c.NOB, 512).transpose(1, 0, 2))
    wdt = np.ascontiguousarray(wd)
    sgt = np.ascontiguousarray(sg.reshape(c.NOC, 128).T.astype(np.float32))
    sut = np.ascontiguousarray(su.reshape(c.NOC, 128).T.astype(np.float32))
    sdt = np.ascontiguousarray(sd.reshape(c.NCC, 128).T.astype(np.float32))
    woutT = np.ascontiguousarray(w_out.T.astype(np.float32))
    in_maps = []
    for core in range(2 * len(xb)):
        b, p = core // 2, core % 2
        hsl = slice(p * c.qw, (p + 1) * c.qw)
        wqkvT = np.ascontiguousarray(np.concatenate(
            [w_qkv[0 * Fi:1 * Fi][hsl].T,
             w_qkv[1 * Fi:2 * Fi][hsl].T,
             w_qkv[2 * Fi:3 * Fi][hsl].T], axis=1).astype(np.float32))
        x = np.ascontiguousarray(xb[b].astype(np.float32))
        m = dict(x=x,
                 xown=np.ascontiguousarray(x[p * c.half:(p + 1) * c.half]),
                 wqkvT=wqkvT, woutT=woutT,
                 wg=wgt, wu=wut, wd=wdt, sg=sgt, su=sut, sd=sdt,
                 selA=np.full((128, 1), 1.0 - p, np.float32),
                 selB=np.full((128, 1), float(p), np.float32))
        m.update(tabs)
        in_maps.append(m)
    return in_maps



class Plan:
    def __init__(self):
        self.ops = {e: [] for e in ENGS}

    def on(self, eng, fn):
        self.ops[eng].append(fn)

    def wait(self, eng, sem, val):
        if val > 0:
            self.on(eng, lambda e, s=sem, v=val: e.wait_ge(s.h, v))

    def replay(self, block):
        deco = dict(sync=block.sync, tensor=block.tensor, vector=block.vector,
                    scalar=block.scalar, gpsimd=block.gpsimd)
        for eng in ENGS:
            ops = self.ops[eng]

            def body(e, ops=ops):
                for fn in ops:
                    fn(e)
            deco[eng](body)


class Sem:
    def __init__(self, nc, ctxs, name):
        cm = nc.semaphore(name)
        self.h = cm.__enter__()
        ctxs.append(cm)
        self.n = 0

    def bump(self, k=1):
        self.n += k
        return self.n


class Arena:
    def __init__(self, ap, nbytes):
        self.ap = ap
        self.nbytes = nbytes
        self.off = 0
        self.marks = []
        self.hi = 0

    def take(self, nelem, dt, name=""):
        sz = mybir.dt.size(dt)
        a = (self.off + 31) // 32 * 32
        b = a + nelem * sz
        assert b <= self.nbytes, f"arena overflow at {name}: {b} > {self.nbytes}"
        self.off = b
        self.hi = max(self.hi, b)
        return self.ap[:, a:b].bitcast(dt)

    def push(self):
        self.marks.append(self.off)

    def pop(self):
        self.off = self.marks.pop()


def barrier(plan, sem):
    tgt = sem.n + len(ENGS)
    for eng in ENGS:
        plan.on(eng, lambda e, s=sem: e.sem_inc(s.h, 1))
    sem.n = tgt
    for eng in ENGS:
        plan.wait(eng, sem, tgt)


def emit_ln(plan, c, lnA, lnD, bufs, src_ap, dst_ap, prod_sem, prod_val,
            dst_free=None):
    D = src_ap.shape[-1]
    sq, st = bufs["sq"], bufs["st"]
    ssum, ssq = st[:, 0:1], st[:, 1:2]
    mean, var = st[:, 2:3], st[:, 3:4]
    rstd, negmb = st[:, 4:5], st[:, 5:6]
    tmp, rvar = st[:, 6:7], st[:, 7:8]
    plan.wait("vector", prod_sem, prod_val)
    plan.on("vector", lambda e, o=ssum, i=src_ap:
            e.reduce_sum(o, i, axis=AX.X))
    plan.wait("scalar", prod_sem, prod_val)
    a1 = lnA.bump()
    plan.on("scalar", lambda e, o=sq[:, 0:D], i=src_ap, a=ssq:
            e.activation(o, i, AF.Square, accum_out=a).then_inc(lnA.h, 1))
    plan.wait("vector", lnA, a1)
    cD = 1.0 / D

    def stats1(e):
        e.drain()
        e.tensor_scalar_mul(mean, ssum, cD)
        e.tensor_scalar_mul(var, ssq, cD)
        e.drain()
        e.tensor_tensor(tmp, mean, mean, ALU.mult)
        e.drain()
        e.tensor_sub(var, var, tmp)
        e.drain()
        e.tensor_scalar_add(var, var, LN_EPS)
        e.drain()
        e.reciprocal(rvar, var).then_inc(lnD.h, 1)
    plan.on("vector", stats1)
    d1 = lnD.bump()
    plan.wait("scalar", lnD, d1)
    a2 = lnA.bump()
    plan.on("scalar", lambda e, o=rstd, i=rvar:
            e.activation(o, i, AF.Sqrt).then_inc(lnA.h, 1))
    plan.wait("vector", lnA, a2)

    def stats2(e):
        e.tensor_tensor(tmp, mean, rstd, ALU.mult)
        e.drain()
        e.tensor_scalar_mul(negmb, tmp, -1.0).then_inc(lnD.h, 1)
    plan.on("vector", stats2)
    d2 = lnD.bump()
    plan.wait("scalar", lnD, d2)
    if dst_free is not None:
        plan.wait("scalar", dst_free[0], dst_free[1])
    a3 = lnA.bump()
    plan.on("scalar", lambda e, o=dst_ap, i=src_ap, sc=rstd, b=negmb:
            e.activation(o, i, AF.Identity, scale=sc, bias=b).then_inc(lnA.h, 1))
    return lnA, a3



def build_nc(cfg, debug_phase=0):
    c = cfg
    nc = bass.Bass("TRN2", target_bir_lowering=False,
                   detect_race_conditions=False)

    def din(name, shape, dt):
        return nc.dram_tensor(name, shape, dt, kind="ExternalInput").ap()

    x_d = din("x", [c.n, c.D], F32)
    xown_d = din("xown", [c.half, c.D], F32)
    wqkvT_d = din("wqkvT", [c.D, c.fw], F32)
    woutT_d = din("woutT", [c.F, c.D], F32)
    wg_d = din("wg", [c.NOB, c.D, 512], I8)
    wu_d = din("wu", [c.NOB, c.D, 512], I8)
    wd_d = din("wd", [c.inter, c.D], I8)
    sg_d = din("sg", [128, c.NOC], F32)
    su_d = din("su", [128, c.NOC], F32)
    sd_d = din("sd", [128, c.NCC], F32)
    cosx_d = din("cosx", [128, c.NT * 32], F32)
    sinx_d = din("sinx", [128, c.NT * 32], F32)
    maskx_d = din("maskx", [128, 896], BF16)
    identb_d = din("identb", [128, 128], BF16)
    identf_d = din("identf", [128, 128], F32)
    qscale_d = din("qscale", [128, 1], F32)
    onesb_d = din("onesb", [1, 128], BF16)
    selA_d = din("selA", [128, 1], F32)
    selB_d = din("selB", [128, 1], F32)

    out_d = nc.dram_tensor("out", [c.half, c.D], F32, kind="ExternalOutput").ap()

    kt_d = nc.dram_tensor("kt_d", [c.hc, 128, c.n], BF16, kind="Internal").ap()
    qt_d = nc.dram_tensor("qt_d", [c.hc, 128, c.n], BF16, kind="Internal").ap()
    v_d = nc.dram_tensor("v_d", [c.NT, 128, c.qw], BF16, kind="Internal").ap()
    hh = c.hc // 2
    ccsrc_t = [nc.dram_tensor(f"ccsrc{i}", [hh * 128, c.n], BF16,
                              kind="Internal") for i in range(2)]
    ccdst_t = [nc.dram_tensor(f"ccdst{i}", [2 * hh * 128, c.n], BF16,
                              kind="Internal") for i in range(2)]
    ccsrc = [t.ap() for t in ccsrc_t]
    ccdst = [t.ap() for t in ccdst_t]
    xatt_d = nc.dram_tensor("xatt_d", [c.half, c.D], F32, kind="Internal").ap()
    x2t_d = nc.dram_tensor("x2t_d", [c.NCC, 128, c.half], BF16, kind="Internal").ap()

    dbg = {}
    if debug_phase == 1:
        dbg["kt"] = nc.dram_tensor("dbg_kt", [c.hc, 128, c.n], BF16,
                                   kind="ExternalOutput").ap()
        dbg["qt"] = nc.dram_tensor("dbg_qt", [c.hc, 128, c.n], BF16,
                                   kind="ExternalOutput").ap()
        dbg["v"] = nc.dram_tensor("dbg_v", [c.NT, 128, c.qw], BF16,
                                  kind="ExternalOutput").ap()
        dbg["nb"] = nc.dram_tensor("dbg_nb", [128, c.NT * c.hc], F32,
                                   kind="ExternalOutput").ap()
    if debug_phase == 2:
        dbg["ot"] = nc.dram_tensor("dbg_ot", [2 * c.hc * 128, c.n], BF16,
                                   kind="ExternalOutput").ap()
    if debug_phase == 3:
        dbg["xatt"] = nc.dram_tensor("dbg_xatt", [c.half, c.D], F32,
                                     kind="ExternalOutput").ap()
        dbg["x2t"] = nc.dram_tensor("dbg_x2t", [c.NCC, 128, c.half], BF16,
                                    kind="ExternalOutput").ap()
    if debug_phase in (4, 10):
        dbg["x2b"] = nc.dram_tensor("dbg_x2b", [c.NB, 128, c.NCC * c.TB], BF16,
                                    kind="ExternalOutput").ap()
        dbg["fuse"] = nc.dram_tensor("dbg_fuse", [c.NB, 128, c.NOC * c.TB], BF16,
                                     kind="ExternalOutput").ap()
        dbg["z"] = nc.dram_tensor("dbg_z", [c.NB, 128, c.NTS * c.D], F32,
                                  kind="ExternalOutput").ap()

    ARENA_BYTES = 211200
    ctxs = []
    arena_cm = nc.sbuf_tensor("arena", [128, ARENA_BYTES], U8)
    arena_t = arena_cm.__enter__()
    ctxs.append(arena_cm)
    pool_cm = nc.psum_tensor("pool", [128, 4096], F32)
    pool_t = pool_cm.__enter__()
    ctxs.append(pool_cm)
    block_cm = nc.Block()
    block = block_cm.__enter__()

    A = Arena(arena_t.ap(), ARENA_BYTES)
    pool = pool_t.ap()

    def bank(i, dt=F32, width=None):
        w = width if width is not None else (512 if dt == F32 else 1024)
        v = pool[:, i * 512:(i + 1) * 512]
        if dt != F32:
            v = v.bitcast(dt)
        return v[:, 0:w]

    plan = Plan()

    def S(name):
        return Sem(nc, ctxs, name)

    def finish():
        plan.replay(block)
        block_cm.__exit__(None, None, None)
        for cm in reversed(ctxs):
            cm.__exit__(None, None, None)
        return nc

    cosx = A.take(c.NT * 32, F32, "cosx")
    sinx = A.take(c.NT * 32, F32, "sinx")
    maskx = A.take(896, BF16, "maskx")
    identb = A.take(128, BF16, "identb")
    identf = A.take(128, F32, "identf")
    qscale = A.take(1, F32, "qscale")
    selA = A.take(1, F32, "selA")
    selB = A.take(1, F32, "selB")
    onesb = A.take(128, BF16, "onesb")
    sg_sb = A.take(c.NOC, F32, "sg")
    su_sb = A.take(c.NOC, F32, "su")
    sd_sb = A.take(c.NCC, F32, "sd")
    qn2 = A.take(c.NT * c.hc, F32, "qn2")
    kn2t = A.take(c.hc, F32, "kn2t")
    kn2max = A.take(c.hc, F32, "kn2max")
    negb = A.take(c.NT * c.hc, F32, "negb")
    km2rep = A.take(c.hc, F32, "km2rep")
    kn2row = A.take(128, F32, "kn2row")
    km2row_bf = A.take(c.hc, BF16, "km2row")
    lparts = A.take(16, F32, "lparts")
    linv = A.take(1, F32, "linv")
    lsum = A.take(1, F32, "lsum")
    st_sb = A.take(8, F32, "lnstats")
    sq_sb = A.take(c.D, BF16, "sq_ln_discard")

    s_const = S("s_const")

    def pconst(e):
        for dst, src in [(cosx, cosx_d), (sinx, sinx_d), (maskx, maskx_d),
                         (identb, identb_d), (identf, identf_d),
                         (qscale, qscale_d), (selA, selA_d), (selB, selB_d),
                         (onesb[0:1, :], onesb_d),
                         (sg_sb, sg_d), (su_sb, su_d), (sd_sb, sd_d)]:
            e.dma_start(out=dst, in_=src).then_inc(s_const.h, 16)
    plan.on("sync", pconst)
    s_const.n = 12 * 16
    CONST = s_const.n
    for eng in ("tensor", "vector", "scalar"):
        plan.wait(eng, s_const, CONST)

    lnbufs = {"sq": sq_sb, "st": st_sb}

    A.push()
    w_sb = A.take(c.NCC * c.fw, BF16, "w_sb")
    xt = [A.take(c.D, F32, f"xt{i}") for i in range(2)]
    x1 = [A.take(c.D, BF16, f"x1_{i}") for i in range(2)]
    x1T = [A.take(c.D, BF16, f"x1T{i}") for i in range(2)]
    q_sb = [A.take(c.qw, BF16, f"q_sb{i}") for i in range(2)]
    k_sb = [A.take(c.qw, BF16, f"k_sb{i}") for i in range(2)]
    rotA = A.take(c.hc * 32, F32, "rotA")
    rotB = A.take(c.hc * 32, F32, "rotB")
    kqstage = [A.take(2 * c.qw, BF16, f"kqst{i}") for i in range(2)]
    vstage = [A.take(c.qw, BF16, f"vst{i}") for i in range(2)]
    sqn = A.take(c.qw, BF16, "sqn")
    sqk = A.take(c.qw, BF16, "sqk")

    s_xp = [S("s_x0"), S("s_x1")]
    s_w = S("s_w")
    s_lnA = S("s_lnA")
    s_lnD = S("s_lnD")
    s_trx = S("s_trx")
    s_trxe = S("s_trxe")
    s_qkm = S("s_qkm")
    s_qkeA = S("s_qkeA")
    s_qkeD = S("s_qkeD")
    s_nsq = S("s_nsq")
    s_nrd = S("s_nrd")
    s_rot = S("s_rot")
    s_trkq = S("s_trkq")
    s_trkqe = S("s_trkqe")
    s_stkq = [S("s_stkq0"), S("s_stkq1")]
    s_stv = [S("s_stv0"), S("s_stv1")]
    s_fin = S("s_fin")

    def pweights(e):
        for cc in range(c.NCC):
            e.dma_start(out=w_sb[:, cc * c.fw:(cc + 1) * c.fw],
                        in_=wqkvT_d[cc * 128:(cc + 1) * 128, :]
                        ).then_inc(s_w.h, 16)
    plan.on("gpsimd", pweights)
    s_w.n = c.NCC * 16
    W_READY = s_w.n

    plan.on("sync", lambda e: e.dma_start(out=xt[0], in_=x_d[0:128, :])
            .then_inc(s_xp[0].h, 16))
    s_xp[0].bump(16)

    nbat = 2 if c.NCC > 8 else 1
    per = c.NCC // nbat
    prev_ln_done = 0
    NFB = len(c.fblocks)
    qke_evac = {}

    ln_done = {}

    def emit_xtrans(tl):
        tbl = tl % 2
        plan.wait("tensor", s_lnA, ln_done[tl])
        for bti in range(nbat):
            gi = tl * nbat + bti
            if gi >= 2:
                plan.wait("tensor", s_trxe, gi - 1)
            pb = bank(gi % 2, BF16)
            for u in range(per):
                ccg = bti * per + u

                def trx(e, src=x1[tbl][:, ccg * 128:(ccg + 1) * 128],
                        dst=pb[:, u * 128:(u + 1) * 128], fin=(u == per - 1)):
                    ins = e.transpose(dst, src, identb)
                    if fin:
                        ins.then_inc(s_trx.h, 1)
                plan.on("tensor", trx)
            s_trx.bump()
            plan.wait("vector", s_trx, s_trx.n)
            if tl >= 2 and bti == 0:
                plan.wait("vector", s_qkm, (tl - 1) * NFB)
            plan.on("vector", lambda e,
                    o=x1T[tbl][:, bti * per * 128:(bti + 1) * per * 128],
                    i=pb[:, 0:per * 128]:
                    e.tensor_copy(o, i).then_inc(s_trxe.h, 1))
            s_trxe.bump()

    for t in range(c.NT):
        tb = t % 2
        if t == 0:
            sem, val = emit_ln(plan, c, s_lnA, s_lnD, lnbufs, xt[0], x1[0],
                               s_xp[0], 16)
            ln_done[0] = val
            emit_xtrans(0)
        if t + 1 < c.NT:
            if t >= 1:
                plan.wait("sync", s_lnA, ln_done[t - 1])
            plan.on("sync", lambda e, i=(t + 1) % 2, tt=t + 1: e.dma_start(
                out=xt[i], in_=x_d[tt * 128:(tt + 1) * 128, :])
                .then_inc(s_xp[i].h, 16))
            s_xp[(t + 1) % 2].bump(16)
        if t + 1 < c.NT:
            dfree = (s_trx, t * nbat) if t >= 1 else None
            sem2, val2 = emit_ln(plan, c, s_lnA, s_lnD, lnbufs, xt[(t + 1) % 2],
                                 x1[(t + 1) % 2], s_xp[(t + 1) % 2],
                                 16 * ((t + 1) // 2 + 1), dst_free=dfree)
            ln_done[t + 1] = val2
        if t == 0:
            plan.wait("tensor", s_w, W_READY)
        plan.wait("tensor", s_trxe, (t + 1) * nbat)
        QKB = (2, 3, 4, 7)
        for fb, (sec, foff, fwid) in enumerate(c.fblocks):
            gb = t * NFB + fb
            if gb >= 4:
                psem, pval = qke_evac[gb - 4]
                plan.wait("tensor", psem, pval)
            pq = bank(QKB[gb % 4])[:, 0:fwid]
            for ccg in range(c.NCC):
                def qmm(e, o=pq, l=x1T[tb][:, ccg * 128:(ccg + 1) * 128],
                        r=w_sb[:, ccg * c.fw + foff:ccg * c.fw + foff + fwid],
                        st=(ccg == 0), sp=(ccg == c.NCC - 1)):
                    ins = e.matmul(o, l, r, start=st, stop=sp)
                    if sp:
                        ins.then_inc(s_qkm.h, 1)
                plan.on("tensor", qmm)
            s_qkm.bump()
            secoff = foff - sec * c.qw
            if sec == 0:
                plan.wait("scalar", s_qkm, s_qkm.n)
                if t >= 2:
                    plan.wait("scalar", s_trkq, 2 * (t - 1))
                plan.on("scalar", lambda e, o=q_sb[tb][:, secoff:secoff + fwid],
                        i=pq, sc=qscale:
                        e.activation(o, i, AF.Identity, scale=sc
                                     ).then_inc(s_qkeA.h, 1))
                qke_evac[gb] = (s_qkeA, s_qkeA.bump())
            elif sec == 1:
                plan.wait("vector", s_qkm, s_qkm.n)
                if t >= 2:
                    plan.wait("vector", s_trkq, 2 * (t - 1))
                plan.on("vector", lambda e, o=k_sb[tb][:, secoff:secoff + fwid],
                        i=pq: e.tensor_copy(o, i).then_inc(s_qkeD.h, 1))
                qke_evac[gb] = (s_qkeD, s_qkeD.bump())
            else:
                plan.wait("vector", s_qkm, s_qkm.n)
                if t >= 2:
                    plan.wait("vector", s_stv[tb], 16 * (t // 2))
                plan.on("vector", lambda e, o=vstage[tb][:, secoff:secoff + fwid],
                        i=pq: e.tensor_copy(o, i).then_inc(s_qkeD.h, 1))
                qke_evac[gb] = (s_qkeD, s_qkeD.bump())
        if t + 1 < c.NT:
            emit_xtrans(t + 1)
        qkeD_tile = s_qkeD.n
        kevac_done = qke_evac[t * NFB + len(c.fblocks) // 3 * 2 - 1][1]
        plan.wait("scalar", s_qkeD, kevac_done)
        if t >= 1:
            plan.wait("scalar", s_nrd, 2 * t)
        plan.on("scalar", lambda e, src=q_sb[tb], dst=sqn:
                e.activation(dst, src, AF.Square).then_inc(s_nsq.h, 1))
        s_nsq.bump()
        plan.on("scalar", lambda e, src=k_sb[tb], dst=sqk:
                e.activation(dst, src, AF.Square).then_inc(s_nsq.h, 1))
        s_nsq.bump()
        plan.wait("vector", s_nsq, s_nsq.n)
        plan.on("vector", lambda e, o=qn2[:, t * c.hc:(t + 1) * c.hc]
                .rearrange("p (h o) -> p h o", o=1),
                i=sqn.rearrange("p (h d) -> p h d", h=c.hc):
                e.reduce_sum(o, i, axis=AX.X).then_inc(s_nrd.h, 1))
        s_nrd.bump()

        def knred(e, i=sqk.rearrange("p (h d) -> p h d", h=c.hc), t=t):
            e.reduce_sum(kn2t.rearrange("p (h o) -> p h o", o=1), i, axis=AX.X)
            e.drain()
            if t == 0:
                e.tensor_copy(kn2max, kn2t).then_inc(s_nrd.h, 1)
            else:
                e.tensor_max(kn2max, kn2max, kn2t).then_inc(s_nrd.h, 1)
        plan.on("vector", knred)
        s_nrd.bump()
        cosbc = cosx[:, t * 32:(t + 1) * 32].unsqueeze(1) \
            .broadcast_to([128, c.hc, 32])
        sinbc = sinx[:, t * 32:(t + 1) * 32].unsqueeze(1) \
            .broadcast_to([128, c.hc, 32])
        sin4 = sinbc.rearrange("p h (two x) -> p h two x", two=2)
        for buf in (q_sb[tb], k_sb[tb]):
            b3 = buf.rearrange("p (h d) -> p h d", h=c.hc)
            rot_v = b3[:, :, 0:32]
            shuf = b3[:, :, 0:32].rearrange(
                "p h (two x) -> p h two x", two=2)[:, :, ::-1, :]
            rA3 = rotA.rearrange("p (h d) -> p h d", h=c.hc)
            rB4 = rotB.rearrange("p (h two x) -> p h two x", h=c.hc, two=2)
            rB3 = rotB.rearrange("p (h d) -> p h d", h=c.hc)
            plan.wait("vector", s_nsq, s_nsq.n)

            def dorot(e, rv=rot_v, sh=shuf, rA=rA3, rB4=rB4, rB=rB3,
                      cb=cosbc, s4=sin4):
                e.tensor_tensor(rA, rv, cb, ALU.mult)
                e.tensor_tensor(rB4, sh, s4, ALU.mult)
                e.drain()
                e.tensor_add(rv, rA, rB).then_inc(s_rot.h, 1)
            plan.on("vector", dorot)
            s_rot.bump()
        def emit_kq_tail(tl):
            tbl = tl % 2
            for buf, stoff in ((k_sb[tbl], 0), (q_sb[tbl], c.qw)):
                plan.wait("tensor", s_rot, 2 * (tl + 1))
                gi = 2 * tl + (0 if stoff == 0 else 1)
                if gi >= 2:
                    plan.wait("tensor", s_trkqe, gi - 1)
                pb = bank(5 + gi % 2, BF16)[:, 0:c.qw]
                for hl in range(c.hc):
                    def trkq(e, src=buf[:, hl * 128:(hl + 1) * 128],
                             dst=pb[:, hl * 128:(hl + 1) * 128],
                             fin=(hl == c.hc - 1)):
                        ins = e.transpose(dst, src, identb)
                        if fin:
                            ins.then_inc(s_trkq.h, 1)
                    plan.on("tensor", trkq)
                s_trkq.bump()
                plan.wait("vector", s_trkq, s_trkq.n)
                if tl >= 2:
                    plan.wait("vector", s_stkq[tbl], 32 * (tl // 2))
                plan.on("vector", lambda e, o=kqstage[tbl][:, stoff:stoff + c.qw],
                        i=pb: e.tensor_copy(o, i).then_inc(s_trkqe.h, 1))
                s_trkqe.bump()
            plan.wait("gpsimd", s_trkqe, s_trkqe.n)

            def dkq(e, src=kqstage[tbl], t=tl):
                e.dma_start(out=kt_d[:, :, t * 128:(t + 1) * 128]
                            .rearrange("h p d -> p h d"),
                            in_=src[:, 0:c.qw].rearrange("p (h d) -> p h d",
                                                         h=c.hc)
                            ).then_inc(s_stkq[t % 2].h, 16)
                e.dma_start(out=qt_d[:, :, t * 128:(t + 1) * 128]
                            .rearrange("h p d -> p h d"),
                            in_=src[:, c.qw:2 * c.qw].rearrange(
                                "p (h d) -> p h d", h=c.hc)
                            ).then_inc(s_stkq[t % 2].h, 16)
            plan.on("gpsimd", dkq)
            s_stkq[tl % 2].bump(32)

        if t >= 1:
            emit_kq_tail(t - 1)
        if t == c.NT - 1:
            emit_kq_tail(t)
        plan.wait("gpsimd", s_qkeD, qkeD_tile)
        plan.on("gpsimd", lambda e, src=vstage[tb], t=t:
                e.dma_start(out=v_d[t], in_=src).then_inc(s_stv[t % 2].h, 16))
        s_stv[t % 2].bump(16)

    for sx in (s_stkq[0], s_stkq[1], s_stv[0], s_stv[1]):
        plan.wait("gpsimd", sx, sx.n)
    plan.wait("vector", s_nrd, s_nrd.n)

    pbF = bank(4, F32)
    plan.wait("tensor", s_nrd, s_nrd.n)
    plan.wait("tensor", s_qkeA, s_qkeA.n)
    plan.wait("tensor", s_qkeD, s_qkeD.n)
    plan.on("tensor", lambda e: e.transpose(pbF[0:c.hc, 0:128], kn2max, identf)
            .then_inc(s_fin.h, 1))
    s_fin.bump()
    plan.wait("vector", s_fin, s_fin.n)
    plan.on("vector", lambda e: e.tensor_copy(kn2row[0:c.hc, :],
                                              pbF[0:c.hc, 0:128])
            .then_inc(s_fin.h, 1))
    s_fin.bump()
    plan.wait("vector", s_fin, s_fin.n)
    plan.on("vector", lambda e: e.reduce_max(st_sb[0:c.hc, 0:1],
                                             kn2row[0:c.hc, :], axis=AX.X)
            .then_inc(s_fin.h, 1))
    s_fin.bump()
    plan.wait("tensor", s_fin, s_fin.n)
    plan.on("tensor", lambda e: e.transpose(pbF[0:1, 128:128 + c.hc],
                                            st_sb[0:c.hc, 0:1],
                                            identf[0:c.hc, 0:c.hc])
            .then_inc(s_fin.h, 1))
    s_fin.bump()
    plan.wait("vector", s_fin, s_fin.n)
    plan.on("vector", lambda e: e.tensor_copy(km2row_bf[0:1, :],
                                              pbF[0:1, 128:128 + c.hc])
            .then_inc(s_fin.h, 1))
    s_fin.bump()
    plan.wait("tensor", s_fin, s_fin.n)
    plan.on("tensor", lambda e: e.matmul(pbF[:, 256:256 + c.hc],
                                         onesb[0:1, :], km2row_bf[0:1, :],
                                         start=True, stop=True)
            .then_inc(s_fin.h, 1))
    s_fin.bump()
    plan.wait("vector", s_fin, s_fin.n)
    plan.on("vector", lambda e: e.tensor_copy(km2rep, pbF[:, 256:256 + c.hc])
            .then_inc(s_fin.h, 1))
    s_fin.bump()
    plan.wait("vector", s_fin, s_fin.n)
    km2bc = km2rep.unsqueeze(1).broadcast_to([128, c.NT, c.hc])
    def _negbmul(e):
        e.drain()
        e.tensor_tensor(
            negb.rearrange("p (t h) -> p t h", h=c.hc),
            qn2.rearrange("p (t h) -> p t h", h=c.hc), km2bc, ALU.mult
        ).then_inc(s_fin.h, 1)
    plan.on("vector", _negbmul)
    s_fin.bump()
    plan.wait("scalar", s_fin, s_fin.n)
    plan.on("scalar", lambda e: e.activation(negb, negb, AF.Sqrt)
            .then_inc(s_fin.h, 1))
    s_fin.bump()
    plan.wait("vector", s_fin, s_fin.n)
    plan.on("vector", lambda e: e.tensor_scalar_mul(negb, negb, -1.0)
            .then_inc(s_fin.h, 1))
    s_fin.bump()
    NEGB = s_fin.n

    s_bar1 = S("s_bar1")
    barrier(plan, s_bar1)

    if debug_phase == 1:
        s_dbg = S("s_dbg")

        def dump1(e):
            e.dma_start(out=dbg["kt"], in_=kt_d).then_inc(s_dbg.h, 16)
            e.dma_start(out=dbg["qt"], in_=qt_d).then_inc(s_dbg.h, 16)
            e.dma_start(out=dbg["v"], in_=v_d).then_inc(s_dbg.h, 16)
            e.dma_start(out=dbg["nb"], in_=negb).then_inc(s_dbg.h, 16)
            e.wait_ge(s_dbg.h, 64)
        plan.on("sync", dump1)
        return finish()

    A.pop()
    A.push()
    wout_sb = A.take(c.NFC * c.D, BF16, "wout_sb")
    A.push()
    ktb = [A.take(c.n, BF16, f"ktb{i}") for i in range(2)]
    qtb = [A.take(c.n, BF16, f"qtb{i}") for i in range(2)]
    vb = [A.take(c.NT * 128, BF16, f"vb{i}") for i in range(2)]
    P_sb = [A.take(c.n, BF16, f"P{i}") for i in range(2)]
    PT_sb = [A.take(c.n, BF16, f"PT{i}") for i in range(2)]
    ot_sb = [A.take(c.n, BF16, f"ot{i}") for i in range(2)]

    s_kqp = [S("s_kq0"), S("s_kq1")]
    s_simm = S("s_simm")
    s_exp = S("s_exp")
    s_pn = S("s_pn")
    s_ptr = S("s_ptr")
    s_pte = S("s_pte")
    s_av = S("s_av")
    s_ote = S("s_ote")
    s_otdp = [S("s_otd0"), S("s_otd1")]
    s_wo = S("s_wo")
    s_cc = S("s_cc")

    NV = 4 if c.NT % 4 == 0 else 1
    PERHEAD_KQ = 16 * (2 + NV)

    def load_head(e, hl):
        e.dma_start(out=ktb[hl % 2], in_=kt_d[hl]).then_inc(s_kqp[hl % 2].h, 16)
        e.dma_start(out=qtb[hl % 2], in_=qt_d[hl]).then_inc(s_kqp[hl % 2].h, 16)
        vv = v_d[:, :, hl * 128:(hl + 1) * 128]
        step = c.NT // NV
        for part in range(0, c.NT, step):
            sl = vv[part:part + step].rearrange("t p d -> p t d")
            dst = vb[hl % 2][:, part * 128:(part + step) * 128].rearrange(
                "p (t d) -> p t d", t=step)
            e.dma_start(out=dst, in_=sl).then_inc(s_kqp[hl % 2].h, 16)

    plan.on("sync", lambda e: load_head(e, 0))
    s_kqp[0].bump(PERHEAD_KQ)

    def pwout(e):
        for fc in range(c.NFC):
            e.dma_start(out=wout_sb[:, fc * c.D:(fc + 1) * c.D],
                        in_=woutT_d[fc * 128:(fc + 1) * 128, :]
                        ).then_inc(s_wo.h, 16)
    plan.on("gpsimd", pwout)
    s_wo.bump(16 * c.NFC)
    WO_READY = s_wo.n

    sim_cnt = 0
    tr_cnt = 0
    chunk_exp = {}
    nexp_step = {}
    steps = c.hc * c.NT
    exp_done_step = {}
    tr_last = {}
    pn_done = {}
    pte_done = {}
    av_done = {}
    ote_done = {}

    for s in range(steps + 2):
        if s < steps:
            hl, j = s // c.NT, s % c.NT
            nc_j = c.nkc[j]
            plen = 128 * (j + 1)
            if j == 0:
                plan.wait("tensor", s_kqp[hl % 2], PERHEAD_KQ * (hl // 2 + 1))
                if hl == 0:
                    plan.wait("tensor", s_fin, NEGB)
                    plan.wait("scalar", s_fin, NEGB)
            if j == 2 and hl + 1 < c.hc:
                if hl >= 1:
                    plan.wait("sync", s_ote, ote_done[hl * c.NT - 1])
                plan.on("sync", lambda e, h=hl + 1: load_head(e, h))
                s_kqp[(hl + 1) % 2].bump(PERHEAD_KQ)
            if s >= 2:
                plan.wait("scalar", s_ptr, tr_last[s - 2])
            pend = None
            n_expi = 0
            for kc in range(nc_j):
                slot = sim_cnt % 4
                wchunk = min(512, plen - kc * 512)
                if sim_cnt >= 4:
                    plan.wait("tensor", s_exp, chunk_exp[sim_cnt - 4])
                diag = (kc == nc_j - 1)
                pq = bank(slot)[:, 0:wchunk]

                def smm(e, o=pq, l=qtb[hl % 2][:, j * 128:(j + 1) * 128],
                        r=ktb[hl % 2][:, kc * 512:kc * 512 + wchunk], diag=diag):
                    ins = e.matmul(o, l, r, start=True, stop=not diag)
                    if not diag:
                        ins.then_inc(s_simm.h, 1)
                plan.on("tensor", smm)
                if diag:
                    r0 = 128 * (j % 4)

                    def mmm(e, o=pq, msl=maskx[:, 384 - r0:384 - r0 + wchunk]):
                        e.matmul(o, identb, msl, start=False, stop=True
                                 ).then_inc(s_simm.h, 1)
                    plan.on("tensor", mmm)
                sim_cnt += 1
                s_simm.bump()
                if slot % 2 == 0 and not diag:
                    pend = (kc, slot)
                    continue
                plan.wait("scalar", s_simm, s_simm.n)
                bia = negb[:, j * c.hc + hl:j * c.hc + hl + 1]
                acol = lparts[:, (s % 2) * 8 + n_expi:(s % 2) * 8 + n_expi + 1]
                if pend is not None and pend[1] == slot - 1:
                    kc0 = pend[0]
                    pw = 512 + wchunk

                    def dexp2(e, o=P_sb[s % 2][:, kc0 * 512:kc0 * 512 + pw],
                              i=pool[:, (slot - 1) * 512:
                                     (slot - 1) * 512 + pw],
                              bia=bia, a=acol):
                        e.activation(o, i, AF.Exp, bias=bia, accum_out=a
                                     ).then_inc(s_exp.h, 1)
                    plan.on("scalar", dexp2)
                    s_exp.bump()
                    chunk_exp[sim_cnt - 2] = s_exp.n
                    chunk_exp[sim_cnt - 1] = s_exp.n
                else:
                    def dexp(e, o=P_sb[s % 2][:, kc * 512:kc * 512 + wchunk],
                             i=pq, bia=bia, a=acol):
                        e.activation(o, i, AF.Exp, bias=bia, accum_out=a
                                     ).then_inc(s_exp.h, 1)
                    plan.on("scalar", dexp)
                    s_exp.bump()
                    chunk_exp[sim_cnt - 1] = s_exp.n
                n_expi += 1
                pend = None
            exp_done_step[s] = s_exp.n
            nexp_step[s] = n_expi
        sb = s - 1
        if 0 <= sb < steps:
            hl, j = sb // c.NT, sb % c.NT
            plen = 128 * (j + 1)
            plan.wait("vector", s_exp, exp_done_step[sb])

            def dnorm_stats(e,
                            lp=lparts[:, (sb % 2) * 8:
                                      (sb % 2) * 8 + nexp_step[sb]]):
                e.reduce_sum(lsum, lp, axis=AX.X)
                e.drain()
                e.reciprocal(linv, lsum)
                e.drain()
            plan.on("vector", dnorm_stats)
            pn_base = s_pn.n
            for g0 in range(0, plen, 512):
                gw = min(512, plen - g0)
                plan.on("vector", lambda e, pp=P_sb[sb % 2], g0=g0, gw=gw:
                        e.tensor_scalar_mul(pp[:, g0:g0 + gw],
                                            pp[:, g0:g0 + gw], linv
                                            ).then_inc(s_pn.h, 1))
                s_pn.bump()
            pn_done[sb] = s_pn.n
        if 0 <= sb < steps:
            hl, j = sb // c.NT, sb % c.NT
            pn_base_c = pn_done[sb] - (plen + 511) // 512
            ngrp = (j + 4) // 4
            for g in range(ngrp):
                plan.wait("tensor", s_pn, min(pn_done[sb], pn_base_c + g + 1))
                k0 = g * 4
                kcs = list(range(k0, min(k0 + 4, j + 1)))
                if tr_cnt >= 2:
                    plan.wait("tensor", s_pte, tr_cnt - 1)
                pb = bank(4 + tr_cnt % 2, BF16)
                for idx, kc2 in enumerate(kcs):
                    def ptr(e, src=P_sb[sb % 2][:, kc2 * 128:(kc2 + 1) * 128],
                            dst=pb[:, idx * 128:(idx + 1) * 128],
                            fin=(idx == len(kcs) - 1)):
                        ins = e.transpose(dst, src, identb)
                        if fin:
                            ins.then_inc(s_ptr.h, 1)
                    plan.on("tensor", ptr)
                tr_cnt += 1
                s_ptr.bump()
                plan.wait("vector", s_ptr, s_ptr.n)
                plan.on("vector", lambda e,
                        o=PT_sb[sb % 2][:, k0 * 128:k0 * 128 + len(kcs) * 128],
                        i=pb[:, 0:len(kcs) * 128]:
                        e.tensor_copy(o, i).then_inc(s_pte.h, 1))
                s_pte.bump()
                pte_done[(sb, g)] = s_pte.n
            tr_last[sb] = s_ptr.n
            if sb >= 2:
                plan.wait("tensor", s_ote, ote_done[sb - 2])
            for kc2 in range(j + 1):
                if kc2 % 4 == 0:
                    plan.wait("tensor", s_pte, pte_done[(sb, kc2 // 4)])

                def avm(e, o=bank(6 + sb % 2)[:, 0:128],
                        l=vb[hl % 2][:, kc2 * 128:(kc2 + 1) * 128],
                        r=PT_sb[sb % 2][:, kc2 * 128:(kc2 + 1) * 128],
                        st=(kc2 == 0), sp=(kc2 == j)):
                    ins = e.matmul(o, l, r, start=st, stop=sp)
                    if sp:
                        ins.then_inc(s_av.h, 1)
                plan.on("tensor", avm)
            s_av.bump()
            av_done[sb] = s_av.n
        sd_ = s - 2
        if 0 <= sd_ < steps:
            hl, j = sd_ // c.NT, sd_ % c.NT
            plan.wait("vector", s_av, av_done[sd_])
            if hl >= 2 and j == 0:
                plan.wait("vector", s_otdp[hl % 2], 16 * (hl // 2))
            plan.on("vector", lambda e,
                    o=ot_sb[hl % 2][:, j * 128:(j + 1) * 128],
                    i=bank(6 + sd_ % 2)[:, 0:128]:
                    e.tensor_copy(o, i).then_inc(s_ote.h, 1))
            s_ote.bump()
            ote_done[sd_] = s_ote.n
            if j == c.NT - 1:
                plan.wait("sync", s_ote, s_ote.n)
                plan.on("sync", lambda e, h=hl, grp=hl // hh, loc=hl % hh:
                        e.dma_start(
                    out=ccsrc[grp][loc * 128:(loc + 1) * 128, :],
                    in_=ot_sb[h % 2]).then_inc(s_otdp[h % 2].h, 16))
                s_otdp[hl % 2].bump(16)
                if hl == hh - 1 or hl == c.hc - 1:
                    grp = hl // hh
                    plan.wait("gpsimd", s_otdp[0], s_otdp[0].n)
                    plan.wait("gpsimd", s_otdp[1], s_otdp[1].n)
                    plan.on("gpsimd", lambda e, g=grp: e.collective_compute(
                        "AllGather", ALU.bypass, replica_groups=PAIRS,
                        ins=[ccsrc_t[g].ap().opt()],
                        outs=[ccdst_t[g].ap().opt()]).then_inc(s_cc.h, 1))
                    s_cc.bump()

    for eng in ENGS:
        plan.wait(eng, s_cc, s_cc.n)

    s_bar2 = S("s_bar2")
    barrier(plan, s_bar2)

    if debug_phase == 2:
        s_dbg = S("s_dbg")

        def dump2(e):
            d4 = dbg["ot"].rearrange("(r h p) n -> r h p n", r=2, h=c.hc, p=128)
            for i in range(2):
                s4 = ccdst[i].rearrange("(r h p) n -> r h p n", r=2, h=hh, p=128)
                for r in range(2):
                    e.dma_start(out=d4[r, i * hh:(i + 1) * hh], in_=s4[r]
                                ).then_inc(s_dbg.h, 16)
            e.wait_ge(s_dbg.h, 64)
        plan.on("sync", dump2)
        return finish()

    A.pop()
    A.push()
    otA = [A.take(c.F, BF16, f"otA{i}") for i in range(2)]
    otB = [A.take(c.F, BF16, f"otB{i}") for i in range(2)]
    otb = [A.take(c.F, BF16, f"otb{i}") for i in range(2)]
    xo = [A.take(c.D, F32, f"xo{i}") for i in range(2)]
    y_sb = [A.take(c.D, F32, f"y_sb{i}") for i in range(2)]
    yn_sb = A.take(c.D, F32, "yn_sb")
    xatt_sb = A.take(c.D, F32, "xatt_sb")
    x2_sb = [A.take(c.D, BF16, f"x2_sb{i}") for i in range(2)]
    x2st = A.take(c.D, BF16, "x2st")

    s_otl = S("s_otl")
    s_bl = S("s_bl")
    s_ymm = S("s_ymm")
    s_ye = S("s_ye")
    s_xolp = [S("s_xol0"), S("s_xol1")]
    s_ln2A = S("s_ln2A")
    s_ln2D = S("s_ln2D")
    s_tr25 = S("s_tr25")
    s_tr25e = S("s_tr25e")
    s_x2d = S("s_x2d")
    s_xad = S("s_xad")

    cc3 = [ccdst[i].rearrange("(r h p) n -> r h p n", r=2, h=hh, p=128)
           for i in range(2)]
    ncb = (c.D + 511) // 512
    nb2 = 2 if c.NCC > 8 else 1
    per2 = c.NCC // nb2

    plan.on("sync", lambda e: e.dma_start(out=xo[0], in_=xown_d[0:128, :])
            .then_inc(s_xolp[0].h, 16))
    s_xolp[0].bump(16)
    prev_aval = 0

    def lot(e, tt):
        for r in range(2):
            for g in range(2):
                for hs in range(2):
                    col = hs * c.half + tt * 128
                    src = cc3[g][r, :, :, col:col + 128].rearrange(
                        "h p d -> p h d")
                    base = r * c.hc * 128 + g * hh * 128
                    dst = (otA if hs == 0 else otB)[tt % 2][
                        :, base:base + hh * 128].rearrange(
                        "p (h d) -> p h d", h=hh)
                    e.dma_start(out=dst, in_=src).then_inc(s_otl.h, 16)

    plan.on("sync", lambda e: lot(e, 0))
    s_otl.bump(128)
    otl_done = {0: s_otl.n}
    lny_done = {}
    aval_of = {}
    x2_done = {}

    for tt in range(c.HT):
        tb = tt % 2
        if tt + 1 < c.HT:
            if tt >= 1:
                plan.wait("sync", s_bl, tt)
            plan.on("sync", lambda e, t2=tt + 1: lot(e, t2))
            s_otl.bump(128)
            otl_done[tt + 1] = s_otl.n
        def emit_blend(tl):
            plan.wait("vector", s_otl, otl_done[tl])
            if tl >= 2:
                plan.wait("vector", s_ymm, tl - 1)

            def blend(e, tbl=tl % 2):
                e.tensor_scalar_mul(otA[tbl], otA[tbl], selA)
                e.drain()
                e.scalar_tensor_tensor(otb[tbl], otB[tbl], selB, otA[tbl],
                                       ALU.mult, ALU.add).then_inc(s_bl.h, 1)
            plan.on("vector", blend)
            s_bl.bump()

        if tt == 0:
            emit_blend(0)
        plan.wait("tensor", s_bl, tt + 1)
        if tt == 0:
            plan.wait("tensor", s_wo, WO_READY)
        else:
            plan.wait("tensor", s_ye, tt)
        for fc in range(c.NFC):
            for cb in range(ncb):
                cw = min(512, c.D - cb * 512)

                def ymm(e, o=bank(cb)[:, 0:cw],
                        l=otb[tb][:, fc * 128:(fc + 1) * 128],
                        r=wout_sb[:, fc * c.D + cb * 512:fc * c.D + cb * 512 + cw],
                        st=(fc == 0), sp=(fc == c.NFC - 1),
                        fin=(fc == c.NFC - 1 and cb == ncb - 1)):
                    ins = e.matmul(o, l, r, start=st, stop=sp)
                    if fin:
                        ins.then_inc(s_ymm.h, 1)
                plan.on("tensor", ymm)
        s_ymm.bump()
        plan.wait("scalar", s_ymm, s_ymm.n)
        if tt >= 2:
            plan.wait("scalar", s_ln2A, lny_done[tt - 2])
        plan.on("scalar", lambda e, o=y_sb[tb]:
                e.activation(o, pool[:, 0:c.D], AF.Identity).then_inc(s_ye.h, 1))
        s_ye.bump()
        if tt + 1 < c.HT:
            emit_blend(tt + 1)

        def chains_ln(tl):
            tbl = tl % 2
            sem, val = emit_ln(plan, c, s_ln2A, s_ln2D, lnbufs, y_sb[tbl],
                               yn_sb, s_ye, tl + 1,
                               dst_free=(s_ln2D, aval_of[tl - 1])
                               if tl >= 1 else None)
            lny_done[tl] = val
            plan.wait("vector", s_ln2A, val)
            plan.wait("vector", s_xolp[tbl], 16 * (tl // 2 + 1))
            if tl >= 1:
                plan.wait("vector", s_xad, 16 * tl)
            plan.on("vector", lambda e, tbl=tbl:
                    e.tensor_add(xatt_sb, yn_sb, xo[tbl]).then_inc(s_ln2D.h, 1))
            aval_of[tl] = s_ln2D.bump()
            sem2, val2 = emit_ln(plan, c, s_ln2A, s_ln2D, lnbufs, xatt_sb,
                                 x2_sb[tbl], s_ln2D, aval_of[tl],
                                 dst_free=(s_tr25, max(0, (tl - 1) * nb2)))
            x2_done[tl] = val2
            plan.wait("gpsimd", sem2, val2)
            plan.on("gpsimd", lambda e, t2=tl: e.dma_start(
                out=xatt_d[t2 * 128:(t2 + 1) * 128, :], in_=xatt_sb)
                .then_inc(s_xad.h, 16))
            s_xad.bump(16)

        def chains_tr(tl):
            tbl = tl % 2
            plan.wait("tensor", s_ln2A, x2_done[tl])
            for bti in range(nb2):
                gi = tl * nb2 + bti
                if gi >= 2:
                    plan.wait("tensor", s_tr25e, gi - 1)
                pb = bank(4 + gi % 2, BF16)
                for u in range(per2):
                    ccg = bti * per2 + u

                    def tr25(e, src=x2_sb[tbl][:, ccg * 128:(ccg + 1) * 128],
                             dst=pb[:, u * 128:(u + 1) * 128],
                             fin=(u == per2 - 1)):
                        ins = e.transpose(dst, src, identb)
                        if fin:
                            ins.then_inc(s_tr25.h, 1)
                    plan.on("tensor", tr25)
                s_tr25.bump()
                plan.wait("vector", s_tr25, s_tr25.n)
                if tl >= 1:
                    plan.wait("vector", s_x2d, 16 * tl)
                plan.on("vector", lambda e,
                        o=x2st[:, bti * per2 * 128:(bti + 1) * per2 * 128],
                        i=pb[:, 0:per2 * 128]:
                        e.tensor_copy(o, i).then_inc(s_tr25e.h, 1))
                s_tr25e.bump()
            plan.wait("gpsimd", s_tr25e, s_tr25e.n)
            plan.on("gpsimd", lambda e, t2=tl: e.dma_start(
                out=x2t_d[:, :, t2 * 128:(t2 + 1) * 128]
                .rearrange("cc p d -> p cc d"),
                in_=x2st.rearrange("p (cc d) -> p cc d", cc=c.NCC))
                .then_inc(s_x2d.h, 16))
            s_x2d.bump(16)

        if tt >= 2:
            chains_tr(tt - 2)
        if tt >= 1:
            chains_ln(tt - 1)
        if tt + 1 < c.HT:
            if tt >= 1:
                plan.wait("sync", s_ln2D, aval_of[tt - 1])
            plan.on("sync", lambda e, i=(tt + 1) % 2, t2=tt + 1: e.dma_start(
                out=xo[i], in_=xown_d[t2 * 128:(t2 + 1) * 128, :])
                .then_inc(s_xolp[i].h, 16))
            s_xolp[(tt + 1) % 2].bump(16)
        if tt == c.HT - 1:
            chains_ln(tt)
            chains_tr(tt - 1)
            chains_tr(tt)

    plan.wait("gpsimd", s_x2d, s_x2d.n)
    plan.wait("gpsimd", s_xad, s_xad.n)
    s_bar3 = S("s_bar3")
    barrier(plan, s_bar3)

    if debug_phase == 3:
        s_dbg = S("s_dbg")

        def dump3(e):
            e.dma_start(out=dbg["xatt"], in_=xatt_d).then_inc(s_dbg.h, 16)
            e.dma_start(out=dbg["x2t"], in_=x2t_d).then_inc(s_dbg.h, 16)
            e.wait_ge(s_dbg.h, 32)
        plan.on("sync", dump3)
        return finish()

    A.pop()
    A.pop()
    A.push()
    x2tb_sb = A.take(c.NCC * c.TB, BF16, "x2tb")
    fuse_sb = A.take(c.NOC * c.TB, BF16, "fuseT")
    wdslot = [A.take(2 * c.D, BF16, f"wd{i}") for i in range(2)]
    wdint = [A.take(2 * c.D, I8, f"wdint{i}") for i in range(2)]
    xa3 = A.take(c.D, F32, "xa3")
    wstar = A.take(c.NCC * 512, BF16, "wstar")
    A.push()
    wslot = [A.take(c.NCC * 512, BF16, f"wslot{i}") for i in range(3)]
    gsilu_sb = A.take(4 * c.TB, BF16, "gsilu")
    A.pop()
    gu_mark = A.off
    A.push()
    zT_sb = A.take(8 * c.TB, F32, "zT")
    z_sb = A.take(c.NTS * c.D, F32, "z_sb")
    _pad = A.take((4 * c.TB * 2 + 31) // 32 * 32 // 4, F32, "pad_gsilu")
    zn_sb = A.take(c.D, F32, "zn")
    A.pop()
    xout_sb = xa3

    s_x2b = S("s_x2b")
    s_wdi = S("s_wdi")
    s_wgs = [S("s_wg0"), S("s_wg1"), S("s_wg2"), S("s_wg3")]
    s_gmm = S("s_gmm")
    s_umm = S("s_umm")
    s_sil = S("s_sil")
    s_fu = S("s_fu")
    s_wd3p = [S("s_wd30"), S("s_wd31")]
    s_zmm = S("s_zmm")
    s_ze = S("s_ze")
    s_ztr = S("s_ztr")
    s_ztre = S("s_ztre")
    s_ln3A = S("s_ln3A")
    s_ln3D = S("s_ln3D")
    s_xa3 = S("s_xa3")
    s_od = S("s_od")
    s_dbg3 = S("s_dbg3")
    s_blk = S("s_blk")
    prev_ov = 0
    aval3 = {}
    wd_oddcnt = [0]
    wd_conv = {}

    wload_val = {}

    hcc2 = max(1, c.NCC // 2)

    def emit_x2bl(b):
        if b >= 1:
            plan.wait("sync", s_umm, 4 * c.NOB * b)

        def x2bl(e, b=b, hcc=hcc2):
            for part in range(0, c.NCC, hcc):
                e.dma_start(
                    out=x2tb_sb[:, part * c.TB:(part + hcc) * c.TB]
                    .rearrange("p (cc t) -> p cc t", cc=hcc),
                    in_=x2t_d[part:part + hcc, :, b * c.TB:(b + 1) * c.TB]
                    .rearrange("cc p t -> p cc t")).then_inc(s_x2b.h, 16)
        plan.on("sync", x2bl)
        s_x2b.bump(16 * (c.NCC // hcc2))

    emit_x2bl(0)
    for bb in range(c.NB):
        X2B = 16 * (c.NCC // hcc2) * (bb + 1)
        base_fu = s_fu.n

        def emit_wload2(bb2, idx):
            ob, which = idx // 2, ("g" if idx % 2 == 0 else "u")
            bg = 4 * c.NOB * bb2
            if idx == 0:
                slot, buf = 3, wstar
                if bb2 >= 1:
                    plan.wait("gpsimd", s_gmm, bg - 4 * (c.NOB - 1))
            else:
                slot = (idx - 1) % 3
                buf = wslot[slot]
                if bb2 >= 1 and idx <= 3:
                    if idx == 1:
                        ncg = (c.NCC + min(8, c.NCC) - 1) // min(8, c.NCC)
                        plan.wait("gpsimd", s_ztre, bb2 * ncg * c.NTS)
                    elif idx == 2:
                        plan.wait("gpsimd", s_ln3D, aval3[(bb2 - 1, 1)])
                    else:
                        plan.wait("gpsimd", s_ln3D,
                                  aval3[(bb2 - 1, c.NTS - 1)])
                if idx >= 4:
                    pob, pw = (idx - 3) // 2, (idx - 3) % 2
                    semref = s_gmm if pw == 0 else s_umm
                    plan.wait("gpsimd", semref, bg + 4 * (pob + 1))
            src = (wg_d if which == "g" else wu_d)[ob].rearrange(
                "(cc p) o -> p cc o", p=128)
            hcc = max(1, c.NCC // 2)

            def wl(e, src=src, slot=slot, buf=buf, hcc=hcc):
                for part in range(0, c.NCC, hcc):
                    e.dma_start(
                        out=buf[:, part * 512:(part + hcc) * 512]
                        .rearrange("p (cc o) -> p cc o", cc=hcc),
                        in_=src[:, part:part + hcc, :]
                    ).then_inc(s_wgs[slot].h, 16)
            plan.on("gpsimd", wl)
            s_wgs[slot].bump(16 * (c.NCC // hcc))
            wload_val[(bb2, idx)] = (slot, s_wgs[slot].n, buf)

        if bb == 0:
            for _i in range(min(4, 2 * c.NOB)):
                emit_wload2(0, _i)
        nextload = 4
        for ob in range(c.NOB):
            gsl, gv, gbuf = wload_val[(bb, 2 * ob)]
            usl, uv, ubuf = wload_val[(bb, 2 * ob + 1)]
            plan.wait("tensor", s_wgs[gsl], gv)
            plan.wait("tensor", s_x2b, X2B)
            if bb >= 1 and ob == 0:
                plan.wait("tensor", s_ztre, s_ztre.n)
            for oc in range(4):
                if s_gmm.n - 4 * c.NOB * bb >= 3:
                    plan.wait("tensor", s_sil, s_gmm.n - 2)
                pg = bank((0, 1, 4)[(ob * 4 + oc) % 3])[:, 0:c.TB]
                for ci in range(c.NCC):
                    def gmm(e, o=pg,
                            l=gbuf[:, ci * 512 + oc * 128:
                                   ci * 512 + (oc + 1) * 128],
                            r=x2tb_sb[:, ci * c.TB:(ci + 1) * c.TB],
                            st=(ci == 0), sp=(ci == c.NCC - 1)):
                        ins = e.matmul(o, l, r, start=st, stop=sp)
                        if sp:
                            ins.then_inc(s_gmm.h, 1)
                    plan.on("tensor", gmm)
                s_gmm.bump()
                plan.wait("scalar", s_gmm, s_gmm.n)
                ocg = ob * 4 + oc
                if ob >= 1:
                    plan.wait("scalar", s_fu, base_fu + (ob - 1) * 4 + oc + 1)

                def sil(e, o=gsilu_sb[:, oc * c.TB:(oc + 1) * c.TB], i=pg,
                        sc=sg_sb[:, ocg:ocg + 1]):
                    e.activation(o, i, AF.Silu, scale=sc).then_inc(s_sil.h, 1)
                plan.on("scalar", sil)
                s_sil.bump()
            if nextload < 2 * c.NOB:
                emit_wload2(bb, nextload)
                nextload += 1
            elif bb + 1 < c.NB and nextload < 2 * c.NOB + 1:
                emit_wload2(bb + 1, nextload - 2 * c.NOB)
                nextload += 1
            plan.wait("tensor", s_wgs[usl], uv)
            for oc in range(4):
                if s_umm.n - 4 * c.NOB * bb >= 3:
                    plan.wait("tensor", s_fu, s_umm.n - 2)
                pu = bank((2, 3, 5)[(ob * 4 + oc) % 3])[:, 0:c.TB]
                for ci in range(c.NCC):
                    def umm(e, o=pu,
                            l=ubuf[:, ci * 512 + oc * 128:
                                   ci * 512 + (oc + 1) * 128],
                            r=x2tb_sb[:, ci * c.TB:(ci + 1) * c.TB],
                            st=(ci == 0), sp=(ci == c.NCC - 1)):
                        ins = e.matmul(o, l, r, start=st, stop=sp)
                        if sp:
                            ins.then_inc(s_umm.h, 1)
                    plan.on("tensor", umm)
                s_umm.bump()
                ocg = ob * 4 + oc
                plan.wait("vector", s_umm, s_umm.n)
                plan.wait("vector", s_sil, s_sil.n)

                def fu(e, o=fuse_sb[:, ocg * c.TB:(ocg + 1) * c.TB], i=pu,
                       sc=su_sb[:, ocg:ocg + 1],
                       g=gsilu_sb[:, oc * c.TB:(oc + 1) * c.TB]):
                    e.scalar_tensor_tensor(o, i, sc, g, ALU.mult, ALU.mult
                                           ).then_inc(s_fu.h, 1)
                plan.on("vector", fu)
                s_fu.bump()
            if nextload < 2 * c.NOB:
                emit_wload2(bb, nextload)
                nextload += 1
            elif bb + 1 < c.NB and nextload < 2 * c.NOB + 1:
                emit_wload2(bb + 1, nextload - 2 * c.NOB)
                nextload += 1
        while bb + 1 < c.NB and nextload < 2 * c.NOB + 1:
            emit_wload2(bb + 1, nextload - 2 * c.NOB)
            nextload += 1
        FUSE_ALL = s_fu.n
        if debug_phase == 6:
            barrier(plan, s_blk)
        if debug_phase == 4:
            plan.wait("sync", s_x2b, X2B)
            plan.on("sync", lambda e, b=bb: e.dma_start(
                out=dbg["x2b"][b], in_=x2tb_sb).then_inc(s_dbg3.h, 16))
            s_dbg3.bump(16)
            plan.wait("sync", s_fu, FUSE_ALL)
            plan.on("sync", lambda e, b=bb: e.dma_start(
                out=dbg["fuse"][b], in_=fuse_sb).then_inc(s_dbg3.h, 16))
            s_dbg3.bump(16)
            dbg_gate = s_dbg3.n
        gsz = min(8, c.NCC)
        n_ccg = (c.NCC + gsz - 1) // gsz
        for ccg in range(n_ccg):
            plan.wait("tensor", s_fu, FUSE_ALL)
            if bb > 0 or ccg > 0:
                plan.wait("tensor", s_ze, s_ze.n)
                plan.wait("tensor", s_ztre, s_ztre.n)
            npair = (c.NOC + 1) // 2
            for pk in range(npair):
                og0 = 2 * pk
                ng = min(2, c.NOC - og0)
                src_ap = wd_d[og0 * 128:(og0 + ng) * 128, :].rearrange(
                    "(g p) cc -> p g cc", p=128)
                if pk % 2 == 0:
                    if s_zmm.n >= 2 * 2:
                        plan.wait("gpsimd", s_zmm, s_zmm.n - 2)
                    plan.on("gpsimd", lambda e, ng=ng, src=src_ap:
                            e.dma_start(
                                out=wdslot[0][:, 0:ng * c.D].rearrange(
                                    "p (g cc) -> p g cc", g=ng),
                                in_=src).then_inc(s_wd3p[0].h, 16))
                    s_wd3p[0].bump(16)
                else:
                    ib = wd_oddcnt[0] % 2
                    if wd_oddcnt[0] >= 2:
                        plan.wait("sync", s_wd3p[1], wd_conv[wd_oddcnt[0] - 2])
                    plan.on("sync", lambda e, ng=ng, src=src_ap, ib=ib:
                            e.dma_start(
                                out=wdint[ib][:, 0:ng * c.D].rearrange(
                                    "p (g cc) -> p g cc", g=ng),
                                in_=src).then_inc(s_wdi.h, 16))
                    s_wdi.bump(16)
                    plan.wait("vector", s_wdi, s_wdi.n)
                    if s_zmm.n >= 2 * 2:
                        plan.wait("vector", s_zmm, s_zmm.n - 2)
                    plan.on("vector", lambda e, ng=ng, ib=ib:
                            e.tensor_copy(wdslot[1][:, 0:ng * c.D],
                                          wdint[ib][:, 0:ng * c.D])
                            .then_inc(s_wd3p[1].h, 1))
                    s_wd3p[1].bump(1)
                    wd_conv[wd_oddcnt[0]] = s_wd3p[1].n
                    wd_oddcnt[0] += 1
                plan.wait("tensor", s_wd3p[pk % 2], s_wd3p[pk % 2].n)
                for g in range(ng):
                    ocg = og0 + g
                    for ci in range(gsz):
                        cc = ccg * gsz + ci

                        def zmm(e, o=bank(ci)[:, 0:c.TB],
                                l=wdslot[pk % 2][:, g * c.D + cc * 128:
                                                 g * c.D + (cc + 1) * 128],
                                r=fuse_sb[:, ocg * c.TB:(ocg + 1) * c.TB],
                                st=(ocg == 0), sp=(ocg == c.NOC - 1),
                                fin=(ci == gsz - 1)):
                            ins = e.matmul(o, l, r, start=st, stop=sp)
                            if fin:
                                ins.then_inc(s_zmm.h, 1)
                        plan.on("tensor", zmm)
                    s_zmm.bump()
            plan.wait("scalar", s_zmm, s_zmm.n)
            for ci in range(gsz):
                cc = ccg * gsz + ci

                def zev(e, o=zT_sb[:, ci * c.TB:(ci + 1) * c.TB],
                        i=bank(ci)[:, 0:c.TB], sc=sd_sb[:, cc:cc + 1]):
                    e.activation(o, i, AF.Identity, scale=sc).then_inc(s_ze.h, 1)
                plan.on("scalar", zev)
                s_ze.bump()
            ZE = s_ze.n
            plan.wait("tensor", s_ze, ZE)
            for ts in range(c.NTS):
                gi = (bb * n_ccg + ccg) * c.NTS + ts
                if debug_phase == 8:
                    barrier(plan, s_blk)
                if gi >= 2:
                    plan.wait("tensor", s_ztre, gi - 1)
                for ci in range(gsz):
                    def ztr(e, src=zT_sb[:, ci * c.TB + ts * 128:
                                         ci * c.TB + (ts + 1) * 128],
                            ci=ci, gi=gi, fin=(ci == gsz - 1)):
                        b = bank((gi % 2) * 2 + ci // 4)
                        dd = b[:, (ci % 4) * 128:(ci % 4 + 1) * 128]
                        ins = e.transpose(dd, src, identf)
                        if fin:
                            ins.then_inc(s_ztr.h, 1)
                    plan.on("tensor", ztr)
                s_ztr.bump()
                plan.wait("vector", s_ztr, s_ztr.n)

                def zevac(e, ts=ts, ccg=ccg, gi=gi, gsz=gsz):
                    nleft = gsz * 128
                    co = ts * c.D + ccg * gsz * 128
                    w0 = min(512, nleft)
                    b0 = bank((gi % 2) * 2)
                    ins = e.tensor_copy(z_sb[:, co:co + w0], b0[:, 0:w0])
                    if nleft > 512:
                        b1 = bank((gi % 2) * 2 + 1)
                        ins = e.tensor_copy(z_sb[:, co + 512:co + nleft],
                                            b1[:, 0:nleft - 512])
                    ins.then_inc(s_ztre.h, 1)
                plan.on("vector", zevac)
                s_ztre.bump()
        ZTRE_ALL = s_ztre.n
        if bb + 1 < c.NB:
            emit_x2bl(bb + 1)
            emit_wload2(bb + 1, 1)
        if debug_phase == 4:
            plan.wait("sync", s_ztre, ZTRE_ALL)
            plan.on("sync", lambda e, b=bb: e.dma_start(
                out=dbg["z"][b], in_=z_sb).then_inc(s_dbg3.h, 16))
            s_dbg3.bump(16)
            plan.wait("gpsimd", s_dbg3, s_dbg3.n)
            plan.wait("vector", s_dbg3, s_dbg3.n)
        if debug_phase == 5:
            barrier(plan, s_blk)
        if debug_phase == 7:
            barrier(plan, s_blk)
        if debug_phase == 10:
            barrier(plan, s_blk)
            plan.on("sync", lambda e, b=bb: e.dma_start(
                out=dbg["z"][b], in_=z_sb).then_inc(s_dbg3.h, 16))
            s_dbg3.bump(16)
            for eng in ENGS:
                plan.wait(eng, s_dbg3, s_dbg3.n)
        for ts in range(c.NTS):
            row0 = bb * c.TB + ts * 128
            if s_xa3.n >= 16:
                plan.wait("sync", s_od, s_od.n)
            plan.on("sync", lambda e, r=row0: e.dma_start(
                out=xa3, in_=xatt_d[r:r + 128, :]).then_inc(s_xa3.h, 16))
            s_xa3.bump(16)
            zt = z_sb[:, ts * c.D:(ts + 1) * c.D]
            sem, val = emit_ln(plan, c, s_ln3A, s_ln3D, lnbufs, zt, zn_sb,
                               s_ztre, ZTRE_ALL,
                               dst_free=(s_ln3D, prev_ov) if prev_ov else None)
            plan.wait("vector", sem, val)
            plan.wait("vector", s_xa3, s_xa3.n)
            plan.on("vector", lambda e: e.tensor_add(xout_sb, zn_sb, xa3)
                    .then_inc(s_ln3D.h, 1))
            ov = s_ln3D.bump()
            prev_ov = ov
            aval3[(bb, ts)] = ov
            plan.wait("sync", s_ln3D, ov)
            plan.on("sync", lambda e, r=row0: e.dma_start(
                out=out_d[r:r + 128, :], in_=xout_sb).then_inc(s_od.h, 16))
            s_od.bump(16)
        if bb + 1 < c.NB:
            emit_wload2(bb + 1, 2)
            emit_wload2(bb + 1, 3)

    plan.wait("sync", s_od, s_od.n)
    return finish()



def kernel(L0_attn_x, L0_attn_w_qkv, L0_attn_w_out, L0_attn_g,
           L0_ffn_w_gate, L0_ffn_w_up, L0_ffn_w_down,
           L0_ffn_s_gate, L0_ffn_s_up, L0_ffn_s_down):
    cfg = Cfg(n=4096, D=2048, hc=8, inter=8192)
    x = np.asarray(L0_attn_x, np.float32)
    in_maps = prep_in_maps(
        cfg, [x[b] for b in range(x.shape[0])],
        np.asarray(L0_attn_w_qkv, np.float32),
        np.asarray(L0_attn_w_out, np.float32),
        np.asarray(L0_ffn_w_gate),
        np.asarray(L0_ffn_w_up),
        np.asarray(L0_ffn_w_down),
        np.asarray(L0_ffn_s_gate, np.float32),
        np.asarray(L0_ffn_s_up, np.float32),
        np.asarray(L0_ffn_s_down, np.float32))
    nc = build_nc(cfg)
    res = run_bass_kernel_spmd(nc, in_maps, core_ids=list(range(8)))
    out = np.empty((4, 4096, 2048), np.float32)
    for core in range(8):
        b, p = core // 2, core % 2
        out[b, p * cfg.half:(p + 1) * cfg.half] = res.results[core]["out"]
    return out


# version 22
# speedup vs baseline: 1.0811x; 1.0747x over previous
import numpy as np
import ml_dtypes
import concourse.bass as bass
import concourse.mybir as mybir
from concourse.bass_utils import run_bass_kernel_spmd

F32 = mybir.dt.float32
BF16 = mybir.dt.bfloat16
U8 = mybir.dt.uint8
I8 = mybir.dt.int8
AF = mybir.ActivationFunctionType
ALU = mybir.AluOpType
AX = mybir.AxisListType

BF = ml_dtypes.bfloat16
NEG_BIG = -1.0e30
LN_EPS = 1e-5
PAIRS = [[0, 1], [2, 3], [4, 5], [6, 7]]
ENGS = ("sync", "tensor", "vector", "scalar", "gpsimd")


class Cfg:
    def __init__(self, n=4096, D=2048, hc=8, inter=8192):
        self.n, self.D, self.hc, self.inter = n, D, hc, inter
        self.dh = 128
        self.H = 2 * hc
        self.F = self.H * self.dh
        self.NT = n // 128
        self.NCC = D // 128
        self.qw = hc * self.dh
        self.fw = 3 * self.qw
        self.half = n // 2
        self.HT = self.half // 128
        self.TB = min(512, self.half)
        self.NB = self.half // self.TB
        self.NTS = self.TB // 128
        self.NOC = inter // 128
        self.NOB = inter // 512
        self.NFC = self.F // 128
        sec_blocks = []
        w = self.qw
        nblk = (w + 511) // 512
        for sec in range(3):
            for b in range(nblk):
                o = b * 512
                sec_blocks.append((sec, sec * w + o, min(512, w - o)))
        self.fblocks = sec_blocks
        self.nkc = [(128 * (j + 1) + 511) // 512 for j in range(self.NT)]
        assert self.qw <= 1024, "k/q transpose staging assumes qw <= 1024"



def np_ln(v):
    m = v.mean(-1, keepdims=True)
    var = ((v - m) ** 2).mean(-1, keepdims=True)
    return (v - m) / np.sqrt(var + LN_EPS)


def np_ref_batch(cfg, x, w_qkv, w_out, wg, wu, wd, sg, su, sd):
    n, D = x.shape
    H, dh, ROT = cfg.H, cfg.dh, 32
    Fi = H * dh
    x1 = np_ln(x)
    qkv = x1 @ w_qkv.T
    q, k, v = qkv[:, :Fi], qkv[:, Fi:2 * Fi], qkv[:, 2 * Fi:]

    def heads(t):
        return t.reshape(n, H, dh).transpose(1, 0, 2)
    q, k, v = map(heads, (q, k, v))
    half = ROT // 2
    inv = 10000.0 ** (-(np.arange(0, ROT, 2)) / ROT)
    th = np.arange(n)[:, None] * inv[None, :]
    cos, sin = np.cos(th).astype(np.float32), np.sin(th).astype(np.float32)

    def rot(t):
        r = t.copy()
        a, b = t[..., 0:half], t[..., half:ROT]
        r[..., 0:half] = a * cos - b * sin
        r[..., half:ROT] = b * cos + a * sin
        return r
    q, k = rot(q), rot(k)
    q = q * dh ** -0.5
    mask = np.triu(np.ones((n, n), bool), 1)
    O = np.zeros((H, n, dh), np.float32)
    for h in range(H):
        s = (q[h] @ k[h].T).astype(np.float32)
        s[mask] = -np.finfo(np.float32).max
        s = s - s.max(-1, keepdims=True)
        p = np.exp(s)
        p /= p.sum(-1, keepdims=True)
        O[h] = p @ v[h]
    O = O.transpose(1, 0, 2).reshape(n, Fi)
    y = O @ w_out.T
    x = x + np_ln(y)
    x2 = np_ln(x)
    gate = (x2 @ wg.astype(np.float32)) * sg
    up = (x2 @ wu.astype(np.float32)) * su
    fuse = gate / (1.0 + np.exp(-gate)) * up
    z = (fuse @ wd.astype(np.float32)) * sd
    return x + np_ln(z)



def host_tables(cfg):
    n, NT = cfg.n, cfg.NT
    inv = 10000.0 ** (-(np.arange(0, 32, 2)) / 32)
    pos = np.arange(n)[:, None] * inv[None, :]
    cos = np.cos(pos).astype(np.float32)
    sin = np.sin(pos).astype(np.float32)
    cosx = np.zeros((128, NT * 32), np.float32)
    sinx = np.zeros((128, NT * 32), np.float32)
    for t in range(NT):
        rows = slice(t * 128, (t + 1) * 128)
        cosx[:, t * 32:t * 32 + 16] = cos[rows]
        cosx[:, t * 32 + 16:t * 32 + 32] = cos[rows]
        sinx[:, t * 32:t * 32 + 16] = -sin[rows]
        sinx[:, t * 32 + 16:t * 32 + 32] = sin[rows]
    ii = np.arange(128)[:, None]
    cc = np.arange(896)[None, :]
    maskx = np.where(cc <= ii + 384, np.float32(0.0),
                     np.float32(NEG_BIG)).astype(BF)
    return dict(cosx=cosx, sinx=sinx, maskx=maskx,
                identb=np.eye(128, dtype=BF),
                identf=np.eye(128, dtype=np.float32),
                qscale=np.full((128, 1), cfg.dh ** -0.5, np.float32),
                onesb=np.ones((1, 128), BF))


def prep_in_maps(cfg, xb, w_qkv, w_out, wg, wu, wd, sg, su, sd):
    c = cfg
    tabs = host_tables(c)
    Fi = c.F
    wgt = np.ascontiguousarray(wg.reshape(c.D, c.NOB, 512).transpose(1, 0, 2))
    wut = np.ascontiguousarray(wu.reshape(c.D, c.NOB, 512).transpose(1, 0, 2))
    wdt = np.ascontiguousarray(wd)
    sgt = np.ascontiguousarray(sg.reshape(c.NOC, 128).T.astype(np.float32))
    sut = np.ascontiguousarray(su.reshape(c.NOC, 128).T.astype(np.float32))
    sdt = np.ascontiguousarray(sd.reshape(c.NCC, 128).T.astype(np.float32))
    woutT = np.ascontiguousarray(w_out.T.astype(np.float32))
    in_maps = []
    for core in range(2 * len(xb)):
        b, p = core // 2, core % 2
        hsl = slice(p * c.qw, (p + 1) * c.qw)
        wqkvT = np.ascontiguousarray(np.concatenate(
            [w_qkv[0 * Fi:1 * Fi][hsl].T,
             w_qkv[1 * Fi:2 * Fi][hsl].T,
             w_qkv[2 * Fi:3 * Fi][hsl].T], axis=1).astype(np.float32))
        x = np.ascontiguousarray(xb[b].astype(np.float32))
        m = dict(x=x,
                 xown=np.ascontiguousarray(x[p * c.half:(p + 1) * c.half]),
                 wqkvT=wqkvT, woutT=woutT,
                 wg=wgt, wu=wut, wd=wdt, sg=sgt, su=sut, sd=sdt,
                 selA=np.full((128, 1), 1.0 - p, np.float32),
                 selB=np.full((128, 1), float(p), np.float32))
        m.update(tabs)
        in_maps.append(m)
    return in_maps



class Plan:
    def __init__(self):
        self.ops = {e: [] for e in ENGS}

    def on(self, eng, fn):
        self.ops[eng].append(fn)

    def wait(self, eng, sem, val):
        if val > 0:
            self.on(eng, lambda e, s=sem, v=val: e.wait_ge(s.h, v))

    def replay(self, block):
        deco = dict(sync=block.sync, tensor=block.tensor, vector=block.vector,
                    scalar=block.scalar, gpsimd=block.gpsimd)
        for eng in ENGS:
            ops = self.ops[eng]

            def body(e, ops=ops):
                for fn in ops:
                    fn(e)
            deco[eng](body)


class Sem:
    def __init__(self, nc, ctxs, name):
        cm = nc.semaphore(name)
        self.h = cm.__enter__()
        ctxs.append(cm)
        self.n = 0

    def bump(self, k=1):
        self.n += k
        return self.n


class Arena:
    def __init__(self, ap, nbytes):
        self.ap = ap
        self.nbytes = nbytes
        self.off = 0
        self.marks = []
        self.hi = 0

    def take(self, nelem, dt, name=""):
        sz = mybir.dt.size(dt)
        a = (self.off + 31) // 32 * 32
        b = a + nelem * sz
        assert b <= self.nbytes, f"arena overflow at {name}: {b} > {self.nbytes}"
        self.off = b
        self.hi = max(self.hi, b)
        return self.ap[:, a:b].bitcast(dt)

    def push(self):
        self.marks.append(self.off)

    def pop(self):
        self.off = self.marks.pop()


def barrier(plan, sem):
    tgt = sem.n + len(ENGS)
    for eng in ENGS:
        plan.on(eng, lambda e, s=sem: e.sem_inc(s.h, 1))
    sem.n = tgt
    for eng in ENGS:
        plan.wait(eng, sem, tgt)


def emit_ln(plan, c, lnA, lnD, bufs, src_ap, dst_ap, prod_sem, prod_val,
            dst_free=None):
    D = src_ap.shape[-1]
    sq, st = bufs["sq"], bufs["st"]
    ssum, ssq = st[:, 0:1], st[:, 1:2]
    mean, var = st[:, 2:3], st[:, 3:4]
    rstd, negmb = st[:, 4:5], st[:, 5:6]
    tmp, rvar = st[:, 6:7], st[:, 7:8]
    plan.wait("vector", prod_sem, prod_val)
    plan.on("vector", lambda e, o=ssum, i=src_ap:
            e.reduce_sum(o, i, axis=AX.X))
    plan.wait("scalar", prod_sem, prod_val)
    a1 = lnA.bump()
    plan.on("scalar", lambda e, o=sq[:, 0:D], i=src_ap, a=ssq:
            e.activation(o, i, AF.Square, accum_out=a).then_inc(lnA.h, 1))
    plan.wait("vector", lnA, a1)
    cD = 1.0 / D

    def stats1(e):
        e.drain()
        e.tensor_scalar_mul(mean, ssum, cD)
        e.tensor_scalar_mul(var, ssq, cD)
        e.drain()
        e.tensor_tensor(tmp, mean, mean, ALU.mult)
        e.drain()
        e.tensor_sub(var, var, tmp)
        e.drain()
        e.tensor_scalar_add(var, var, LN_EPS)
        e.drain()
        e.reciprocal(rvar, var).then_inc(lnD.h, 1)
    plan.on("vector", stats1)
    d1 = lnD.bump()
    plan.wait("scalar", lnD, d1)
    a2 = lnA.bump()
    plan.on("scalar", lambda e, o=rstd, i=rvar:
            e.activation(o, i, AF.Sqrt).then_inc(lnA.h, 1))
    plan.wait("vector", lnA, a2)

    def stats2(e):
        e.tensor_tensor(tmp, mean, rstd, ALU.mult)
        e.drain()
        e.tensor_scalar_mul(negmb, tmp, -1.0).then_inc(lnD.h, 1)
    plan.on("vector", stats2)
    d2 = lnD.bump()
    plan.wait("scalar", lnD, d2)
    if dst_free is not None:
        plan.wait("scalar", dst_free[0], dst_free[1])
    a3 = lnA.bump()
    plan.on("scalar", lambda e, o=dst_ap, i=src_ap, sc=rstd, b=negmb:
            e.activation(o, i, AF.Identity, scale=sc, bias=b).then_inc(lnA.h, 1))
    return lnA, a3



def build_nc(cfg, debug_phase=0):
    c = cfg
    nc = bass.Bass("TRN2", target_bir_lowering=False,
                   detect_race_conditions=False)

    def din(name, shape, dt):
        return nc.dram_tensor(name, shape, dt, kind="ExternalInput").ap()

    x_d = din("x", [c.n, c.D], F32)
    xown_d = din("xown", [c.half, c.D], F32)
    wqkvT_d = din("wqkvT", [c.D, c.fw], F32)
    woutT_d = din("woutT", [c.F, c.D], F32)
    wg_d = din("wg", [c.NOB, c.D, 512], I8)
    wu_d = din("wu", [c.NOB, c.D, 512], I8)
    wd_d = din("wd", [c.inter, c.D], I8)
    sg_d = din("sg", [128, c.NOC], F32)
    su_d = din("su", [128, c.NOC], F32)
    sd_d = din("sd", [128, c.NCC], F32)
    cosx_d = din("cosx", [128, c.NT * 32], F32)
    sinx_d = din("sinx", [128, c.NT * 32], F32)
    maskx_d = din("maskx", [128, 896], BF16)
    identb_d = din("identb", [128, 128], BF16)
    identf_d = din("identf", [128, 128], F32)
    qscale_d = din("qscale", [128, 1], F32)
    onesb_d = din("onesb", [1, 128], BF16)
    selA_d = din("selA", [128, 1], F32)
    selB_d = din("selB", [128, 1], F32)

    out_d = nc.dram_tensor("out", [c.half, c.D], F32, kind="ExternalOutput").ap()

    kt_d = nc.dram_tensor("kt_d", [c.hc, 128, c.n], BF16, kind="Internal").ap()
    qt_d = nc.dram_tensor("qt_d", [c.hc, 128, c.n], BF16, kind="Internal").ap()
    v_d = nc.dram_tensor("v_d", [c.NT, 128, c.qw], BF16, kind="Internal").ap()
    hh = 2
    ngrp_cc = c.hc // hh
    ccsrc_t = [nc.dram_tensor(f"ccsrc{i}", [hh * 128, c.n], BF16,
                              kind="Internal") for i in range(ngrp_cc)]
    ccdst_t = [nc.dram_tensor(f"ccdst{i}", [2 * hh * 128, c.n], BF16,
                              kind="Internal") for i in range(ngrp_cc)]
    ccsrc = [t.ap() for t in ccsrc_t]
    ccdst = [t.ap() for t in ccdst_t]
    xatt_d = nc.dram_tensor("xatt_d", [c.half, c.D], F32, kind="Internal").ap()
    x2t_d = nc.dram_tensor("x2t_d", [c.NCC, 128, c.half], BF16, kind="Internal").ap()

    dbg = {}
    if debug_phase == 1:
        dbg["kt"] = nc.dram_tensor("dbg_kt", [c.hc, 128, c.n], BF16,
                                   kind="ExternalOutput").ap()
        dbg["qt"] = nc.dram_tensor("dbg_qt", [c.hc, 128, c.n], BF16,
                                   kind="ExternalOutput").ap()
        dbg["v"] = nc.dram_tensor("dbg_v", [c.NT, 128, c.qw], BF16,
                                  kind="ExternalOutput").ap()
        dbg["nb"] = nc.dram_tensor("dbg_nb", [128, c.NT * c.hc], F32,
                                   kind="ExternalOutput").ap()
    if debug_phase == 2:
        dbg["ot"] = nc.dram_tensor("dbg_ot", [2 * c.hc * 128, c.n], BF16,
                                   kind="ExternalOutput").ap()
    if debug_phase == 3:
        dbg["xatt"] = nc.dram_tensor("dbg_xatt", [c.half, c.D], F32,
                                     kind="ExternalOutput").ap()
        dbg["x2t"] = nc.dram_tensor("dbg_x2t", [c.NCC, 128, c.half], BF16,
                                    kind="ExternalOutput").ap()
    if debug_phase in (4, 10):
        dbg["x2b"] = nc.dram_tensor("dbg_x2b", [c.NB, 128, c.NCC * c.TB], BF16,
                                    kind="ExternalOutput").ap()
        dbg["fuse"] = nc.dram_tensor("dbg_fuse", [c.NB, 128, c.NOC * c.TB], BF16,
                                     kind="ExternalOutput").ap()
        dbg["z"] = nc.dram_tensor("dbg_z", [c.NB, 128, c.NTS * c.D], F32,
                                  kind="ExternalOutput").ap()

    ARENA_BYTES = 200 * 1024
    ctxs = []
    arena_cm = nc.sbuf_tensor("arena", [128, ARENA_BYTES], U8)
    arena_t = arena_cm.__enter__()
    ctxs.append(arena_cm)
    pool_cm = nc.psum_tensor("pool", [128, 4096], F32)
    pool_t = pool_cm.__enter__()
    ctxs.append(pool_cm)
    block_cm = nc.Block()
    block = block_cm.__enter__()

    A = Arena(arena_t.ap(), ARENA_BYTES)
    pool = pool_t.ap()

    def bank(i, dt=F32, width=None):
        w = width if width is not None else (512 if dt == F32 else 1024)
        v = pool[:, i * 512:(i + 1) * 512]
        if dt != F32:
            v = v.bitcast(dt)
        return v[:, 0:w]

    plan = Plan()

    def S(name):
        return Sem(nc, ctxs, name)

    def finish():
        plan.replay(block)
        block_cm.__exit__(None, None, None)
        for cm in reversed(ctxs):
            cm.__exit__(None, None, None)
        return nc

    cosx = A.take(c.NT * 32, F32, "cosx")
    sinx = A.take(c.NT * 32, F32, "sinx")
    maskx = A.take(896, BF16, "maskx")
    identb = A.take(128, BF16, "identb")
    identf = A.take(128, F32, "identf")
    qscale = A.take(1, F32, "qscale")
    selA = A.take(1, F32, "selA")
    selB = A.take(1, F32, "selB")
    onesb = A.take(128, BF16, "onesb")
    sg_sb = A.take(c.NOC, F32, "sg")
    su_sb = A.take(c.NOC, F32, "su")
    sd_sb = A.take(c.NCC, F32, "sd")
    qn2 = A.take(c.NT * c.hc, F32, "qn2")
    kn2t = A.take(c.hc, F32, "kn2t")
    kn2max = A.take(c.hc, F32, "kn2max")
    negb = A.take(c.NT * c.hc, F32, "negb")
    km2rep = A.take(c.hc, F32, "km2rep")
    kn2row = A.take(128, F32, "kn2row")
    km2row_bf = A.take(c.hc, BF16, "km2row")
    lparts = A.take(16, F32, "lparts")
    linv = A.take(1, F32, "linv")
    lsum = A.take(1, F32, "lsum")
    st_sb = A.take(8, F32, "lnstats")
    sq_sb = A.take(c.D, BF16, "sq_ln_discard")

    s_const = S("s_const")

    def pconst(e):
        for dst, src in [(cosx, cosx_d), (sinx, sinx_d), (maskx, maskx_d),
                         (identb, identb_d), (identf, identf_d),
                         (qscale, qscale_d), (selA, selA_d), (selB, selB_d),
                         (onesb[0:1, :], onesb_d),
                         (sg_sb, sg_d), (su_sb, su_d), (sd_sb, sd_d)]:
            e.dma_start(out=dst, in_=src).then_inc(s_const.h, 16)
    plan.on("sync", pconst)
    s_const.n = 12 * 16
    CONST = s_const.n
    for eng in ("tensor", "vector", "scalar"):
        plan.wait(eng, s_const, CONST)

    lnbufs = {"sq": sq_sb, "st": st_sb}

    A.push()
    w_sb = A.take(c.NCC * c.fw, BF16, "w_sb")
    xt = [A.take(c.D, F32, f"xt{i}") for i in range(2)]
    x1 = [A.take(c.D, BF16, f"x1_{i}") for i in range(2)]
    x1T = [A.take(c.D, BF16, f"x1T{i}") for i in range(2)]
    q_sb = [A.take(c.qw, BF16, f"q_sb{i}") for i in range(2)]
    k_sb = [A.take(c.qw, BF16, f"k_sb{i}") for i in range(2)]
    rotA = A.take(c.hc * 32, F32, "rotA")
    rotB = A.take(c.hc * 32, F32, "rotB")
    kqstage = [A.take(2 * c.qw, BF16, f"kqst{i}") for i in range(2)]
    vstage = [A.take(c.qw, BF16, f"vst{i}") for i in range(2)]
    sqn = A.take(c.qw, BF16, "sqn")
    sqk = A.take(c.qw, BF16, "sqk")

    s_xp = [S("s_x0"), S("s_x1")]
    s_w = S("s_w")
    s_lnA = S("s_lnA")
    s_lnD = S("s_lnD")
    s_trx = S("s_trx")
    s_trxe = S("s_trxe")
    s_qkm = S("s_qkm")
    s_qkeA = S("s_qkeA")
    s_qkeD = S("s_qkeD")
    s_nsq = S("s_nsq")
    s_nrd = S("s_nrd")
    s_rot = S("s_rot")
    s_trkq = S("s_trkq")
    s_trkqe = S("s_trkqe")
    s_stkq = [S("s_stkq0"), S("s_stkq1")]
    s_stv = [S("s_stv0"), S("s_stv1")]
    s_fin = S("s_fin")

    def pweights(e):
        for cc in range(c.NCC):
            e.dma_start(out=w_sb[:, cc * c.fw:(cc + 1) * c.fw],
                        in_=wqkvT_d[cc * 128:(cc + 1) * 128, :]
                        ).then_inc(s_w.h, 16)
    plan.on("gpsimd", pweights)
    s_w.n = c.NCC * 16
    W_READY = s_w.n

    plan.on("sync", lambda e: e.dma_start(out=xt[0], in_=x_d[0:128, :])
            .then_inc(s_xp[0].h, 16))
    s_xp[0].bump(16)

    nbat = 2 if c.NCC > 8 else 1
    per = c.NCC // nbat
    prev_ln_done = 0
    NFB = len(c.fblocks)
    qke_evac = {}

    ln_done = {}

    def emit_xtrans(tl):
        tbl = tl % 2
        plan.wait("tensor", s_lnA, ln_done[tl])
        for bti in range(nbat):
            gi = tl * nbat + bti
            if gi >= 2:
                plan.wait("tensor", s_trxe, gi - 1)
            pb = bank(gi % 2, BF16)
            for u in range(per):
                ccg = bti * per + u

                def trx(e, src=x1[tbl][:, ccg * 128:(ccg + 1) * 128],
                        dst=pb[:, u * 128:(u + 1) * 128], fin=(u == per - 1)):
                    ins = e.transpose(dst, src, identb)
                    if fin:
                        ins.then_inc(s_trx.h, 1)
                plan.on("tensor", trx)
            s_trx.bump()
            plan.wait("vector", s_trx, s_trx.n)
            if tl >= 2 and bti == 0:
                plan.wait("vector", s_qkm, (tl - 1) * NFB)
            plan.on("vector", lambda e,
                    o=x1T[tbl][:, bti * per * 128:(bti + 1) * per * 128],
                    i=pb[:, 0:per * 128]:
                    e.tensor_copy(o, i).then_inc(s_trxe.h, 1))
            s_trxe.bump()

    for t in range(c.NT):
        tb = t % 2
        if t == 0:
            sem, val = emit_ln(plan, c, s_lnA, s_lnD, lnbufs, xt[0], x1[0],
                               s_xp[0], 16)
            ln_done[0] = val
            emit_xtrans(0)
        if t + 1 < c.NT:
            if t >= 1:
                plan.wait("sync", s_lnA, ln_done[t - 1])
            plan.on("sync", lambda e, i=(t + 1) % 2, tt=t + 1: e.dma_start(
                out=xt[i], in_=x_d[tt * 128:(tt + 1) * 128, :])
                .then_inc(s_xp[i].h, 16))
            s_xp[(t + 1) % 2].bump(16)
        if t + 1 < c.NT:
            dfree = (s_trx, t * nbat) if t >= 1 else None
            sem2, val2 = emit_ln(plan, c, s_lnA, s_lnD, lnbufs, xt[(t + 1) % 2],
                                 x1[(t + 1) % 2], s_xp[(t + 1) % 2],
                                 16 * ((t + 1) // 2 + 1), dst_free=dfree)
            ln_done[t + 1] = val2
        if t == 0:
            plan.wait("tensor", s_w, W_READY)
        plan.wait("tensor", s_trxe, (t + 1) * nbat)
        QKB = (2, 3, 4, 7)
        for fb, (sec, foff, fwid) in enumerate(c.fblocks):
            gb = t * NFB + fb
            if gb >= 4:
                psem, pval = qke_evac[gb - 4]
                plan.wait("tensor", psem, pval)
            pq = bank(QKB[gb % 4])[:, 0:fwid]
            for ccg in range(c.NCC):
                def qmm(e, o=pq, l=x1T[tb][:, ccg * 128:(ccg + 1) * 128],
                        r=w_sb[:, ccg * c.fw + foff:ccg * c.fw + foff + fwid],
                        st=(ccg == 0), sp=(ccg == c.NCC - 1)):
                    ins = e.matmul(o, l, r, start=st, stop=sp)
                    if sp:
                        ins.then_inc(s_qkm.h, 1)
                plan.on("tensor", qmm)
            s_qkm.bump()
            secoff = foff - sec * c.qw
            if sec == 0:
                plan.wait("scalar", s_qkm, s_qkm.n)
                if t >= 2:
                    plan.wait("scalar", s_trkq, 2 * (t - 1))
                plan.on("scalar", lambda e, o=q_sb[tb][:, secoff:secoff + fwid],
                        i=pq, sc=qscale:
                        e.activation(o, i, AF.Identity, scale=sc
                                     ).then_inc(s_qkeA.h, 1))
                qke_evac[gb] = (s_qkeA, s_qkeA.bump())
            elif sec == 1:
                plan.wait("vector", s_qkm, s_qkm.n)
                if t >= 2:
                    plan.wait("vector", s_trkq, 2 * (t - 1))
                plan.on("vector", lambda e, o=k_sb[tb][:, secoff:secoff + fwid],
                        i=pq: e.tensor_copy(o, i).then_inc(s_qkeD.h, 1))
                qke_evac[gb] = (s_qkeD, s_qkeD.bump())
            else:
                plan.wait("vector", s_qkm, s_qkm.n)
                if t >= 2:
                    plan.wait("vector", s_stv[tb], 16 * (t // 2))
                plan.on("vector", lambda e, o=vstage[tb][:, secoff:secoff + fwid],
                        i=pq: e.tensor_copy(o, i).then_inc(s_qkeD.h, 1))
                qke_evac[gb] = (s_qkeD, s_qkeD.bump())
        if t + 1 < c.NT:
            emit_xtrans(t + 1)
        qkeD_tile = s_qkeD.n
        kevac_done = qke_evac[t * NFB + len(c.fblocks) // 3 * 2 - 1][1]
        plan.wait("scalar", s_qkeD, kevac_done)
        if t >= 1:
            plan.wait("scalar", s_nrd, 2 * t)
        plan.on("scalar", lambda e, src=q_sb[tb], dst=sqn:
                e.activation(dst, src, AF.Square).then_inc(s_nsq.h, 1))
        s_nsq.bump()
        plan.on("scalar", lambda e, src=k_sb[tb], dst=sqk:
                e.activation(dst, src, AF.Square).then_inc(s_nsq.h, 1))
        s_nsq.bump()
        plan.wait("vector", s_nsq, s_nsq.n)
        plan.on("vector", lambda e, o=qn2[:, t * c.hc:(t + 1) * c.hc]
                .rearrange("p (h o) -> p h o", o=1),
                i=sqn.rearrange("p (h d) -> p h d", h=c.hc):
                e.reduce_sum(o, i, axis=AX.X).then_inc(s_nrd.h, 1))
        s_nrd.bump()

        def knred(e, i=sqk.rearrange("p (h d) -> p h d", h=c.hc), t=t):
            e.reduce_sum(kn2t.rearrange("p (h o) -> p h o", o=1), i, axis=AX.X)
            e.drain()
            if t == 0:
                e.tensor_copy(kn2max, kn2t).then_inc(s_nrd.h, 1)
            else:
                e.tensor_max(kn2max, kn2max, kn2t).then_inc(s_nrd.h, 1)
        plan.on("vector", knred)
        s_nrd.bump()
        cosbc = cosx[:, t * 32:(t + 1) * 32].unsqueeze(1) \
            .broadcast_to([128, c.hc, 32])
        sinbc = sinx[:, t * 32:(t + 1) * 32].unsqueeze(1) \
            .broadcast_to([128, c.hc, 32])
        sin4 = sinbc.rearrange("p h (two x) -> p h two x", two=2)
        for buf in (q_sb[tb], k_sb[tb]):
            b3 = buf.rearrange("p (h d) -> p h d", h=c.hc)
            rot_v = b3[:, :, 0:32]
            shuf = b3[:, :, 0:32].rearrange(
                "p h (two x) -> p h two x", two=2)[:, :, ::-1, :]
            rA3 = rotA.rearrange("p (h d) -> p h d", h=c.hc)
            rB4 = rotB.rearrange("p (h two x) -> p h two x", h=c.hc, two=2)
            rB3 = rotB.rearrange("p (h d) -> p h d", h=c.hc)
            plan.wait("vector", s_nsq, s_nsq.n)

            def dorot(e, rv=rot_v, sh=shuf, rA=rA3, rB4=rB4, rB=rB3,
                      cb=cosbc, s4=sin4):
                e.tensor_tensor(rA, rv, cb, ALU.mult)
                e.tensor_tensor(rB4, sh, s4, ALU.mult)
                e.drain()
                e.tensor_add(rv, rA, rB).then_inc(s_rot.h, 1)
            plan.on("vector", dorot)
            s_rot.bump()
        def emit_kq_tail(tl):
            tbl = tl % 2
            for buf, stoff in ((k_sb[tbl], 0), (q_sb[tbl], c.qw)):
                plan.wait("tensor", s_rot, 2 * (tl + 1))
                gi = 2 * tl + (0 if stoff == 0 else 1)
                if gi >= 2:
                    plan.wait("tensor", s_trkqe, gi - 1)
                pb = bank(5 + gi % 2, BF16)[:, 0:c.qw]
                for hl in range(c.hc):
                    def trkq(e, src=buf[:, hl * 128:(hl + 1) * 128],
                             dst=pb[:, hl * 128:(hl + 1) * 128],
                             fin=(hl == c.hc - 1)):
                        ins = e.transpose(dst, src, identb)
                        if fin:
                            ins.then_inc(s_trkq.h, 1)
                    plan.on("tensor", trkq)
                s_trkq.bump()
                plan.wait("vector", s_trkq, s_trkq.n)
                if tl >= 2:
                    plan.wait("vector", s_stkq[tbl], 32 * (tl // 2))
                plan.on("vector", lambda e, o=kqstage[tbl][:, stoff:stoff + c.qw],
                        i=pb: e.tensor_copy(o, i).then_inc(s_trkqe.h, 1))
                s_trkqe.bump()
            plan.wait("gpsimd", s_trkqe, s_trkqe.n)

            def dkq(e, src=kqstage[tbl], t=tl):
                e.dma_start(out=kt_d[:, :, t * 128:(t + 1) * 128]
                            .rearrange("h p d -> p h d"),
                            in_=src[:, 0:c.qw].rearrange("p (h d) -> p h d",
                                                         h=c.hc)
                            ).then_inc(s_stkq[t % 2].h, 16)
                e.dma_start(out=qt_d[:, :, t * 128:(t + 1) * 128]
                            .rearrange("h p d -> p h d"),
                            in_=src[:, c.qw:2 * c.qw].rearrange(
                                "p (h d) -> p h d", h=c.hc)
                            ).then_inc(s_stkq[t % 2].h, 16)
            plan.on("gpsimd", dkq)
            s_stkq[tl % 2].bump(32)

        if t >= 1:
            emit_kq_tail(t - 1)
        if t == c.NT - 1:
            emit_kq_tail(t)
        plan.wait("gpsimd", s_qkeD, qkeD_tile)
        plan.on("gpsimd", lambda e, src=vstage[tb], t=t:
                e.dma_start(out=v_d[t], in_=src).then_inc(s_stv[t % 2].h, 16))
        s_stv[t % 2].bump(16)

    for sx in (s_stkq[0], s_stkq[1], s_stv[0], s_stv[1]):
        plan.wait("gpsimd", sx, sx.n)
    plan.wait("vector", s_nrd, s_nrd.n)

    pbF = bank(4, F32)
    plan.wait("tensor", s_nrd, s_nrd.n)
    plan.wait("tensor", s_qkeA, s_qkeA.n)
    plan.wait("tensor", s_qkeD, s_qkeD.n)
    plan.on("tensor", lambda e: e.transpose(pbF[0:c.hc, 0:128], kn2max, identf)
            .then_inc(s_fin.h, 1))
    s_fin.bump()
    plan.wait("vector", s_fin, s_fin.n)
    plan.on("vector", lambda e: e.tensor_copy(kn2row[0:c.hc, :],
                                              pbF[0:c.hc, 0:128])
            .then_inc(s_fin.h, 1))
    s_fin.bump()
    plan.wait("vector", s_fin, s_fin.n)
    plan.on("vector", lambda e: e.reduce_max(st_sb[0:c.hc, 0:1],
                                             kn2row[0:c.hc, :], axis=AX.X)
            .then_inc(s_fin.h, 1))
    s_fin.bump()
    plan.wait("tensor", s_fin, s_fin.n)
    plan.on("tensor", lambda e: e.transpose(pbF[0:1, 128:128 + c.hc],
                                            st_sb[0:c.hc, 0:1],
                                            identf[0:c.hc, 0:c.hc])
            .then_inc(s_fin.h, 1))
    s_fin.bump()
    plan.wait("vector", s_fin, s_fin.n)
    plan.on("vector", lambda e: e.tensor_copy(km2row_bf[0:1, :],
                                              pbF[0:1, 128:128 + c.hc])
            .then_inc(s_fin.h, 1))
    s_fin.bump()
    plan.wait("tensor", s_fin, s_fin.n)
    plan.on("tensor", lambda e: e.matmul(pbF[:, 256:256 + c.hc],
                                         onesb[0:1, :], km2row_bf[0:1, :],
                                         start=True, stop=True)
            .then_inc(s_fin.h, 1))
    s_fin.bump()
    plan.wait("vector", s_fin, s_fin.n)
    plan.on("vector", lambda e: e.tensor_copy(km2rep, pbF[:, 256:256 + c.hc])
            .then_inc(s_fin.h, 1))
    s_fin.bump()
    plan.wait("vector", s_fin, s_fin.n)
    km2bc = km2rep.unsqueeze(1).broadcast_to([128, c.NT, c.hc])
    def _negbmul(e):
        e.drain()
        e.tensor_tensor(
            negb.rearrange("p (t h) -> p t h", h=c.hc),
            qn2.rearrange("p (t h) -> p t h", h=c.hc), km2bc, ALU.mult
        ).then_inc(s_fin.h, 1)
    plan.on("vector", _negbmul)
    s_fin.bump()
    plan.wait("scalar", s_fin, s_fin.n)
    plan.on("scalar", lambda e: e.activation(negb, negb, AF.Sqrt)
            .then_inc(s_fin.h, 1))
    s_fin.bump()
    plan.wait("vector", s_fin, s_fin.n)
    plan.on("vector", lambda e: e.tensor_scalar_mul(negb, negb, -1.0)
            .then_inc(s_fin.h, 1))
    s_fin.bump()
    NEGB = s_fin.n

    s_bar1 = S("s_bar1")
    barrier(plan, s_bar1)

    if debug_phase == 1:
        s_dbg = S("s_dbg")

        def dump1(e):
            e.dma_start(out=dbg["kt"], in_=kt_d).then_inc(s_dbg.h, 16)
            e.dma_start(out=dbg["qt"], in_=qt_d).then_inc(s_dbg.h, 16)
            e.dma_start(out=dbg["v"], in_=v_d).then_inc(s_dbg.h, 16)
            e.dma_start(out=dbg["nb"], in_=negb).then_inc(s_dbg.h, 16)
            e.wait_ge(s_dbg.h, 64)
        plan.on("sync", dump1)
        return finish()

    A.pop()
    A.push()
    wout_sb = A.take(c.NFC * c.D, BF16, "wout_sb")
    A.push()
    ktb = [A.take(c.n, BF16, f"ktb{i}") for i in range(2)]
    qtb = [A.take(c.n, BF16, f"qtb{i}") for i in range(2)]
    vb = [A.take(c.NT * 128, BF16, f"vb{i}") for i in range(2)]
    P_sb = [A.take(c.n, BF16, f"P{i}") for i in range(2)]
    PT_sb = [A.take(c.n, BF16, f"PT{i}") for i in range(2)]
    ot_sb = [A.take(c.n, BF16, f"ot{i}") for i in range(2)]

    s_kqp = [S("s_kq0"), S("s_kq1")]
    s_vp = [S("s_vp0"), S("s_vp1")]
    s_simm = S("s_simm")
    s_exp = S("s_exp")
    s_pn = S("s_pn")
    s_ptr = S("s_ptr")
    s_pte = S("s_pte")
    s_av = S("s_av")
    s_ote = S("s_ote")
    s_otdp = [S("s_otd0"), S("s_otd1")]
    s_wo = S("s_wo")
    s_cc = S("s_cc")

    NV = 4 if c.NT % 4 == 0 else 1
    PERHEAD_KQ = 16 * (2 + NV)

    def load_head(e, hl):
        e.dma_start(out=ktb[hl % 2], in_=kt_d[hl]).then_inc(s_kqp[hl % 2].h, 16)
        e.dma_start(out=qtb[hl % 2], in_=qt_d[hl]).then_inc(s_kqp[hl % 2].h, 16)
        vv = v_d[:, :, hl * 128:(hl + 1) * 128]
        step = c.NT // NV
        for part in range(0, c.NT, step):
            sl = vv[part:part + step].rearrange("t p d -> p t d")
            dst = vb[hl % 2][:, part * 128:(part + step) * 128].rearrange(
                "p (t d) -> p t d", t=step)
            e.dma_start(out=dst, in_=sl).then_inc(s_vp[hl % 2].h, 16)

    plan.on("sync", lambda e: load_head(e, 0))
    s_kqp[0].bump(32)
    s_vp[0].bump(16 * NV)

    def pwout(e):
        for fc in range(c.NFC):
            e.dma_start(out=wout_sb[:, fc * c.D:(fc + 1) * c.D],
                        in_=woutT_d[fc * 128:(fc + 1) * 128, :]
                        ).then_inc(s_wo.h, 16)
    plan.on("gpsimd", pwout)
    s_wo.bump(16 * c.NFC)
    WO_READY = s_wo.n

    sim_cnt = 0
    tr_cnt = 0
    chunk_exp = {}
    nexp_step = {}
    steps = c.hc * c.NT
    exp_done_step = {}
    tr_last = {}
    pn_done = {}
    pte_done = {}
    av_done = {}
    ote_done = {}

    for s in range(steps + 2):
        if s < steps:
            hl, j = s // c.NT, s % c.NT
            nc_j = c.nkc[j]
            plen = 128 * (j + 1)
            if j == 0:
                plan.wait("tensor", s_kqp[hl % 2], 32 * (hl // 2 + 1))
                if hl == 0:
                    plan.wait("tensor", s_fin, NEGB)
                    plan.wait("scalar", s_fin, NEGB)
            if j == 2 and hl + 1 < c.hc:
                if hl >= 1:
                    plan.wait("sync", s_ote, ote_done[hl * c.NT - 1])
                plan.on("sync", lambda e, h=hl + 1: load_head(e, h))
                s_kqp[(hl + 1) % 2].bump(32)
                s_vp[(hl + 1) % 2].bump(16 * NV)
            if s >= 2:
                plan.wait("scalar", s_ptr, tr_last[s - 2])
            pend = None
            n_expi = 0
            for kc in range(nc_j):
                slot = sim_cnt % 4
                wchunk = min(512, plen - kc * 512)
                if sim_cnt >= 4:
                    plan.wait("tensor", s_exp, chunk_exp[sim_cnt - 4])
                diag = (kc == nc_j - 1)
                pq = bank(slot)[:, 0:wchunk]

                def smm(e, o=pq, l=qtb[hl % 2][:, j * 128:(j + 1) * 128],
                        r=ktb[hl % 2][:, kc * 512:kc * 512 + wchunk], diag=diag):
                    ins = e.matmul(o, l, r, start=True, stop=not diag)
                    if not diag:
                        ins.then_inc(s_simm.h, 1)
                plan.on("tensor", smm)
                if diag:
                    r0 = 128 * (j % 4)

                    def mmm(e, o=pq, msl=maskx[:, 384 - r0:384 - r0 + wchunk]):
                        e.matmul(o, identb, msl, start=False, stop=True
                                 ).then_inc(s_simm.h, 1)
                    plan.on("tensor", mmm)
                sim_cnt += 1
                s_simm.bump()
                if slot % 2 == 0 and not diag:
                    pend = (kc, slot)
                    continue
                plan.wait("scalar", s_simm, s_simm.n)
                bia = negb[:, j * c.hc + hl:j * c.hc + hl + 1]
                acol = lparts[:, (s % 2) * 8 + n_expi:(s % 2) * 8 + n_expi + 1]
                if pend is not None and pend[1] == slot - 1:
                    kc0 = pend[0]
                    pw = 512 + wchunk

                    def dexp2(e, o=P_sb[s % 2][:, kc0 * 512:kc0 * 512 + pw],
                              i=pool[:, (slot - 1) * 512:
                                     (slot - 1) * 512 + pw],
                              bia=bia, a=acol):
                        e.activation(o, i, AF.Exp, bias=bia, accum_out=a
                                     ).then_inc(s_exp.h, 1)
                    plan.on("scalar", dexp2)
                    s_exp.bump()
                    chunk_exp[sim_cnt - 2] = s_exp.n
                    chunk_exp[sim_cnt - 1] = s_exp.n
                else:
                    def dexp(e, o=P_sb[s % 2][:, kc * 512:kc * 512 + wchunk],
                             i=pq, bia=bia, a=acol):
                        e.activation(o, i, AF.Exp, bias=bia, accum_out=a
                                     ).then_inc(s_exp.h, 1)
                    plan.on("scalar", dexp)
                    s_exp.bump()
                    chunk_exp[sim_cnt - 1] = s_exp.n
                n_expi += 1
                pend = None
            exp_done_step[s] = s_exp.n
            nexp_step[s] = n_expi
        sb = s - 1
        if 0 <= sb < steps:
            hl, j = sb // c.NT, sb % c.NT
            plen = 128 * (j + 1)
            plan.wait("vector", s_exp, exp_done_step[sb])

            def dnorm_stats(e,
                            lp=lparts[:, (sb % 2) * 8:
                                      (sb % 2) * 8 + nexp_step[sb]]):
                e.reduce_sum(lsum, lp, axis=AX.X)
                e.drain()
                e.reciprocal(linv, lsum)
                e.drain()
            plan.on("vector", dnorm_stats)
            pn_base = s_pn.n
            for g0 in range(0, plen, 512):
                gw = min(512, plen - g0)
                plan.on("vector", lambda e, pp=P_sb[sb % 2], g0=g0, gw=gw:
                        e.tensor_scalar_mul(pp[:, g0:g0 + gw],
                                            pp[:, g0:g0 + gw], linv
                                            ).then_inc(s_pn.h, 1))
                s_pn.bump()
            pn_done[sb] = s_pn.n
        if 0 <= sb < steps:
            hl, j = sb // c.NT, sb % c.NT
            pn_base_c = pn_done[sb] - (plen + 511) // 512
            ngrp = (j + 4) // 4
            for g in range(ngrp):
                plan.wait("tensor", s_pn, min(pn_done[sb], pn_base_c + g + 1))
                k0 = g * 4
                kcs = list(range(k0, min(k0 + 4, j + 1)))
                if tr_cnt >= 2:
                    plan.wait("tensor", s_pte, tr_cnt - 1)
                pb = bank(4 + tr_cnt % 2, BF16)
                for idx, kc2 in enumerate(kcs):
                    def ptr(e, src=P_sb[sb % 2][:, kc2 * 128:(kc2 + 1) * 128],
                            dst=pb[:, idx * 128:(idx + 1) * 128],
                            fin=(idx == len(kcs) - 1)):
                        ins = e.transpose(dst, src, identb)
                        if fin:
                            ins.then_inc(s_ptr.h, 1)
                    plan.on("tensor", ptr)
                tr_cnt += 1
                s_ptr.bump()
                plan.wait("vector", s_ptr, s_ptr.n)
                plan.on("vector", lambda e,
                        o=PT_sb[sb % 2][:, k0 * 128:k0 * 128 + len(kcs) * 128],
                        i=pb[:, 0:len(kcs) * 128]:
                        e.tensor_copy(o, i).then_inc(s_pte.h, 1))
                s_pte.bump()
                pte_done[(sb, g)] = s_pte.n
            tr_last[sb] = s_ptr.n
            if sb >= 2:
                plan.wait("tensor", s_ote, ote_done[sb - 2])
            if j == 0:
                plan.wait("tensor", s_vp[hl % 2], 16 * NV * (hl // 2 + 1))
            for kc2 in range(j + 1):
                if kc2 % 4 == 0:
                    plan.wait("tensor", s_pte, pte_done[(sb, kc2 // 4)])

                def avm(e, o=bank(6 + sb % 2)[:, 0:128],
                        l=vb[hl % 2][:, kc2 * 128:(kc2 + 1) * 128],
                        r=PT_sb[sb % 2][:, kc2 * 128:(kc2 + 1) * 128],
                        st=(kc2 == 0), sp=(kc2 == j)):
                    ins = e.matmul(o, l, r, start=st, stop=sp)
                    if sp:
                        ins.then_inc(s_av.h, 1)
                plan.on("tensor", avm)
            s_av.bump()
            av_done[sb] = s_av.n
        sd_ = s - 2
        if 0 <= sd_ < steps:
            hl, j = sd_ // c.NT, sd_ % c.NT
            plan.wait("vector", s_av, av_done[sd_])
            if hl >= 2 and j == 0:
                plan.wait("vector", s_otdp[hl % 2], 16 * (hl // 2))
            plan.on("vector", lambda e,
                    o=ot_sb[hl % 2][:, j * 128:(j + 1) * 128],
                    i=bank(6 + sd_ % 2)[:, 0:128]:
                    e.tensor_copy(o, i).then_inc(s_ote.h, 1))
            s_ote.bump()
            ote_done[sd_] = s_ote.n
            if j == c.NT - 1:
                plan.wait("sync", s_ote, s_ote.n)
                plan.on("sync", lambda e, h=hl, grp=hl // hh, loc=hl % hh:
                        e.dma_start(
                    out=ccsrc[grp][loc * 128:(loc + 1) * 128, :],
                    in_=ot_sb[h % 2]).then_inc(s_otdp[h % 2].h, 16))
                s_otdp[hl % 2].bump(16)
                if hl % hh == hh - 1 or hl == c.hc - 1:
                    grp = hl // hh
                    plan.wait("gpsimd", s_otdp[0], s_otdp[0].n)
                    plan.wait("gpsimd", s_otdp[1], s_otdp[1].n)
                    plan.on("gpsimd", lambda e, g=grp: e.collective_compute(
                        "AllGather", ALU.bypass, replica_groups=PAIRS,
                        ins=[ccsrc_t[g].ap().opt()],
                        outs=[ccdst_t[g].ap().opt()]).then_inc(s_cc.h, 1))
                    s_cc.bump()

    for eng in ENGS:
        plan.wait(eng, s_cc, s_cc.n)

    s_bar2 = S("s_bar2")
    barrier(plan, s_bar2)

    if debug_phase == 2:
        s_dbg = S("s_dbg")

        def dump2(e):
            d4 = dbg["ot"].rearrange("(r h p) n -> r h p n", r=2, h=c.hc, p=128)
            for i in range(2):
                s4 = ccdst[i].rearrange("(r h p) n -> r h p n", r=2, h=hh, p=128)
                for r in range(2):
                    e.dma_start(out=d4[r, i * hh:(i + 1) * hh], in_=s4[r]
                                ).then_inc(s_dbg.h, 16)
            e.wait_ge(s_dbg.h, 64)
        plan.on("sync", dump2)
        return finish()

    A.pop()
    A.push()
    otA = [A.take(c.F, BF16, f"otA{i}") for i in range(2)]
    otB = [A.take(c.F, BF16, f"otB{i}") for i in range(2)]
    otb = [A.take(c.F, BF16, f"otb{i}") for i in range(2)]
    xo = [A.take(c.D, F32, f"xo{i}") for i in range(2)]
    y_sb = [A.take(c.D, F32, f"y_sb{i}") for i in range(2)]
    yn_sb = A.take(c.D, F32, "yn_sb")
    xatt_sb = A.take(c.D, F32, "xatt_sb")
    x2_sb = [A.take(c.D, BF16, f"x2_sb{i}") for i in range(2)]
    x2st = A.take(c.D, BF16, "x2st")

    s_otl = S("s_otl")
    s_bl = S("s_bl")
    s_ymm = S("s_ymm")
    s_ye = S("s_ye")
    s_xolp = [S("s_xol0"), S("s_xol1")]
    s_ln2A = S("s_ln2A")
    s_ln2D = S("s_ln2D")
    s_tr25 = S("s_tr25")
    s_tr25e = S("s_tr25e")
    s_x2d = S("s_x2d")
    s_xad = S("s_xad")

    cc3 = [ccdst[i].rearrange("(r h p) n -> r h p n", r=2, h=hh, p=128)
           for i in range(ngrp_cc)]
    NOTL = 16 * 2 * ngrp_cc * 2
    ncb = (c.D + 511) // 512
    nb2 = 2 if c.NCC > 8 else 1
    per2 = c.NCC // nb2

    plan.on("sync", lambda e: e.dma_start(out=xo[0], in_=xown_d[0:128, :])
            .then_inc(s_xolp[0].h, 16))
    s_xolp[0].bump(16)
    prev_aval = 0

    def lot(e, tt):
        for r in range(2):
            for g in range(ngrp_cc):
                for hs in range(2):
                    col = hs * c.half + tt * 128
                    src = cc3[g][r, :, :, col:col + 128].rearrange(
                        "h p d -> p h d")
                    base = r * c.hc * 128 + g * hh * 128
                    dst = (otA if hs == 0 else otB)[tt % 2][
                        :, base:base + hh * 128].rearrange(
                        "p (h d) -> p h d", h=hh)
                    e.dma_start(out=dst, in_=src).then_inc(s_otl.h, 16)

    plan.on("sync", lambda e: lot(e, 0))
    s_otl.bump(NOTL)
    otl_done = {0: s_otl.n}
    lny_done = {}
    aval_of = {}
    x2_done = {}

    for tt in range(c.HT):
        tb = tt % 2
        if tt + 1 < c.HT:
            if tt >= 1:
                plan.wait("sync", s_bl, tt)
            plan.on("sync", lambda e, t2=tt + 1: lot(e, t2))
            s_otl.bump(NOTL)
            otl_done[tt + 1] = s_otl.n
        def emit_blend(tl):
            plan.wait("vector", s_otl, otl_done[tl])
            if tl >= 2:
                plan.wait("vector", s_ymm, tl - 1)

            def blend(e, tbl=tl % 2):
                e.tensor_scalar_mul(otA[tbl], otA[tbl], selA)
                e.drain()
                e.scalar_tensor_tensor(otb[tbl], otB[tbl], selB, otA[tbl],
                                       ALU.mult, ALU.add).then_inc(s_bl.h, 1)
            plan.on("vector", blend)
            s_bl.bump()

        if tt == 0:
            emit_blend(0)
        plan.wait("tensor", s_bl, tt + 1)
        if tt == 0:
            plan.wait("tensor", s_wo, WO_READY)
        else:
            plan.wait("tensor", s_ye, tt)
        for fc in range(c.NFC):
            for cb in range(ncb):
                cw = min(512, c.D - cb * 512)

                def ymm(e, o=bank(cb)[:, 0:cw],
                        l=otb[tb][:, fc * 128:(fc + 1) * 128],
                        r=wout_sb[:, fc * c.D + cb * 512:fc * c.D + cb * 512 + cw],
                        st=(fc == 0), sp=(fc == c.NFC - 1),
                        fin=(fc == c.NFC - 1 and cb == ncb - 1)):
                    ins = e.matmul(o, l, r, start=st, stop=sp)
                    if fin:
                        ins.then_inc(s_ymm.h, 1)
                plan.on("tensor", ymm)
        s_ymm.bump()
        plan.wait("scalar", s_ymm, s_ymm.n)
        if tt >= 2:
            plan.wait("scalar", s_ln2A, lny_done[tt - 2])
        plan.on("scalar", lambda e, o=y_sb[tb]:
                e.activation(o, pool[:, 0:c.D], AF.Identity).then_inc(s_ye.h, 1))
        s_ye.bump()
        if tt + 1 < c.HT:
            emit_blend(tt + 1)

        def chains_ln(tl):
            tbl = tl % 2
            sem, val = emit_ln(plan, c, s_ln2A, s_ln2D, lnbufs, y_sb[tbl],
                               yn_sb, s_ye, tl + 1,
                               dst_free=(s_ln2D, aval_of[tl - 1])
                               if tl >= 1 else None)
            lny_done[tl] = val
            plan.wait("vector", s_ln2A, val)
            plan.wait("vector", s_xolp[tbl], 16 * (tl // 2 + 1))
            if tl >= 1:
                plan.wait("vector", s_xad, 16 * tl)
            plan.on("vector", lambda e, tbl=tbl:
                    e.tensor_add(xatt_sb, yn_sb, xo[tbl]).then_inc(s_ln2D.h, 1))
            aval_of[tl] = s_ln2D.bump()
            sem2, val2 = emit_ln(plan, c, s_ln2A, s_ln2D, lnbufs, xatt_sb,
                                 x2_sb[tbl], s_ln2D, aval_of[tl],
                                 dst_free=(s_tr25, max(0, (tl - 1) * nb2)))
            x2_done[tl] = val2
            plan.wait("gpsimd", sem2, val2)
            plan.on("gpsimd", lambda e, t2=tl: e.dma_start(
                out=xatt_d[t2 * 128:(t2 + 1) * 128, :], in_=xatt_sb)
                .then_inc(s_xad.h, 16))
            s_xad.bump(16)

        def chains_tr(tl):
            tbl = tl % 2
            plan.wait("tensor", s_ln2A, x2_done[tl])
            for bti in range(nb2):
                gi = tl * nb2 + bti
                if gi >= 2:
                    plan.wait("tensor", s_tr25e, gi - 1)
                pb = bank(4 + gi % 2, BF16)
                for u in range(per2):
                    ccg = bti * per2 + u

                    def tr25(e, src=x2_sb[tbl][:, ccg * 128:(ccg + 1) * 128],
                             dst=pb[:, u * 128:(u + 1) * 128],
                             fin=(u == per2 - 1)):
                        ins = e.transpose(dst, src, identb)
                        if fin:
                            ins.then_inc(s_tr25.h, 1)
                    plan.on("tensor", tr25)
                s_tr25.bump()
                plan.wait("vector", s_tr25, s_tr25.n)
                if tl >= 1:
                    plan.wait("vector", s_x2d, 16 * tl)
                plan.on("vector", lambda e,
                        o=x2st[:, bti * per2 * 128:(bti + 1) * per2 * 128],
                        i=pb[:, 0:per2 * 128]:
                        e.tensor_copy(o, i).then_inc(s_tr25e.h, 1))
                s_tr25e.bump()
            plan.wait("gpsimd", s_tr25e, s_tr25e.n)
            plan.on("gpsimd", lambda e, t2=tl: e.dma_start(
                out=x2t_d[:, :, t2 * 128:(t2 + 1) * 128]
                .rearrange("cc p d -> p cc d"),
                in_=x2st.rearrange("p (cc d) -> p cc d", cc=c.NCC))
                .then_inc(s_x2d.h, 16))
            s_x2d.bump(16)

        if tt >= 2:
            chains_tr(tt - 2)
        if tt >= 1:
            chains_ln(tt - 1)
        if tt + 1 < c.HT:
            if tt >= 1:
                plan.wait("sync", s_ln2D, aval_of[tt - 1])
            plan.on("sync", lambda e, i=(tt + 1) % 2, t2=tt + 1: e.dma_start(
                out=xo[i], in_=xown_d[t2 * 128:(t2 + 1) * 128, :])
                .then_inc(s_xolp[i].h, 16))
            s_xolp[(tt + 1) % 2].bump(16)
        if tt == c.HT - 1:
            chains_ln(tt)
            chains_tr(tt - 1)
            chains_tr(tt)

    plan.wait("gpsimd", s_x2d, s_x2d.n)
    plan.wait("gpsimd", s_xad, s_xad.n)
    s_bar3 = S("s_bar3")
    barrier(plan, s_bar3)

    if debug_phase == 3:
        s_dbg = S("s_dbg")

        def dump3(e):
            e.dma_start(out=dbg["xatt"], in_=xatt_d).then_inc(s_dbg.h, 16)
            e.dma_start(out=dbg["x2t"], in_=x2t_d).then_inc(s_dbg.h, 16)
            e.wait_ge(s_dbg.h, 32)
        plan.on("sync", dump3)
        return finish()

    A.pop()
    A.pop()
    A.push()
    x2tb_sb = A.take(c.NCC * c.TB, BF16, "x2tb")
    fuse_sb = A.take(c.NOC * c.TB, BF16, "fuseT")
    wdslot = [A.take(2 * c.D, BF16, f"wd{i}") for i in range(2)]
    xa3 = A.take(c.D, F32, "xa3")
    wstar = A.take(c.NCC * 512, BF16, "wstar")
    A.push()
    wslot = [A.take(c.NCC * 512, BF16, f"wslot{i}") for i in range(3)]
    gsilu_sb = A.take(4 * c.TB, BF16, "gsilu")
    A.pop()
    gu_mark = A.off
    A.push()
    zT_sb = A.take(8 * c.TB, F32, "zT")
    z_sb = A.take(c.NTS * c.D, F32, "z_sb")
    _pad = A.take((4 * c.TB * 2 + 31) // 32 * 32 // 4, F32, "pad_gsilu")
    zn_sb = A.take(c.D, F32, "zn")
    A.pop()
    xout_sb = xa3

    s_x2b = S("s_x2b")
    s_wgs = [S("s_wg0"), S("s_wg1"), S("s_wg2"), S("s_wg3")]
    s_gmm = S("s_gmm")
    s_umm = S("s_umm")
    s_sil = S("s_sil")
    s_fu = S("s_fu")
    s_wd3p = [S("s_wd30"), S("s_wd31")]
    s_zmm = S("s_zmm")
    s_ze = S("s_ze")
    s_ztr = S("s_ztr")
    s_ztre = S("s_ztre")
    s_ln3A = S("s_ln3A")
    s_ln3D = S("s_ln3D")
    s_xa3 = S("s_xa3")
    s_od = S("s_od")
    s_dbg3 = S("s_dbg3")
    s_blk = S("s_blk")
    prev_ov = 0
    aval3 = {}

    wload_val = {}

    hcc2 = max(1, c.NCC // 2)

    def emit_x2bl(b):
        if b >= 1:
            plan.wait("sync", s_umm, 4 * c.NOB * b)

        def x2bl(e, b=b, hcc=hcc2):
            for part in range(0, c.NCC, hcc):
                e.dma_start(
                    out=x2tb_sb[:, part * c.TB:(part + hcc) * c.TB]
                    .rearrange("p (cc t) -> p cc t", cc=hcc),
                    in_=x2t_d[part:part + hcc, :, b * c.TB:(b + 1) * c.TB]
                    .rearrange("cc p t -> p cc t")).then_inc(s_x2b.h, 16)
        plan.on("sync", x2bl)
        s_x2b.bump(16 * (c.NCC // hcc2))

    emit_x2bl(0)
    for bb in range(c.NB):
        X2B = 16 * (c.NCC // hcc2) * (bb + 1)
        base_fu = s_fu.n

        def emit_wload2(bb2, idx):
            ob, which = idx // 2, ("g" if idx % 2 == 0 else "u")
            bg = 4 * c.NOB * bb2
            if idx == 0:
                slot, buf = 3, wstar
                if bb2 >= 1:
                    plan.wait("gpsimd", s_gmm, bg - 4 * (c.NOB - 1))
            else:
                slot = (idx - 1) % 3
                buf = wslot[slot]
                if bb2 >= 1 and idx <= 3:
                    if idx == 1:
                        ncg = (c.NCC + min(8, c.NCC) - 1) // min(8, c.NCC)
                        plan.wait("gpsimd", s_ztre, bb2 * ncg * c.NTS)
                    elif idx == 2:
                        plan.wait("gpsimd", s_ln3D, aval3[(bb2 - 1, 1)])
                    else:
                        plan.wait("gpsimd", s_ln3D,
                                  aval3[(bb2 - 1, c.NTS - 1)])
                if idx >= 4:
                    pob, pw = (idx - 3) // 2, (idx - 3) % 2
                    semref = s_gmm if pw == 0 else s_umm
                    plan.wait("gpsimd", semref, bg + 4 * (pob + 1))
            src = (wg_d if which == "g" else wu_d)[ob].rearrange(
                "(cc p) o -> p cc o", p=128)
            hcc = max(1, c.NCC // 2)

            def wl(e, src=src, slot=slot, buf=buf, hcc=hcc):
                for part in range(0, c.NCC, hcc):
                    e.dma_start(
                        out=buf[:, part * 512:(part + hcc) * 512]
                        .rearrange("p (cc o) -> p cc o", cc=hcc),
                        in_=src[:, part:part + hcc, :]
                    ).then_inc(s_wgs[slot].h, 16)
            plan.on("gpsimd", wl)
            s_wgs[slot].bump(16 * (c.NCC // hcc))
            wload_val[(bb2, idx)] = (slot, s_wgs[slot].n, buf)

        if bb == 0:
            for _i in range(min(4, 2 * c.NOB)):
                emit_wload2(0, _i)
        nextload = 4
        for ob in range(c.NOB):
            gsl, gv, gbuf = wload_val[(bb, 2 * ob)]
            usl, uv, ubuf = wload_val[(bb, 2 * ob + 1)]
            plan.wait("tensor", s_wgs[gsl], gv)
            plan.wait("tensor", s_x2b, X2B)
            if bb >= 1 and ob == 0:
                plan.wait("tensor", s_ztre, s_ztre.n)
            for oc in range(4):
                if s_gmm.n - 4 * c.NOB * bb >= 3:
                    plan.wait("tensor", s_sil, s_gmm.n - 2)
                pg = bank((0, 1, 4)[(ob * 4 + oc) % 3])[:, 0:c.TB]
                for ci in range(c.NCC):
                    def gmm(e, o=pg,
                            l=gbuf[:, ci * 512 + oc * 128:
                                   ci * 512 + (oc + 1) * 128],
                            r=x2tb_sb[:, ci * c.TB:(ci + 1) * c.TB],
                            st=(ci == 0), sp=(ci == c.NCC - 1)):
                        ins = e.matmul(o, l, r, start=st, stop=sp)
                        if sp:
                            ins.then_inc(s_gmm.h, 1)
                    plan.on("tensor", gmm)
                s_gmm.bump()
                plan.wait("scalar", s_gmm, s_gmm.n)
                ocg = ob * 4 + oc
                if ob >= 1:
                    plan.wait("scalar", s_fu, base_fu + (ob - 1) * 4 + oc + 1)

                def sil(e, o=gsilu_sb[:, oc * c.TB:(oc + 1) * c.TB], i=pg,
                        sc=sg_sb[:, ocg:ocg + 1]):
                    e.activation(o, i, AF.Silu, scale=sc).then_inc(s_sil.h, 1)
                plan.on("scalar", sil)
                s_sil.bump()
            if nextload < 2 * c.NOB:
                emit_wload2(bb, nextload)
                nextload += 1
            elif bb + 1 < c.NB and nextload < 2 * c.NOB + 1:
                emit_wload2(bb + 1, nextload - 2 * c.NOB)
                nextload += 1
            plan.wait("tensor", s_wgs[usl], uv)
            for oc in range(4):
                if s_umm.n - 4 * c.NOB * bb >= 3:
                    plan.wait("tensor", s_fu, s_umm.n - 2)
                pu = bank((2, 3, 5)[(ob * 4 + oc) % 3])[:, 0:c.TB]
                for ci in range(c.NCC):
                    def umm(e, o=pu,
                            l=ubuf[:, ci * 512 + oc * 128:
                                   ci * 512 + (oc + 1) * 128],
                            r=x2tb_sb[:, ci * c.TB:(ci + 1) * c.TB],
                            st=(ci == 0), sp=(ci == c.NCC - 1)):
                        ins = e.matmul(o, l, r, start=st, stop=sp)
                        if sp:
                            ins.then_inc(s_umm.h, 1)
                    plan.on("tensor", umm)
                s_umm.bump()
                ocg = ob * 4 + oc
                plan.wait("vector", s_umm, s_umm.n)
                plan.wait("vector", s_sil, s_sil.n)

                def fu(e, o=fuse_sb[:, ocg * c.TB:(ocg + 1) * c.TB], i=pu,
                       sc=su_sb[:, ocg:ocg + 1],
                       g=gsilu_sb[:, oc * c.TB:(oc + 1) * c.TB]):
                    e.scalar_tensor_tensor(o, i, sc, g, ALU.mult, ALU.mult
                                           ).then_inc(s_fu.h, 1)
                plan.on("vector", fu)
                s_fu.bump()
            if nextload < 2 * c.NOB:
                emit_wload2(bb, nextload)
                nextload += 1
            elif bb + 1 < c.NB and nextload < 2 * c.NOB + 1:
                emit_wload2(bb + 1, nextload - 2 * c.NOB)
                nextload += 1
        while bb + 1 < c.NB and nextload < 2 * c.NOB + 1:
            emit_wload2(bb + 1, nextload - 2 * c.NOB)
            nextload += 1
        FUSE_ALL = s_fu.n
        if debug_phase == 6:
            barrier(plan, s_blk)
        if debug_phase == 4:
            plan.wait("sync", s_x2b, X2B)
            plan.on("sync", lambda e, b=bb: e.dma_start(
                out=dbg["x2b"][b], in_=x2tb_sb).then_inc(s_dbg3.h, 16))
            s_dbg3.bump(16)
            plan.wait("sync", s_fu, FUSE_ALL)
            plan.on("sync", lambda e, b=bb: e.dma_start(
                out=dbg["fuse"][b], in_=fuse_sb).then_inc(s_dbg3.h, 16))
            s_dbg3.bump(16)
            dbg_gate = s_dbg3.n
        gsz = min(8, c.NCC)
        n_ccg = (c.NCC + gsz - 1) // gsz
        for ccg in range(n_ccg):
            plan.wait("tensor", s_fu, FUSE_ALL)
            if bb > 0 or ccg > 0:
                plan.wait("tensor", s_ze, s_ze.n)
                plan.wait("tensor", s_ztre, s_ztre.n)
            npair = (c.NOC + 1) // 2
            for pk in range(npair):
                og0 = 2 * pk
                ng = min(2, c.NOC - og0)
                wq = "gpsimd"
                if s_zmm.n >= 2 * 2:
                    plan.wait(wq, s_zmm, s_zmm.n - 2)
                plan.on(wq, lambda e, o=wdslot[pk % 2], og0=og0, ng=ng,
                        pkb=pk % 2:
                        e.dma_start(
                            out=o[:, 0:ng * c.D].rearrange(
                                "p (g cc) -> p g cc", g=ng),
                            in_=wd_d[og0 * 128:(og0 + ng) * 128, :].rearrange(
                                "(g p) cc -> p g cc", p=128)
                        ).then_inc(s_wd3p[pkb].h, 16))
                s_wd3p[pk % 2].bump(16)
                plan.wait("tensor", s_wd3p[pk % 2], s_wd3p[pk % 2].n)
                for g in range(ng):
                    ocg = og0 + g
                    for ci in range(gsz):
                        cc = ccg * gsz + ci

                        def zmm(e, o=bank(ci)[:, 0:c.TB],
                                l=wdslot[pk % 2][:, g * c.D + cc * 128:
                                                 g * c.D + (cc + 1) * 128],
                                r=fuse_sb[:, ocg * c.TB:(ocg + 1) * c.TB],
                                st=(ocg == 0), sp=(ocg == c.NOC - 1),
                                fin=(ci == gsz - 1)):
                            ins = e.matmul(o, l, r, start=st, stop=sp)
                            if fin:
                                ins.then_inc(s_zmm.h, 1)
                        plan.on("tensor", zmm)
                    s_zmm.bump()
            plan.wait("scalar", s_zmm, s_zmm.n)
            for ci in range(gsz):
                cc = ccg * gsz + ci

                def zev(e, o=zT_sb[:, ci * c.TB:(ci + 1) * c.TB],
                        i=bank(ci)[:, 0:c.TB], sc=sd_sb[:, cc:cc + 1]):
                    e.activation(o, i, AF.Identity, scale=sc).then_inc(s_ze.h, 1)
                plan.on("scalar", zev)
                s_ze.bump()
            ZE = s_ze.n
            plan.wait("tensor", s_ze, ZE)
            for ts in range(c.NTS):
                gi = (bb * n_ccg + ccg) * c.NTS + ts
                if debug_phase == 8:
                    barrier(plan, s_blk)
                if gi >= 2:
                    plan.wait("tensor", s_ztre, gi - 1)
                for ci in range(gsz):
                    def ztr(e, src=zT_sb[:, ci * c.TB + ts * 128:
                                         ci * c.TB + (ts + 1) * 128],
                            ci=ci, gi=gi, fin=(ci == gsz - 1)):
                        b = bank((gi % 2) * 2 + ci // 4)
                        dd = b[:, (ci % 4) * 128:(ci % 4 + 1) * 128]
                        ins = e.transpose(dd, src, identf)
                        if fin:
                            ins.then_inc(s_ztr.h, 1)
                    plan.on("tensor", ztr)
                s_ztr.bump()
                plan.wait("vector", s_ztr, s_ztr.n)

                def zevac(e, ts=ts, ccg=ccg, gi=gi, gsz=gsz):
                    nleft = gsz * 128
                    co = ts * c.D + ccg * gsz * 128
                    w0 = min(512, nleft)
                    b0 = bank((gi % 2) * 2)
                    ins = e.tensor_copy(z_sb[:, co:co + w0], b0[:, 0:w0])
                    if nleft > 512:
                        b1 = bank((gi % 2) * 2 + 1)
                        ins = e.tensor_copy(z_sb[:, co + 512:co + nleft],
                                            b1[:, 0:nleft - 512])
                    ins.then_inc(s_ztre.h, 1)
                plan.on("vector", zevac)
                s_ztre.bump()
        ZTRE_ALL = s_ztre.n
        if bb + 1 < c.NB:
            emit_x2bl(bb + 1)
            emit_wload2(bb + 1, 1)
        if debug_phase == 4:
            plan.wait("sync", s_ztre, ZTRE_ALL)
            plan.on("sync", lambda e, b=bb: e.dma_start(
                out=dbg["z"][b], in_=z_sb).then_inc(s_dbg3.h, 16))
            s_dbg3.bump(16)
            plan.wait("gpsimd", s_dbg3, s_dbg3.n)
            plan.wait("vector", s_dbg3, s_dbg3.n)
        if debug_phase == 5:
            barrier(plan, s_blk)
        if debug_phase == 7:
            barrier(plan, s_blk)
        if debug_phase == 10:
            barrier(plan, s_blk)
            plan.on("sync", lambda e, b=bb: e.dma_start(
                out=dbg["z"][b], in_=z_sb).then_inc(s_dbg3.h, 16))
            s_dbg3.bump(16)
            for eng in ENGS:
                plan.wait(eng, s_dbg3, s_dbg3.n)
        for ts in range(c.NTS):
            row0 = bb * c.TB + ts * 128
            if s_xa3.n >= 16:
                plan.wait("sync", s_od, s_od.n)
            plan.on("sync", lambda e, r=row0: e.dma_start(
                out=xa3, in_=xatt_d[r:r + 128, :]).then_inc(s_xa3.h, 16))
            s_xa3.bump(16)
            zt = z_sb[:, ts * c.D:(ts + 1) * c.D]
            sem, val = emit_ln(plan, c, s_ln3A, s_ln3D, lnbufs, zt, zn_sb,
                               s_ztre, ZTRE_ALL,
                               dst_free=(s_ln3D, prev_ov) if prev_ov else None)
            plan.wait("vector", sem, val)
            plan.wait("vector", s_xa3, s_xa3.n)
            plan.on("vector", lambda e: e.tensor_add(xout_sb, zn_sb, xa3)
                    .then_inc(s_ln3D.h, 1))
            ov = s_ln3D.bump()
            prev_ov = ov
            aval3[(bb, ts)] = ov
            plan.wait("sync", s_ln3D, ov)
            plan.on("sync", lambda e, r=row0: e.dma_start(
                out=out_d[r:r + 128, :], in_=xout_sb).then_inc(s_od.h, 16))
            s_od.bump(16)
        if bb + 1 < c.NB:
            emit_wload2(bb + 1, 2)
            emit_wload2(bb + 1, 3)

    plan.wait("sync", s_od, s_od.n)
    return finish()



def kernel(L0_attn_x, L0_attn_w_qkv, L0_attn_w_out, L0_attn_g,
           L0_ffn_w_gate, L0_ffn_w_up, L0_ffn_w_down,
           L0_ffn_s_gate, L0_ffn_s_up, L0_ffn_s_down):
    cfg = Cfg(n=4096, D=2048, hc=8, inter=8192)
    x = np.asarray(L0_attn_x, np.float32)
    in_maps = prep_in_maps(
        cfg, [x[b] for b in range(x.shape[0])],
        np.asarray(L0_attn_w_qkv, np.float32),
        np.asarray(L0_attn_w_out, np.float32),
        np.asarray(L0_ffn_w_gate),
        np.asarray(L0_ffn_w_up),
        np.asarray(L0_ffn_w_down),
        np.asarray(L0_ffn_s_gate, np.float32),
        np.asarray(L0_ffn_s_up, np.float32),
        np.asarray(L0_ffn_s_down, np.float32))
    nc = build_nc(cfg)
    res = run_bass_kernel_spmd(nc, in_maps, core_ids=list(range(8)))
    out = np.empty((4, 4096, 2048), np.float32)
    for core in range(8):
        b, p = core // 2, core % 2
        out[b, p * cfg.half:(p + 1) * cfg.half] = res.results[core]["out"]
    return out
